# version 04
# baseline (speedup 1.0000x reference)
import math
import os

import numpy as np

import concourse.bass as bass
import concourse.mybir as mybir
from concourse.bass_utils import run_bass_kernel_spmd

F32 = mybir.dt.float32
F32R = mybir.dt.float32r
BF16 = mybir.dt.bfloat16
OP = mybir.AluOpType
AF = mybir.ActivationFunctionType
AX = mybir.AxisListType

D = 2048
KT = D // 128
H = 16
CONV_DIM = 3 * D
NCT = CONV_DIM // 128
CK = 4
E = 8
FI = 8192
FIT = FI // 128
GRP = 4

T_OWN = int(os.environ.get("MK_DEV_T") or 1024)
CH = 128
NPH = 4
NCH = T_OWN // CH
T_HALF = min(512, T_OWN)
NHALF = T_OWN // T_HALF
NM = max(1, T_HALF // 512)
MV = min(512, T_HALF)
DEV_DUMP = bool(int(os.environ.get("MK_DEV_DUMP", "0")))
DEV_RAWO = bool(int(os.environ.get("MK_DEV_RAWO", "0")))

LN_EPS = 1e-5
GOLDEN_CENTER = 1.0 / math.e
GOLDEN_LOWER = 0.5 - math.log(4.0 / 3.0)
GOLDEN_UPPER = 0.5
MAGIC = float(np.float32(1.5 * 2.0 ** 23))
RG = [[0, 1, 2, 3], [4, 5, 6, 7]]


class Tok:
    __slots__ = ("sem", "used", "val", "inc")

    def __init__(self, sem, inc):
        self.sem = sem
        self.used = False
        self.val = None
        self.inc = inc


class Plan:
    ENGINES = ("sp", "gp", "act", "dve", "pe")

    def __init__(self):
        self.ops = {e: [] for e in self.ENGINES}
        self.producers = {}

    def op(self, eng, emit, deps=(), sem=None, inc=1, force=False):
        tok = None
        if sem is not None:
            tok = Tok(sem, inc)
            tok.used = force
            self.producers.setdefault(sem, []).append(tok)
        deps = tuple(d for d in deps if d is not None)
        for d in deps:
            d.used = True
        self.ops[eng].append((emit, deps, tok))
        return tok

    def finalize(self):
        for sem, lst in self.producers.items():
            v = 0
            for t in lst:
                if t.used:
                    v += t.inc
                    t.val = v

    def replay(self, eng, e, sems):
        waited = {}
        for emit, deps, tok in self.ops[eng]:
            needed = []
            for d in deps:
                if waited.get(d.sem, -1) < d.val:
                    needed.append((d.sem, d.val))
                    waited[d.sem] = d.val
            for s, v in needed[:-1]:
                e.wait_ge(sems[s], v)
            ins = emit(e)
            if needed:
                s, v = needed[-1]
                try:
                    ins._wait_ge(sems[s], v)
                except Exception:
                    e_ = None
                    raise
            if tok is not None and tok.used:
                ins.then_inc(sems[tok.sem], tok.inc)


class Ctx:
    pass


def pe(c, emit, deps=()):
    return c.plan.op("pe", emit, deps, sem="pe")


def dve(c, emit, deps=()):
    return c.plan.op("dve", emit, deps, sem="dve")


def act(c, emit, deps=()):
    return c.plan.op("act", emit, deps, sem="act")


def gp(c, emit, deps=()):
    return c.plan.op("gp", emit, deps, sem="gp")


def ccop(c, emit, deps=()):
    return c.plan.op("gp", emit, deps, sem="cc" if c.sem("cc") else "cc",
                     force=True)


def _dma(c, eng, stream, out_ap, in_ap, deps):
    if not hasattr(c, "dma_prev"):
        c.dma_prev = {}
    deps = list(deps) + [c.dma_prev.get(stream)]
    tok = c.plan.op(eng, lambda e, o=out_ap, i=in_ap: e.dma_start(out=o, in_=i),
                    deps, sem=c.sem(stream), inc=16, force=True)
    c.dma_prev[stream] = tok
    return tok


def dma_in(c, stream, out_ap, in_ap, deps=()):
    return _dma(c, "sp", stream, out_ap, in_ap, deps)


def dma_out(c, stream, out_ap, in_ap, deps=()):
    return _dma(c, "gp", stream, out_ap, in_ap, deps)


def mm_group(c, emits, deps=()):
    tok = None
    for i, em in enumerate(emits):
        d = deps if i == 0 else ()
        if i == len(emits) - 1:
            tok = c.plan.op("pe", em, d, sem="pe")
        else:
            c.plan.op("pe", em, d)
    return tok


def fm(dram_ap, p=128):
    return dram_ap.rearrange("(n p) t -> p n t", p=p)


def build_program():
    nc = bass.Bass("TRN2", target_bir_lowering=False, debug=False)
    c = Ctx()
    c.nc = nc
    c.plan = Plan()
    c.tensors = {}
    scr_kind = "ExternalOutput" if DEV_DUMP else "Internal"

    def dram(name, shape, dtype=F32, kind="Internal"):
        return nc.dram_tensor(name, list(shape), dtype, kind=kind).ap()

    din = {}
    for name, shape in [
        ("xT", (D, T_OWN)), ("w_qkv", (D, CONV_DIM)),
        ("conv_w", (128, NCT * CK)), ("w_z", (D, D)), ("w_b", (D, H)),
        ("w_a", (D, H)), ("dt_bias", (H, 1)), ("a_log", (H, 1)),
        ("norm_w", (128, 1)), ("w_out0", (D, D)), ("ffn_wg", (E, D, D)),
        ("ffn_wu", (E, D, D)), ("ffn_wd", (E, D, D)), ("ffn_wr", (D, E)),
        ("temp", (1, 1)), ("l1_w_in", (D, D)), ("l1_b_in", (128, KT)),
        ("l1_w_dec", (D, D)), ("l1_b_dec", (128, KT)),
        ("l1_w_gate", (D, D)), ("l1_b_gate", (128, KT)),
        ("l1_w_out", (D, D)), ("l1_b_out", (128, KT)),
        ("l1_dec_bias", (128, KT)), ("bn_wgT", (D, FI)), ("bn_wuT", (D, FI)),
        ("bn_wdT", (FI, D)), ("eye", (128, 128)), ("ones", (128, 128)),
        ("oh16", (16, 16 * 128)), ("mask_ge", (128, 128)),
        ("maskT_incl", (128, 128)), ("maskT_strict", (128, 128)),
        ("sel", (128, GRP)), ("kcol", (128, 8)), ("had128", (128, 128)),
    ]:
        din[name] = dram(name, shape, kind="ExternalInput")
    c.din = din

    ds = {}
    for name, shape in [
        ("mixed_scr", (CONV_DIM, T_OWN)), ("conv_scr", (CONV_DIM, T_OWN)),
        ("z_scr", (D, T_OWN)), ("ogn_scr", (D, T_OWN)),
        ("sattn_scr", (D, T_OWN)), ("x1_scr", (D, T_OWN)),
        ("s1_scr", (D, T_OWN)), ("int_scr", (FI, T_OWN)),
        ("dec_scr", (D, T_OWN)), ("dd1_scr", (D, T_OWN)),
        ("gate_scr", (D, T_OWN)),
    ]:
        ds[name] = dram(name, shape, kind=scr_kind)
    for name, shape in [
        ("cc1_in", (CONV_DIM, 3)), ("cc1_out", (GRP, CONV_DIM, 3)),
        ("cc2_in", (H * 2 * 128, 128)), ("cc2_out", (GRP, H * 2 * 128, 128)),
        ("cc3_in", (D, 2)), ("cc3_out", (GRP, D, 2)),
    ]:
        ds[name] = dram(name, shape)
    ds["wdq_scr"] = dram("wdq_scr", (FI, D), BF16)
    c.ds = ds
    c.dout = dram("outT", (D, T_OWN), kind="ExternalOutput")

    import contextlib
    es = contextlib.ExitStack()
    c.phase_es = None

    def sbuf_p(name, shape, dtype=F32):
        t = es.enter_context(nc.sbuf_tensor("sb_" + name, list(shape), dtype,
                                            side="left"))
        c.tensors[name] = t
        return t

    def sbuf(name, shape, dtype=F32):
        t = (c.phase_es or es).enter_context(
            nc.sbuf_tensor("sb_" + name, list(shape), dtype, side="right"))
        c.tensors[name] = t
        return t

    c.sbuf_p = sbuf_p
    c.sbuf = sbuf
    c.sems = {}

    def sem(name):
        if name not in c.sems:
            c.sems[name] = es.enter_context(nc.semaphore(name))
        return name

    c.sem = sem
    for e in ("pe", "dve", "act", "gp"):
        sem(e)

    with es:
        c.ps = es.enter_context(nc.psum_tensor("ps", [128, 8, 512], F32))
        plan_all(c)
        c.plan.finalize()
        with nc.Block() as block:
            @block.sync
            def _(e):
                c.plan.replay("sp", e, c.sems)

            @block.gpsimd
            def _(e):
                c.plan.replay("gp", e, c.sems)

            @block.scalar
            def _(e):
                c.plan.replay("act", e, c.sems)

            @block.vector
            def _(e):
                c.plan.replay("dve", e, c.sems)

            @block.tensor
            def _(e):
                c.plan.replay("pe", e, c.sems)
    return nc


def barrier(c):
    deps = [lst[-1] for lst in c.plan.producers.values() if lst]
    return gp(c, lambda e: e.engine_nop(), deps=deps)


def plan_all(c):
    import contextlib
    plan_P0(c)
    for fn in (plan_PA, plan_delta, plan_wout, plan_ffn, plan_l1attn,
               plan_bitnet, plan_final):
        with contextlib.ExitStack() as pes:
            c.phase_es = pes
            c.bar = barrier(c)
            c.bank_gate2 = {i: c.bar for i in range(8)}
            c.drr = 0
            fn(c)
    c.phase_es = None


def plan_P0(c):
    sb = c.sbuf
    c.eye = sb("eye", (128, 128))
    c.ones = sb("ones", (128, 128))
    c.oh16 = sb("oh16", (16, 16 * 128))
    c.mask_ge = sb("mask_ge", (128, 128))
    c.maskT_incl = sb("maskT_incl", (128, 128))
    c.maskT_strict = sb("maskT_strict", (128, 128))
    c.sel = sb("sel", (128, GRP))
    c.convw = sb("convw", (128, NCT * CK))
    c.norm_w = sb("norm_w", (128, 1))
    c.dtb = sb("dtb", (H, 1))
    c.alog = sb("alog", (H, 1))
    c.negexpA = sb("negexpA", (H, 1))
    c.rtau = sb("rtau", (128, 1))

    tc = {}
    for name, dst in (("eye", c.eye), ("ones", c.ones), ("oh16", c.oh16),
                      ("mask_ge", c.mask_ge), ("maskT_incl", c.maskT_incl),
                      ("maskT_strict", c.maskT_strict), ("sel", c.sel),
                      ("conv_w", c.convw), ("norm_w", c.norm_w),
                      ("dt_bias", c.dtb), ("a_log", c.alog)):
        tc[name] = dma_in(c, "dconst", dst[:], c.din[name])
    tc["tau"] = dma_in(c, "dconst", c.rtau[:],
                       c.din["temp"].to_broadcast((128, 1)))
    c.kcol = sb("kcol", (128, 8))
    tc["kcol"] = dma_in(c, "dconst", c.kcol[:], c.din["kcol"])
    c.kc = lambda i: c.kcol[:, i:i + 1]
    t_ = sb("had128", (128, 128))
    tc["had128"] = dma_in(c, "dconst", t_[:], c.din["had128"])
    for nm, key in (("l1bi", "l1_b_in"), ("l1bd", "l1_b_dec"),
                    ("l1bg", "l1_b_gate"), ("l1bo", "l1_b_out"),
                    ("l1db", "l1_dec_bias")):
        t_ = sb(nm, (128, KT))
        tc[key] = dma_in(c, "dconst", t_[:], c.din[key])
    c.tc = tc
    t1 = act(c, lambda e: e.activation(c.negexpA[:], c.alog[:], AF.Exp),
             deps=[tc["a_log"]])
    c.tok_negexpA = dve(c, lambda e: e.tensor_scalar_mul(
        c.negexpA[:], c.negexpA[:], -1.0), deps=[t1])
    c.tok_rtau = dve(c, lambda e: e.reciprocal(c.rtau[:], c.rtau[:]),
                     deps=[tc["tau"]])


def plan_stats(c, xh, sqx, mbc, rstdn, vsc, tx, war_deps):
    toks = []
    for m in range(NM):
        msl = slice(m * 512, m * 512 + MV)
        t1 = mm_group(c, [
            (lambda e, k=k, msl=msl, f=(k == 0), l=(k == KT - 1):
             e.matmul(c.ps[:, 4, :MV], c.ones[:], xh[:, k, msl],
                      start=f, stop=l)) for k in range(KT)],
            deps=[tx, c.tc["ones"]] + war_deps)
        sq_toks = []
        for k in range(KT):
            war2 = (sq_toks[-2:-1] if len(sq_toks) >= 2 else
                    ([c.sqx_war] if getattr(c, "sqx_war", None) else []))
            tsq = act(c, lambda e, k=k, msl=msl: e.activation(
                sqx[:, k % 2, :], xh[:, k, msl], AF.Square),
                deps=[tx] + war_deps + war2)
            tmm = c.plan.op(
                "pe",
                (lambda e, k=k, f=(k == 0), l=(k == KT - 1):
                 e.matmul(c.ps[:, 5, :MV], c.ones[:], sqx[:, k % 2, :],
                          start=f, stop=l)),
                [tsq] + ([t1] if k == 0 else []), sem="pe")
            sq_toks.append(tmm)
        t2 = sq_toks[-1]
        c.sqx_war = t2
        mb, rs, vs = mbc[:, msl], rstdn[:, msl], vsc[:, msl]
        t3 = dve(c, lambda e, mb=mb: e.tensor_scalar_mul(
            mb, c.ps[:, 4, :MV], 1.0 / D), deps=[t1])
        t4 = dve(c, lambda e, rs=rs: e.tensor_scalar_mul(
            rs, c.ps[:, 5, :MV], 1.0 / D), deps=[t2])
        t5 = dve(c, lambda e, mb=mb, vs=vs: e.tensor_tensor(
            vs, mb, mb, OP.mult), deps=[t3])
        t6 = dve(c, lambda e, rs=rs, vs=vs: e.tensor_tensor(
            rs, rs, vs, OP.subtract), deps=[t4, t5])
        t7 = act(c, lambda e, rs=rs: e.activation(rs, rs, AF.Sqrt,
                                                  bias=c.kc(0)), deps=[t6, c.tc["kcol"]])
        t8 = dve(c, lambda e, rs=rs: e.reciprocal(rs, rs), deps=[t7])
        t9 = dve(c, lambda e, rs=rs: e.tensor_scalar_mul(rs, rs, -1.0),
                 deps=[t8])
        toks += [t3, t9]
    return toks


def weight_stream(c, name, nstrips, wsrc_fn, xh, tx, h, post_fn,
                  cw=128, rows=128):
    for s in range(nstrips):
        buf = s % 2
        war = c.wstage_gate.get(buf)
        tl = dma_in(c, "dw%d" % buf, c.w_stage[:, buf, :, :cw], wsrc_fn(s),
                    [war] if war else [])
        last_mm = None
        for m in range(NM):
            msl = slice(m * 512, m * 512 + MV)
            bank = c.bank_rr
            c.bank_rr = (c.bank_rr + 1) % 4
            bwar = c.bank_gate.get(bank)
            tmm = mm_group(c, [
                (lambda e, k=k, msl=msl, bank=bank, buf=buf, cw=cw,
                 f=(k == 0), l=(k == KT - 1):
                 e.matmul(c.ps[:cw, bank, :MV], c.w_stage[:, buf, k, :cw],
                          xh[:, k, msl], start=f, stop=l))
                for k in range(KT)],
                deps=[tl, tx] + ([bwar] if bwar else []))
            tpost = post_fn(s, m, c.ps[:cw, bank, :MV], tmm)
            c.bank_gate[bank] = tpost
            last_mm = tpost
        c.wstage_gate[buf] = last_mm


def wsplit(c, whi, wlo, wf, deps):
    t1 = act(c, lambda e: e.activation(whi, wf, AF.Copy), deps=deps)
    t2 = dve(c, lambda e: e.tensor_tensor(wlo, wf, whi, OP.subtract),
             deps=[t1] + list(deps))
    return t1, t2


def weight_stream3(c, name, nstrips, wsrc_fn, xhi, xlo, tx, post_fn, cw=128):
    def load(s):
        buf = s % 2
        war = c.wstage_gate.get(buf)
        tl = dma_in(c, "dw%d" % buf, c.w_stage[:, buf, :, :cw], wsrc_fn(s),
                    [war] if war else [])
        wg = c.whl_gate.get(buf)
        th, tsp = wsplit(c, c.whl[:, buf, 0, :, :cw],
                         c.whl[:, buf, 1, :, :cw],
                         c.w_stage[:, buf, :, :cw],
                         [tl] + ([wg] if wg else []))
        c.wstage_gate[buf] = tsp
        return buf, th, tsp

    pend = load(0)
    for s in range(nstrips):
        buf, th, tsp = pend
        if s + 1 < nstrips:
            pend = load(s + 1)
        last_mm = None
        for m in range(NM):
            msl = slice(m * 512, m * 512 + MV)
            bank = c.bank_rr
            c.bank_rr = (c.bank_rr + 1) % 4
            bwar = c.bank_gate.get(bank)
            emits = []
            for k in range(KT):
                emits.append(lambda e, k=k, msl=msl, bank=bank, buf=buf,
                             cw=cw, f=(k == 0):
                             e.matmul(c.ps[:cw, bank, :MV],
                                      c.whl[:, buf, 0, k, :cw],
                                      xhi[:, k, msl], start=f, stop=False))
                emits.append(lambda e, k=k, msl=msl, bank=bank, buf=buf,
                             cw=cw:
                             e.matmul(c.ps[:cw, bank, :MV],
                                      c.whl[:, buf, 0, k, :cw],
                                      xlo[:, k, msl], start=False,
                                      stop=False))
            for k in range(KT):
                emits.append(lambda e, k=k, msl=msl, bank=bank, buf=buf,
                             cw=cw, l=(k == KT - 1):
                             e.matmul(c.ps[:cw, bank, :MV],
                                      c.whl[:, buf, 1, k, :cw],
                                      xhi[:, k, msl], start=False, stop=l))
            tmm = mm_group(c, emits,
                           deps=[th, tsp] + list(tx) +
                           ([bwar] if bwar else []))
            tpost = post_fn(s, m, c.ps[:cw, bank, :MV], tmm)
            c.bank_gate[bank] = tpost
            last_mm = tpost
        c.whl_gate[buf] = last_mm


def xsplit_k(c, xhi_k, xlo_k, src_k, deps):
    t1 = act(c, lambda e: e.activation(xhi_k, src_k, AF.Copy), deps=deps)
    t2 = dve(c, lambda e: e.tensor_tensor(xlo_k, src_k, xhi_k, OP.subtract),
             deps=[t1] + list(deps))
    return t1, t2


def build_xn(c, dst, src, mbc, rstdn, deps, war):
    toks = []
    for k in range(KT):
        t1 = dve(c, lambda e, k=k: e.scalar_tensor_tensor(
            dst[:, k, :], mbc, 1.0, src[:, k, :], OP.mult, OP.subtract),
            deps=deps + ([war[k]] if war and war[k] else []))
        t2 = dve(c, lambda e, k=k: e.tensor_tensor(
            dst[:, k, :], dst[:, k, :], rstdn, OP.mult), deps=[t1])
        toks.append(t2)
    return toks


def plan_PA(c):
    sb = c.sbuf
    sbp = c.sbuf_p
    c.xh = sb("xh", (128, KT, T_HALF))
    c.xnscr = sb("xnscr", (128, T_HALF))
    c.xnhi = sb("xnhi", (128, KT, T_HALF), BF16)
    c.xnlo = sb("xnlo", (128, KT, T_HALF), BF16)
    c.sqx = sb("sqx", (128, 2, MV))
    c.vsc = sb("vsc", (128, T_HALF))
    c.mbc0 = sbp("mbc0", (128, NHALF, T_HALF))
    c.rstdn0 = sbp("rstdn0", (128, NHALF, T_HALF))
    c.w_stage = sb("w_stage", (128, 2, KT, 128))
    c.whl = sb("whl", (128, 2, 2, KT, 128), BF16)
    c.mx_stage = sb("mx_stage", (128, 4, MV))
    c.beta16 = sbp("beta16", (H, T_OWN))
    c.bneg16 = sbp("bneg16", (H, T_OWN))
    c.g16 = sbp("g16", (H, T_OWN))
    c.B16 = sbp("B16", (H, T_OWN))
    c.expB16 = sbp("expB16", (H, T_OWN))
    c.wt8 = sbp("wt8", (8, T_OWN))
    c.halo = sbp("halo", (128, NCT, 3))
    c.cc1stage = sb("cc1stage", (128, NCT, 3))
    c.rt = sb("rt", (128, 10, 512))

    c.wstage_gate = {}
    c.whl_gate = {}
    c.bank_gate = dict(c.bank_gate2)
    c.bank_rr = 0
    c.mx_gate = {}
    c.rt_pswar = None
    c.ba_gate = None
    tok_cc1 = []
    c.tok_mixed_w = []
    c.tok_z_w = []
    prev_half_done = None
    stats_war = []
    xn_war = None

    for h in range(NHALF):
        hsl = slice(h * T_HALF, (h + 1) * T_HALF)
        xsrc = fm(c.din["xT"])[:, :, hsl]
        tx = dma_in(c, "dx%d" % (h % 2), c.xh[:], xsrc,
                    [prev_half_done] if prev_half_done else [])
        mbc = c.mbc0[:, h, :]
        rstdn = c.rstdn0[:, h, :]
        tok_stats = plan_stats(c, c.xh, c.sqx, mbc, rstdn, c.vsc, tx,
                               stats_war)
        stats_war = list(tok_stats)
        scr_war = None
        thi = tlo = None
        for k in range(KT):
            d0 = [tx] + tok_stats + ([xn_war] if xn_war else []) + \
                ([scr_war] if scr_war else [])
            t1 = dve(c, lambda e, k=k, mbc=mbc: e.scalar_tensor_tensor(
                c.xnscr[:], mbc, 1.0, c.xh[:, k, :], OP.mult, OP.subtract),
                deps=d0)
            t2 = dve(c, lambda e, rstdn=rstdn: e.tensor_tensor(
                c.xnscr[:], c.xnscr[:], rstdn, OP.mult), deps=[t1])
            thi, tlo = xsplit_k(c, c.xnhi[:, k, :], c.xnlo[:, k, :],
                                c.xnscr[:], [t2])
            scr_war = tlo
        txn = [thi, tlo]

        def mk_post(dst_scr, cc_stage):
            def post(s, m, psum_ap, tmm):
                buf2 = c.mx_rr
                c.mx_rr = (c.mx_rr + 1) % 4
                out_ap = c.mx_stage[:, buf2, :MV]
                war = c.mx_gate.get(buf2)
                te = dve(c, lambda e, out_ap=out_ap, psum_ap=psum_ap:
                         e.tensor_copy(out_ap, psum_ap),
                         deps=[tmm] + ([war] if war else []))
                toks = [te]
                if cc_stage is not None and h == NHALF - 1 and m == NM - 1:
                    tok_cc1.append(dve(c, lambda e, s=s, out_ap=out_ap:
                                       e.tensor_copy(c.cc1stage[:, s, :],
                                                     out_ap[:, MV - 3:MV]),
                                       deps=[te]))
                    toks.append(tok_cc1[-1])
                dst = fm(c.ds[dst_scr])[
                    :, s, slice(h * T_HALF + m * 512,
                                h * T_HALF + m * 512 + MV)]
                tw = dma_out(c, "ws%d" % buf2, dst, out_ap, toks)
                c.mx_gate[buf2] = tw
                if dst_scr == "mixed_scr":
                    c.tok_mixed_w.append(tw)
                else:
                    c.tok_z_w.append(tw)
                return tw
            return post

        c.mx_rr = 0

        def qkv_src(s):
            return c.din["w_qkv"][:, s * 128:(s + 1) * 128].rearrange(
                "(k p) c -> p k c", p=128)

        weight_stream3(c, "qkv", NCT, qkv_src, c.xnhi, c.xnlo, txn,
                       mk_post("mixed_scr", True))

        def z_src(s):
            return c.din["w_z"][:, s * 128:(s + 1) * 128].rearrange(
                "(k p) c -> p k c", p=128)

        weight_stream3(c, "z", KT, z_src, c.xnhi, c.xnlo, txn,
                       mk_post("z_scr", None))

        def fin_b(s, m, psum_ap, tmm):
            osl = slice(h * T_HALF + m * 512, h * T_HALF + m * 512 + MV)
            t1 = act(c, lambda e, psum_ap=psum_ap, osl=osl: e.activation(
                c.beta16[:, osl], psum_ap, AF.Sigmoid), deps=[tmm])
            t2 = dve(c, lambda e, osl=osl: e.tensor_scalar_mul(
                c.bneg16[:, osl], c.beta16[:, osl], -1.0), deps=[t1])
            return t2

        def fin_a(s, m, psum_ap, tmm):
            osl = slice(h * T_HALF + m * 512, h * T_HALF + m * 512 + MV)
            t1 = act(c, lambda e, psum_ap=psum_ap, osl=osl: e.activation(
                c.g16[:, osl], psum_ap, AF.Exp, bias=c.dtb[:]),
                deps=[tmm, c.tc["dt_bias"]])
            t2 = act(c, lambda e, osl=osl: e.activation(
                c.g16[:, osl], c.g16[:, osl], AF.Ln, bias=c.kc(1)[:16]),
                deps=[t1, c.tc["kcol"]])
            t3 = dve(c, lambda e, osl=osl: e.tensor_scalar(
                c.g16[:, osl], c.g16[:, osl], c.negexpA[:], None, OP.mult),
                deps=[t2, c.tok_negexpA])
            tlast = t3
            base = h * T_HALF + m * 512
            for ci in range(MV // CH):
                csl = slice(base + ci * CH, base + (ci + 1) * CH)
                ts = dve(c, lambda e, csl=csl: e.tensor_tensor_scan(
                    c.B16[:, csl], c.ones[:16, :CH], c.g16[:, csl], 0.0,
                    OP.mult, OP.add), deps=[t3, c.tc["ones"]])
                te2 = act(c, lambda e, csl=csl: e.activation(
                    c.expB16[:, csl], c.B16[:, csl], AF.Exp), deps=[ts])
                tlast = te2
            return tlast

        def wb_src(s):
            return c.din["w_b"].rearrange("(k p) c -> p k c", p=128)

        def wa_src(s):
            return c.din["w_a"].rearrange("(k p) c -> p k c", p=128)

        weight_stream3(c, "w_b", 1, wb_src, c.xnhi, c.xnlo, txn, fin_b,
                       cw=16)
        weight_stream3(c, "w_a", 1, wa_src, c.xnhi, c.xnlo, txn, fin_a,
                       cw=16)

        def wr_src(s):
            return c.din["ffn_wr"].rearrange("(k p) c -> p k c", p=128)

        rt_toks = []

        def fin_r(s, m, psum_ap, tmm):
            t1 = act(c, lambda e, psum_ap=psum_ap: e.activation(
                c.rt[:8, 0, :MV], psum_ap, AF.Sigmoid, scale=c.rtau[:8]),
                deps=[tmm, c.tok_rtau] +
                ([c.rt_pswar] if c.rt_pswar else []))
            tprev = t1
            for ci in range(MV // 128):
                tprev = plan_router_chunk(
                    c, h * T_HALF + m * 512 + ci * 128,
                    c.rt[:8, 0, ci * 128:(ci + 1) * 128], tprev)
                rt_toks.append(tprev)
            return tprev

        weight_stream3(c, "wr", 1, wr_src, c.xnhi, c.xnlo, txn, fin_r,
                       cw=8)

        prev_half_done = rt_toks[-1]
        xn_war = rt_toks[-1]

    c.gath1 = sb("gath1", (128, GRP, NCT, 3))
    t1 = dma_out(c, "wcc1", fm(c.ds["cc1_in"]), c.cc1stage[:], tok_cc1)
    tcc = ccop(c, lambda e: e.collective_compute(
        "AllGather", OP.bypass, replica_groups=RG,
        ins=[c.ds["cc1_in"]], outs=[c.ds["cc1_out"]]), deps=[t1])
    t2 = dma_in(c, "dcc1", c.gath1[:],
                c.ds["cc1_out"].rearrange("g (n p) t -> p g n t", p=128),
                [tcc])
    th = dve(c, lambda e: e.tensor_scalar(
        c.halo[:], c.gath1[:, 0], c.sel[:, 1:2], None, OP.mult),
        deps=[t2, c.tc["sel"]])
    for j in (2, 3):
        th = dve(c, lambda e, j=j: e.scalar_tensor_tensor(
            c.halo[:], c.gath1[:, j - 1], c.sel[:, j:j + 1], c.halo[:],
            OP.mult, OP.add), deps=[th])
    c.tok_halo = th

    if DEV_DUMP:
        dbg = c.nc.dram_tensor("dbg_pa", [64, T_OWN], F32,
                               kind="ExternalOutput").ap()
        deps = [c.ba_gate, c.rt_pswar, th]
        dma_out(c, "wdbg", dbg[0:16, :], c.beta16[:], deps)
        dma_out(c, "wdbg", dbg[16:32, :], c.g16[:], deps)
        dma_out(c, "wdbg", dbg[32:48, :], c.B16[:], deps)
        dma_out(c, "wdbg", dbg[48:56, :], c.wt8[:], deps)


def plan_router_chunk(c, tok0, inh_ap, dep):
    ps7 = c.ps[:, 7, :]
    tt = pe(c, lambda e: e.transpose(ps7[:, :8], inh_ap, c.eye[:8, :8]),
            deps=[dep, c.tc["eye"], c.rt_pswar] if c.rt_pswar
            else [dep, c.tc["eye"]])
    A = c.rt[:, 1, :8]
    Dst = c.rt[:, 2, :8]
    Z1 = c.rt[:, 3, :8]
    W = c.rt[:, 4, :8]
    FB = c.rt[:, 5, :8]
    M1 = c.rt[:, 6, :8]
    M2 = c.rt[:, 7, :8]
    cols = c.rt[:, 8, :8]
    tA = dve(c, lambda e: e.tensor_copy(A, ps7[:128, :8]), deps=[tt])
    td = act(c, lambda e: e.activation(Dst, A, AF.Abs, bias=c.kc(3)),
             deps=[tA, c.tc["kcol"]])
    tz1 = dve(c, lambda e: e.tensor_single_scalar(Z1, A, GOLDEN_LOWER,
                                                  OP.is_ge), deps=[tA])
    tw = act(c, lambda e: e.activation(W, Dst, AF.Exp, scale=-10.0),
             deps=[td])
    tw = dve(c, lambda e: e.tensor_tensor(W, W, Z1, OP.mult),
             deps=[tw, tz1])
    tz2 = dve(c, lambda e: e.tensor_single_scalar(Z1, A, GOLDEN_UPPER,
                                                  OP.is_le), deps=[tA, tw])
    tw = dve(c, lambda e: e.tensor_tensor(W, W, Z1, OP.mult),
             deps=[tw, tz2])
    ws = cols[:, 0:1]
    no = cols[:, 1:2]
    tws = dve(c, lambda e: e.tensor_reduce(ws, W, AX.X, OP.add), deps=[tw])
    tno = dve(c, lambda e: e.tensor_single_scalar(no, ws, 1e-8, OP.is_lt),
              deps=[tws])
    tfb = act(c, lambda e: e.activation(FB, Dst, AF.Exp, scale=-10.0 / 3.0),
              deps=[td])
    mx = cols[:, 2:3]
    tm1 = dve(c, lambda e: e.tensor_reduce(mx, FB, AX.X, OP.max), deps=[tfb])
    tM1 = dve(c, lambda e: e.tensor_scalar(M1, FB, mx, None, OP.is_ge),
              deps=[tm1, tfb])
    tfb2 = dve(c, lambda e: e.scalar_tensor_tensor(
        M2, M1, -1e30, FB, OP.mult, OP.add), deps=[tM1, tfb])
    tm2 = dve(c, lambda e: e.tensor_reduce(mx, M2, AX.X, OP.max),
              deps=[tfb2, tM1])
    tM2 = dve(c, lambda e: e.tensor_scalar(M2, M2, mx, None, OP.is_ge),
              deps=[tm2])
    tmk = dve(c, lambda e: e.tensor_tensor(M1, M1, M2, OP.add),
              deps=[tM1, tM2])
    tfw = dve(c, lambda e: e.tensor_tensor(FB, FB, M1, OP.mult),
              deps=[tmk, tfb])
    fs = cols[:, 3:4]
    tfs = dve(c, lambda e: e.tensor_reduce(fs, FB, AX.X, OP.add), deps=[tfw])
    tfs = dve(c, lambda e: e.tensor_single_scalar(fs, fs, 1e-8, OP.max),
              deps=[tfs])
    tfs = dve(c, lambda e: e.reciprocal(fs, fs), deps=[tfs])
    tfw = dve(c, lambda e: e.tensor_scalar(FB, FB, fs, None, OP.mult),
              deps=[tfs, tfw])
    tdel = dve(c, lambda e: e.tensor_tensor(FB, FB, W, OP.subtract),
               deps=[tfw, tw])
    tsel = dve(c, lambda e: e.scalar_tensor_tensor(
        W, FB, no, W, OP.mult, OP.add), deps=[tdel, tno, tw])
    tws2 = dve(c, lambda e: e.tensor_reduce(ws, W, AX.X, OP.add),
               deps=[tsel])
    tws2 = dve(c, lambda e: e.tensor_single_scalar(ws, ws, 1e-8, OP.max),
               deps=[tws2])
    tws2 = dve(c, lambda e: e.reciprocal(ws, ws), deps=[tws2])
    tfin = dve(c, lambda e: e.tensor_scalar(W, W, ws, None, OP.mult),
               deps=[tws2, tsel])
    ttb = pe(c, lambda e: e.transpose(ps7[:8, :128], W, c.eye[:]),
             deps=[tfin, tA])
    tout = dve(c, lambda e, tok0=tok0: e.tensor_copy(
        c.wt8[:, tok0:tok0 + 128], ps7[:8, :128]), deps=[ttb])
    c.rt_pswar = tout
    return tout


def mmp(c, emit_fns, deps, width=128):
    b = c.drr
    c.drr = (c.drr + 1) % 8
    ap = c.ps[:, b, :width]
    war = c.bank_gate2.get(b)
    tok = mm_group(c, [(lambda e, f=f, ap=ap: f(e, ap)) for f in emit_fns],
                   list(deps) + [war])
    return ap, tok, b


def trp(c, in_ap, ident, deps, width=128, rows=128):
    b = c.drr
    c.drr = (c.drr + 1) % 8
    ap = c.ps[:rows, b, :width]
    war = c.bank_gate2.get(b)
    tok = pe(c, lambda e, ap=ap, i=in_ap, idn=ident: e.transpose(ap, i, idn),
             list(deps) + [war, c.tc["eye"]])
    return ap, tok, b


def delta_ingest(c, i, pass2):
    par = i % 2
    gate = c.chunk_gate[par]
    chsl = slice(i * CH, (i + 1) * CH)
    if pass2:
        tin = dma_in(c, "dc2%d" % par, c.cvo[:, 0],
                     fm(c.ds["conv_scr"])[:, :, chsl],
                     c.chunk_gate[0] + c.chunk_gate[1] + [c.cvw_tok[i]])
        toks_cv = [tin]
    else:
        tmain = dma_in(c, "dch", c.mxch[:, :, 3:],
                       fm(c.ds["mixed_scr"])[:, :, chsl],
                       c.chunk_gate[0] + c.chunk_gate[1] + c.mx_war)
        if i == 0:
            tpre = dve(c, lambda e: e.tensor_copy(
                c.mxch[:, :, 0:3], c.halo[:]),
                deps=[c.tok_halo, tmain] + gate)
        else:
            tpre = dma_in(c, "dch", c.mxch[:, :, 0:3],
                          fm(c.ds["mixed_scr"])[:, :, i * CH - 3:i * CH], gate)
        conv_toks = []
        for ct in range(NCT):
            t0 = dve(c, lambda e, ct=ct: e.tensor_scalar(
                c.cvo[:, 0, ct, :], c.mxch[:, ct, 0:CH],
                c.convw[:, ct * CK:ct * CK + 1], None, OP.mult),
                deps=[tmain, tpre, c.tc["conv_w"]] +
                c.chunk_gate[0] + c.chunk_gate[1])
            for j in (1, 2, 3):
                t0 = dve(c, lambda e, par=par, ct=ct, j=j:
                         e.scalar_tensor_tensor(
                             c.cvo[:, 0, ct, :], c.mxch[:, ct, j:j + CH],
                             c.convw[:, ct * CK + j:ct * CK + j + 1],
                             c.cvo[:, 0, ct, :], OP.mult, OP.add),
                         deps=[t0])
            conv_toks.append(t0)
        c.mx_war = conv_toks
        tsilu = act(c, lambda e, par=par: e.activation(
            c.cvo[:, 0], c.cvo[:, 0], AF.Silu), deps=conv_toks)
        toks_cv = []
        for ct2 in range(32):
            p2 = ct2 % 2
            tsq = act(c, lambda e, par=par, ct2=ct2, p2=p2: e.activation(
                c.sqk[:, p2, :], c.cvo[:, 0, ct2, :], AF.Square),
                deps=[tsilu] + c.sq_war.get(p2, []))
            ap, tmm, b = mmp(c, [
                (lambda e, ap, p2=p2: e.matmul(
                    ap, c.ones[:], c.sqk[:, p2, :], start=True, stop=True))],
                [tsq, c.tc["ones"]])
            c.sq_war[p2] = [tmm]
            tsr = act(c, lambda e, p2=p2, ap=ap: e.activation(
                c.rnq[:, p2, :], ap, AF.Sqrt, bias=c.kc(2)),
                deps=[tmm, c.tc["kcol"]] + c.rn_war.get(p2, []))
            c.bank_gate2[b] = tsr
            trc = dve(c, lambda e, p2=p2: e.reciprocal(
                c.rnq[:, p2, :], c.rnq[:, p2, :]), deps=[tsr])
            if ct2 < 16:
                tsc = dve(c, lambda e, par=par, ct2=ct2, p2=p2:
                          e.scalar_tensor_tensor(
                              c.cvo[:, 0, ct2, :], c.cvo[:, 0, ct2, :],
                              128.0 ** -0.5, c.rnq[:, p2, :],
                              OP.mult, OP.mult), deps=[trc, tsilu])
            else:
                tsc = dve(c, lambda e, par=par, ct2=ct2, p2=p2: e.tensor_tensor(
                    c.cvo[:, 0, ct2, :], c.cvo[:, 0, ct2, :],
                    c.rnq[:, p2, :], OP.mult), deps=[trc, tsilu])
            c.rn_war[p2] = [tsc]
            toks_cv.append(tsc)
        tw = dma_out(c, "wcv%d" % par, fm(c.ds["conv_scr"])[:, :, chsl],
                     c.cvo[:, 0], toks_cv + [tsilu])
        c.cvw_tok[i] = tw
        toks_cv.append(tw)
    col_toks = []
    for src, dst in ((c.B16, c.Btr), (c.beta16, c.btr),
                     (c.bneg16, c.bntr), (c.expB16, c.eBtr)):
        ap, tt, b = trp(c, src[:, chsl], c.eye[:16, :16],
                        list(gate), width=16)
        tev = act(c, lambda e, dst=dst, ap=ap, par=par: e.activation(
            dst[:, par], ap, AF.Copy), deps=[tt])
        c.bank_gate2[b] = tev
        col_toks.append(tev)
    return toks_cv, col_toks


def delta_head_common(c, i, h, par, ph, dep_cv, dep_cols):
    r = {}
    gate = c.head_gate[ph]
    chsl = slice(i * CH, (i + 1) * CH)
    kn = c.cvo[:, 0, 16 + h, :]
    bap, tb, bb = mmp(c, [
        (lambda e, ap, h=h, chsl=chsl: e.matmul(
            ap, c.oh16[:, h * 128:(h + 1) * 128], c.B16[:, chsl],
            start=True, stop=True))], dep_cols + gate + [c.tc["oh16"]])
    tmp = c.tmp[:, ph]
    t_tmp = dve(c, lambda e, tmp=tmp, bap=bap, h=h, par=par: e.tensor_scalar(
        tmp, bap, c.Btr[:, par, h:h + 1], None, OP.subtract),
        deps=[tb] + dep_cols)
    t_gbc = act(c, lambda e, bap=bap, ph=ph: e.activation(
        c.gbc[:, ph], bap, AF.Exp), deps=[tb, t_tmp])
    c.bank_gate2[bb] = t_gbc
    nap, tn, bn = mmp(c, [
        (lambda e, ap, h=h, chsl=chsl: e.matmul(
            ap, c.oh16[:, h * 128:(h + 1) * 128], c.bneg16[:, chsl],
            start=True, stop=True))], dep_cols + gate + [c.tc["oh16"]])
    kap, tkk, bk = mmp(c, [
        (lambda e, ap, kn=kn: e.matmul(ap, kn, kn, start=True, stop=True))],
        dep_cv + gate)
    t1 = c.t1[:, ph]
    ta = dve(c, lambda e, t1=t1, tmp=tmp: e.tensor_tensor(
        t1, tmp, c.mask_ge[:], OP.max), deps=[t_tmp, c.tc["mask_ge"]])
    tb2 = act(c, lambda e, t1=t1: e.activation(t1, t1, AF.Exp, scale=-1.0),
              deps=[ta])
    N = c.Nsb[:, ph]
    tc1 = dve(c, lambda e, N=N, t1=t1, kap=kap: e.tensor_tensor(
        N, t1, kap, OP.mult), deps=[tb2, tkk])
    tc2 = dve(c, lambda e, N=N, h=h, par=par: e.tensor_scalar(
        N, N, c.bntr[:, par, h:h + 1], None, OP.mult), deps=[tc1] + dep_cols)
    t1b = c.t1b[:, ph]
    td = dve(c, lambda e, t1b=t1b, tmp=tmp: e.tensor_tensor(
        t1b, tmp, c.maskT_strict[:], OP.min),
        deps=[t_tmp, c.tc["maskT_strict"]])
    te = act(c, lambda e, t1b=t1b: e.activation(t1b, t1b, AF.Exp), deps=[td])
    Z = c.Zsb[:, ph]
    tf = dve(c, lambda e, Z=Z, t1b=t1b, kap=kap: e.tensor_tensor(
        Z, t1b, kap, OP.mult), deps=[te, tkk, tc1])
    tg = dve(c, lambda e, Z=Z, nap=nap: e.tensor_tensor(
        Z, Z, nap, OP.mult), deps=[tf, tn])
    c.bank_gate2[bk] = tf
    c.bank_gate2[bn] = tg
    PZ = c.PZ[:, ph]
    tpz = dve(c, lambda e, PZ=PZ, Z=Z: e.tensor_tensor(
        PZ, Z, c.eye[:], OP.add), deps=[tg, c.tc["eye"]])
    Ms = [c.Ma[:, ph], c.Mb[:, ph]]
    Zs = [c.Za[:, ph], c.Zb[:, ph]]
    tM, tZ = tc2, tg
    Mcur, Zcur = N, Z
    for j in range(1, 7):
        pj = j % 2
        map_, tmj, bm = mmp(c, [
            (lambda e, ap, Z=Zcur, M=Mcur: e.matmul(ap, Z, M,
                                                    start=True, stop=True))],
            [tM, tZ])
        tMn = act(c, lambda e, d=Ms[pj], ap=map_: e.activation(
            d, ap, AF.Copy), deps=[tmj])
        c.bank_gate2[bm] = tMn
        if j < 6:
            zap, tzj, bz = mmp(c, [
                (lambda e, ap, Z=Zcur, M=Mcur: e.matmul(
                    ap, M, Z, start=True, stop=True))], [tM, tZ])
            tZn = act(c, lambda e, d=Zs[pj], ap=zap: e.activation(
                d, ap, AF.Copy), deps=[tzj])
            c.bank_gate2[bz] = tZn
        else:
            tZn = tZ
        pap, tpj, bp = mmp(c, [
            (lambda e, ap, M=Ms[pj], PZ=PZ: e.matmul(
                ap, M, PZ, start=True, stop=True))], [tMn, tpz])
        tpz = dve(c, lambda e, PZ=PZ, ap=pap: e.tensor_tensor(
            PZ, PZ, ap, OP.add), deps=[tpj, tpz])
        c.bank_gate2[bp] = tpz
        Mcur, Zcur = Ms[pj], Zs[pj]
        tM, tZ = tMn, tZn
    r["PZ"], r["tPZ"] = PZ, tpz
    ghat = c.cols[:, ph, 0:1]
    gC = c.cols[:, ph, 1:2]
    bg = c.cols[:, ph, 2:3]
    tgh = act(c, lambda e, ghat=ghat, tmp=tmp: e.activation(
        ghat, tmp[:, 127:128], AF.Exp), deps=[t_tmp])
    tgc = dve(c, lambda e, gC=gC, ghat=ghat, h=h, par=par: e.tensor_scalar(
        gC, ghat, c.eBtr[:, par, h:h + 1], None, OP.mult),
        deps=[tgh] + dep_cols)
    tbg = dve(c, lambda e, bg=bg, h=h, par=par: e.tensor_tensor(
        bg, c.btr[:, par, h:h + 1], c.eBtr[:, par, h:h + 1], OP.mult),
        deps=dep_cols + gate)
    r.update(ghat=ghat, gC=gC, bg=bg, tgh=tgh, tgc=tgc, tbg=tbg)
    kap2, tkt, bkt = trp(c, kn, c.eye[:], dep_cv + gate)
    kT = c.kT[:, ph]
    tktev = act(c, lambda e, kT=kT, ap=kap2: e.activation(kT, ap, AF.Copy),
                deps=[tkt])
    c.bank_gate2[bkt] = tktev
    r.update(kT=kT, tkT=tktev, kn=kn, tmp=tmp, t_tmp=t_tmp,
             gbc=c.gbc[:, ph], t_gbc=t_gbc)
    Khat = c.Khat[:, ph]
    tkh = dve(c, lambda e, Khat=Khat, kT=kT, ghat=ghat: e.tensor_scalar(
        Khat, kT, ghat, None, OP.mult), deps=[tktev, tgh])
    r.update(Khat=Khat, tKhat=tkh)
    return r


def delta_state_update(c, i, h, par, ph, r, S_in_ap, S_in_tok, S_out_ap):
    kn = r["kn"]
    v = c.cvo[:, 0, 32 + h, :]
    kb = c.kb[:, ph]
    tkb = dve(c, lambda e, kb=kb, kn=kn: e.tensor_tensor(
        kb, kn, r["gbc"], OP.mult), deps=[r["t_gbc"], c.head_gate[ph][0]])
    ksap, tks, bks = mmp(c, [
        (lambda e, ap, kb=kb, S=S_in_ap: e.matmul(ap, kb, S,
                                                  start=True, stop=True))],
        [tkb, S_in_tok])
    vap, tvt, bvt = trp(c, v, c.eye[:], [c.head_gate[ph][0]])
    vT = c.vT[:, ph]
    tvte = act(c, lambda e, vT=vT, ap=vap: e.activation(vT, ap, AF.Copy),
               deps=[tvt])
    c.bank_gate2[bvt] = tvte
    RHS = c.RHS[:, ph]
    t1 = dve(c, lambda e, RHS=RHS, vT=vT, ap=ksap: e.tensor_tensor(
        RHS, vT, ap, OP.subtract), deps=[tvte, tks])
    c.bank_gate2[bks] = t1
    t2 = dve(c, lambda e, RHS=RHS, h=h, par=par: e.tensor_scalar(
        RHS, RHS, c.btr[:, par, h:h + 1], None, OP.mult), deps=[t1])
    dap, tdl, bdl = mmp(c, [
        (lambda e, ap, PZ=r["PZ"], RHS=RHS: e.matmul(
            ap, PZ, RHS, start=True, stop=True))], [r["tPZ"], t2])
    dlt = c.dlt[:, ph]
    tdle = act(c, lambda e, dlt=dlt, ap=dap: e.activation(dlt, ap, AF.Copy),
               deps=[tdl])
    c.bank_gate2[bdl] = tdle
    kdap, tkd, bkd = mmp(c, [
        (lambda e, ap, Khat=r["Khat"], dlt=dlt: e.matmul(
            ap, Khat, dlt, start=True, stop=True))], [r["tKhat"], tdle])
    tso = dve(c, lambda e, So=S_out_ap, Si=S_in_ap, gC=r["gC"], ap=kdap:
              e.scalar_tensor_tensor(So, Si, gC, ap, OP.mult, OP.add),
              deps=[tkd, r["tgc"], S_in_tok])
    c.bank_gate2[bkd] = tso
    return tso, dlt, tdle


def plan_delta(c):
    sb = c.sbuf
    c.mxch = sb("mxch", (128, NCT, CH + 3))
    c.cvo = sb("cvo", (128, 1, NCT, CH))
    c.sqk = sb("sqk", (128, 2, CH))
    c.rnq = sb("rnq", (128, 2, CH))
    c.S_a = sb("S_a", (128, H, 128))
    c.S_b = sb("S_b", (128, H, 128))
    c.Xt = sb("Xt", (128, H, 128))
    c.XTt = sb("XTt", (128, H, 128))
    c.Btr = sb("Btr", (128, 2, 16))
    c.btr = sb("btr", (128, 2, 16))
    c.bntr = sb("bntr", (128, 2, 16))
    c.eBtr = sb("eBtr", (128, 2, 16))
    for nm in ("tmp", "t1", "t1b", "Nsb", "Zsb", "PZ", "Ma", "Mb", "Za",
               "Zb", "kb", "kT", "vT", "Khat", "RHS", "dlt", "Ysb", "MT"):
        setattr(c, nm, sb(nm, (128, NPH, 128)))
    c.cols = sb("dcols", (128, NPH, 8))
    c.gbc = sb("gbc", (128, NPH, 128))

    c.chunk_gate = {0: [c.bar], 1: [c.bar]}
    c.head_gate = {p: [c.bar] for p in range(NPH)}
    c.cvw_tok = {}
    c.mx_war = []
    c.sq_war = {}
    c.rn_war = {}

    tz = [gp(c, lambda e: e.memset(c.S_a[:], 0.0), deps=[c.bar])]
    tS = {h: tz[0] for h in range(H)}
    S_cur, S_nxt = c.S_a, c.S_b

    tX = {}
    for i in range(NCH):
        par = i % 2
        dep_cv, dep_cols = delta_ingest(c, i, pass2=False)
        chunk_term = []
        for h in range(H):
            ph = h % NPH
            r = delta_head_common(c, i, h, par, ph, dep_cv, dep_cols)
            tso, dlt, tdle = delta_state_update(
                c, i, h, par, ph, r, S_cur[:, h, :], tS[h], S_nxt[:, h, :])
            tS[h] = tso
            kbb = c.kb[:, ph]
            tkbb = dve(c, lambda e, kbb=kbb, kT=r["kT"], bg=r["bg"]:
                       e.tensor_scalar(kbb, kT, bg, None, OP.mult),
                       deps=[r["tkT"], r["tbg"], tso])
            yap, tY, by = mmp(c, [
                (lambda e, ap, PZ=r["PZ"], kbb=kbb: e.matmul(
                    ap, PZ, kbb, start=True, stop=True))],
                [r["tPZ"], tkbb])
            Y = c.Ysb[:, ph]
            tYe = act(c, lambda e, Y=Y, ap=yap: e.activation(Y, ap, AF.Copy),
                      deps=[tY])
            c.bank_gate2[by] = tYe
            ykap, tYK, byk = mmp(c, [
                (lambda e, ap, Y=Y, Khat=r["Khat"]: e.matmul(
                    ap, Y, Khat, start=True, stop=True))],
                [tYe, r["tKhat"]])
            MT = c.MT[:, ph]
            tMT = dve(c, lambda e, MT=MT, gC=r["gC"], ap=ykap:
                      e.scalar_tensor_tensor(MT, c.eye[:], gC, ap,
                                             OP.mult, OP.subtract),
                      deps=[tYK, r["tgc"], c.tc["eye"]])
            c.bank_gate2[byk] = tMT
            if i == 0:
                tx1 = act(c, lambda e, h=h, MT=MT: e.activation(
                    c.Xt[:, h, :], MT, AF.Copy), deps=[tMT])
                tx2 = act(c, lambda e, h=h, MT=MT: e.activation(
                    c.XTt[:, h, :], MT, AF.Copy), deps=[tMT])
                tX[h] = [tx1, tx2]
            else:
                xap, txa, bx = mmp(c, [
                    (lambda e, ap, h=h, MT=MT: e.matmul(
                        ap, c.XTt[:, h, :], MT, start=True, stop=True))],
                    tX[h] + [tMT])
                x2ap, txb, bx2 = mmp(c, [
                    (lambda e, ap, h=h, MT=MT: e.matmul(
                        ap, MT, c.XTt[:, h, :], start=True, stop=True))],
                    tX[h] + [tMT])
                tx1 = act(c, lambda e, h=h, ap=xap: e.activation(
                    c.Xt[:, h, :], ap, AF.Copy), deps=[txa, txb])
                tx2 = act(c, lambda e, h=h, ap=x2ap: e.activation(
                    c.XTt[:, h, :], ap, AF.Copy), deps=[txb, tx1])
                c.bank_gate2[bx] = tx1
                c.bank_gate2[bx2] = tx2
                tX[h] = [tx1, tx2]
            c.head_gate[ph] = [tMT, tx2, tso]
            chunk_term += [tso, tx2]
        c.chunk_gate[par] = chunk_term
        S_cur, S_nxt = S_nxt, S_cur
    cc2v = c.ds["cc2_in"].rearrange("(h s p) t -> p h s t", p=128, s=2)
    tw1 = dma_out(c, "wcc2", cc2v[:, :, 0, :], c.Xt[:],
                  [t for h in range(H) for t in tX[h]])
    tw2 = dma_out(c, "wcc2", cc2v[:, :, 1, :], S_cur[:],
                  [tS[h] for h in range(H)] + [tw1])
    tcc = ccop(c, lambda e: e.collective_compute(
        "AllGather", OP.bypass, replica_groups=RG,
        ins=[c.ds["cc2_in"]], outs=[c.ds["cc2_out"]]), deps=[tw2])
    gout = c.ds["cc2_out"].rearrange("g (h s p) t -> p g h s t", p=128, s=2)
    Sin = S_nxt
    gbuf = c.gx = sb("gx", (128, GRP, 2, 128))
    pre = c.pre = sb("pre", (128, 2, 128))
    tSin = {}
    for h in range(H):
        war_g = [tSin[h - 1]] if h > 0 else []
        tga = dma_in(c, "dcc2", gbuf[:, :, 0, :],
                     gout[:, :, h, 0, :], [tcc] + war_g)
        tg = dma_in(c, "dcc2", gbuf[:, :, 1, :],
                    gout[:, :, h, 1, :], [tga])
        tp = dve(c, lambda e: e.tensor_copy(pre[:, 0, :], gbuf[:, 0, 1, :]),
                 deps=[tg])
        tsel = dve(c, lambda e, h=h: e.tensor_scalar(
            Sin[:, h, :], pre[:, 0, :], c.sel[:, 1:2], None, OP.mult),
            deps=[tp, c.tc["sel"], tS[h]])
        for rr in (2, 3):
            pap, tpp, bpp = mmp(c, [
                (lambda e, ap, rr=rr: e.matmul(
                    ap, gbuf[:, rr - 1, 0, :], pre[:, (rr - 2) % 2, :],
                    start=True, stop=True))], [tp, tg])
            tp = dve(c, lambda e, rr=rr, ap=pap: e.tensor_tensor(
                pre[:, (rr - 1) % 2, :], ap, gbuf[:, rr - 1, 1, :], OP.add),
                deps=[tpp, tg])
            c.bank_gate2[bpp] = tp
            tsel = dve(c, lambda e, h=h, rr=rr: e.scalar_tensor_tensor(
                Sin[:, h, :], pre[:, (rr - 1) % 2, :], c.sel[:, rr:rr + 1],
                Sin[:, h, :], OP.mult, OP.add), deps=[tp, tsel])
        tSin[h] = tsel
    c.zch = sb("zch", (128, 1, KT, CH))
    c.ogn = sb("ogn", (128, 1, KT, CH))
    c.oT = sb("oT", (128, NPH, 128))
    c.rms = sb("rms", (128, NPH, 128))
    S_cur, S_nxt = Sin, S_cur
    tS = dict(tSin)
    for i in range(NCH):
        par = i % 2
        chsl = slice(i * CH, (i + 1) * CH)
        dep_cv, dep_cols = delta_ingest(c, i, pass2=True)
        tzc = dma_in(c, "dz2", c.zch[:, 0],
                     fm(c.ds["z_scr"])[:, :, chsl],
                     c.chunk_gate[0] + c.chunk_gate[1])
        tzs = act(c, lambda e: e.activation(
            c.zch[:, 0], c.zch[:, 0], AF.Silu), deps=[tzc])
        chunk_term = []
        ogn_toks = []
        for h in range(H):
            ph = h % NPH
            r = delta_head_common(c, i, h, par, ph, dep_cv, dep_cols)
            tso, dlt, tdle = delta_state_update(
                c, i, h, par, ph, r, S_cur[:, h, :], tS[h], S_nxt[:, h, :])
            tS[h] = tso
            qn = c.cvo[:, 0, h, :]
            kqap, tkq, bkq = mmp(c, [
                (lambda e, ap, kn=r["kn"], qn=qn: e.matmul(
                    ap, kn, qn, start=True, stop=True))],
                dep_cv + [c.head_gate[ph][0]])
            t1b = c.t1b[:, ph]
            tgt = dve(c, lambda e, t1b=t1b, tmp=r["tmp"]: e.tensor_tensor(
                t1b, tmp, c.maskT_incl[:], OP.min),
                deps=[r["t_tmp"], c.tc["maskT_incl"], tso])
            tge = act(c, lambda e, t1b=t1b: e.activation(t1b, t1b, AF.Exp),
                      deps=[tgt])
            PT = c.MT[:, ph]
            tpt = dve(c, lambda e, PT=PT, t1b=t1b, ap=kqap: e.tensor_tensor(
                PT, t1b, ap, OP.mult), deps=[tge, tkq])
            c.bank_gate2[bkq] = tpt
            qb = c.kb[:, ph]
            tqb = dve(c, lambda e, qb=qb, qn=qn, gb=r["gbc"]: e.tensor_tensor(
                qb, qn, gb, OP.mult), deps=[r["t_gbc"], tpt])
            oap, to1, bo = mmp(c, [
                (lambda e, ap, S=S_cur[:, h, :], qb=qb: e.matmul(
                    ap, S, qb, start=True, stop=False)),
                (lambda e, ap, dlt=dlt, PT=PT: e.matmul(
                    ap, dlt, PT, start=False, stop=True))],
                [tso, tqb, tdle, tpt])
            if DEV_DUMP and i == 1 and h in (0, 2):
                if not hasattr(c, "dbg_war"):
                    c.dbg_war = []
                if not hasattr(c, "dbg_d"):
                    c.dbg_d = c.nc.dram_tensor(
                        "dbg_d", [2, 4, 128, 128], F32,
                        kind="ExternalOutput").ap()
                    c.dbg_d2 = c.nc.dram_tensor(
                        "dbg_d2", [2, 128, 128], F32,
                        kind="ExternalOutput").ap()
                hh = 0 if h == 0 else 1
                d1 = dma_out(c, "wdbgd", c.dbg_d[hh, 0], dlt, [tdle])
                d2 = dma_out(c, "wdbgd", c.dbg_d[hh, 1], PT, [tpt, d1])
                d3 = dma_out(c, "wdbgd", c.dbg_d[hh, 2], qb, [tqb, d2])
                d3b = dma_out(c, "wdbgd", c.dbg_d2[hh], r["gbc"],
                              [r["t_gbc"], d3, tqb])
                t1ap, tt1, bt1 = mmp(c, [
                    (lambda e, ap, S=S_cur[:, h, :], qb=qb: e.matmul(
                        ap, S, qb, start=True, stop=True))],
                    [tso, tqb])
                if not hasattr(c, "dbgoT"):
                    c.dbgoT = c.sbuf("dbgoT", (128, 128))
                tt1e = dve(c, lambda e, ap=t1ap: e.tensor_copy(
                    c.dbgoT[:], ap), deps=[tt1] + c.dbg_war)
                c.bank_gate2[bt1] = tt1e
                d4 = dma_out(c, "wdbgd", c.dbg_d[hh, 3], c.dbgoT[:],
                             [tt1e, d3])
                c.dbg_war = [d4]
                c.dbg_extra = [d3, d3b, d4]
            oT = c.oT[:, ph]
            if DEV_RAWO:
                tg1 = dve(c, lambda e, oT=oT, ap=oap: e.tensor_copy(oT, ap),
                          deps=[to1, tzs])
            else:
                tg1 = dve(c, lambda e, oT=oT, ap=oap, h=h:
                          e.tensor_tensor(oT, ap, c.zch[:, 0, h, :], OP.mult),
                          deps=[to1, tzs])
            c.bank_gate2[bo] = tg1
            tg2 = act(c, lambda e, oT=oT, ph=ph: e.activation(
                c.rms[:, ph], oT, AF.Square), deps=[tg1])
            rmap, tr1, br = mmp(c, [
                (lambda e, ap, ph=ph: e.matmul(
                    ap, c.ones[:], c.rms[:, ph], start=True, stop=True))],
                [tg2, c.tc["ones"]])
            trs = act(c, lambda e, ph=ph, ap=rmap: e.activation(
                c.rms[:, ph], ap, AF.Sqrt, bias=c.kc(2), scale=1.0 / 128.0),
                deps=[tr1])
            c.bank_gate2[br] = trs
            trr = dve(c, lambda e, ph=ph: e.reciprocal(
                c.rms[:, ph], c.rms[:, ph]), deps=[trs])
            if DEV_RAWO:
                tg3 = dve(c, lambda e, oT=oT, ph=ph: e.tensor_scalar_mul(
                    c.rms[:, ph], c.rms[:, ph], 1.0), deps=[trr, tg1])
                tg3 = tg1
            else:
                tg3 = dve(c, lambda e, oT=oT, ph=ph: e.tensor_tensor(
                    oT, oT, c.rms[:, ph], OP.mult), deps=[trr, tg1])
            if DEV_RAWO:
                tg4 = tg3
            else:
                tg4 = dve(c, lambda e, oT=oT: e.tensor_scalar(
                    oT, oT, c.norm_w[:], None, OP.mult),
                    deps=[tg3, c.tc["norm_w"]])
            tg5 = act(c, lambda e, h=h, oT=oT: e.activation(
                c.ogn[:, 0, h, :], oT, AF.Copy),
                deps=[tg4] + c.chunk_gate[0] + c.chunk_gate[1])
            ogn_toks.append(tg5)
            c.head_gate[ph] = [tso, tg5] + getattr(c, "dbg_extra", [])
            c.dbg_extra = []
            chunk_term += [tso, tg5]
        tw = dma_out(c, "wog", fm(c.ds["ogn_scr"])[:, :, chsl],
                     c.ogn[:, 0], ogn_toks)
        if DEV_DUMP:
            if not hasattr(c, "dbg_s"):
                c.dbg_s = c.nc.dram_tensor("dbg_s", [NCH, H * 128, 128], F32,
                                           kind="ExternalOutput").ap()
            dma_out(c, "wdbgs", c.dbg_s[i].rearrange("(h p) t -> p h t", p=128),
                    S_nxt[:], [tS[h] for h in range(H)])
        c.chunk_gate[par] = chunk_term + [tw]
        S_cur, S_nxt = S_nxt, S_cur
    c.tok_ogn_done = barrier(c)


def plan_wout(c):
    sb = c.sbuf
    ognb = sb("ognb", (128, 2, T_OWN))
    wslab = sb("wslab", (128, 2, 512))
    stg = sb("wo_stg", (128, 2, 512))
    NMC = T_OWN // 512 if T_OWN >= 512 else 1
    MVC = min(512, T_OWN)
    NG = 4
    ob_gate = {0: c.bar, 1: c.bar}
    ws_gate = {0: c.bar, 1: c.bar}
    stg_gate = {0: c.bar, 1: c.bar}
    for ng in range(NG):
        mm_toks = {}
        for k in range(KT):
            kb = k % 2
            tog = dma_in(c, "dogn%d" % kb, ognb[:, kb, :],
                         fm(c.ds["ogn_scr"])[:, k, :],
                         [ob_gate[kb], c.bar])
            tws = dma_in(c, "dwo%d" % kb, wslab[:, kb, :],
                         c.din["w_out0"][k * 128:(k + 1) * 128,
                                         ng * 512:(ng + 1) * 512],
                         [ws_gate[kb]])
            for j in range(4):
                for mc in range(NMC):
                    bank = j * NMC + mc
                    msl = slice(mc * 512, mc * 512 + MVC)
                    war = c.bank_gate2.get(bank) if k == 0 else None
                    tmm = mm_group(c, [
                        (lambda e, k=k, kb=kb, j=j, msl=msl, bank=bank:
                         e.matmul(c.ps[:, bank, :MVC],
                                  wslab[:, kb, j * 128:(j + 1) * 128],
                                  ognb[:, kb, msl],
                                  start=(k == 0), stop=(k == KT - 1)))],
                        [tog, tws] + ([war] if war else []))
                    mm_toks[(j, mc)] = tmm
            ob_gate[kb] = mm_toks[(3, NMC - 1)]
            ws_gate[kb] = mm_toks[(3, NMC - 1)]
        for j in range(4):
            for mc in range(NMC):
                bank = j * NMC + mc
                sg = (j * NMC + mc) % 2
                tev = dve(c, lambda e, bank=bank, sg=sg: e.tensor_copy(
                    stg[:, sg, :MVC], c.ps[:, bank, :MVC]),
                    deps=[mm_toks[(j, mc)]] +
                    ([stg_gate[sg]] if stg_gate[sg] else []))
                c.bank_gate2[bank] = tev
                n = ng * 4 + j
                tw = dma_out(c, "wsat%d" % sg,
                             fm(c.ds["sattn_scr"])[:, n,
                                                   mc * 512:mc * 512 + MVC],
                             stg[:, sg, :MVC], [tev])
                stg_gate[sg] = tw


def plan_ffn(c):
    sb = c.sbuf
    c.xh2 = sb("xh2", (128, KT, T_HALF))
    c.xnhi2 = sb("xnhi2", (128, KT, T_HALF), BF16)
    c.xnlo2 = sb("xnlo2", (128, KT, T_HALF), BF16)
    c.hbhi = sb("hbhi", (128, KT, T_HALF), BF16)
    c.hblo = sb("hblo", (128, KT, T_HALF), BF16)
    wstF = sb("w_stage2", (128, 2, KT, 128))
    whlF = sb("whl2", (128, 2, 2, KT, 128), BF16)
    c.wtbc = sb("wtbc", (128, T_HALF))
    c.gstg = sb("gstg", (128, MV))
    c.sath = sb("sath", (128, 2, T_HALF))
    c.xstr = sb("xstr", (128, 2, T_HALF))
    c.mbc1 = c.sbuf_p("mbc1", (128, NHALF, T_HALF))
    c.rstdn1 = c.sbuf_p("rstdn1", (128, NHALF, T_HALF))
    c.vsc = sb("vsc2", (128, T_HALF))
    c.sqx = sb("sqx2", (128, 2, MV))

    c.wstage_gate = {}
    c.whl_gate = {}
    c.bank_gate = dict(c.bank_gate2)
    c.bank_rr = 0
    c.gstg_war0 = None
    c.hbuf_war = {}
    c.wtbc_war = None
    c.b7_war = c.bar
    c.sath_war = {}
    c.xstr_war = {}
    tok_x1w = []

    def mm3(c, bank, buf, mhi, mlo, deps):
        emits = []
        for k in range(KT):
            emits.append(lambda e, k=k, bank=bank, buf=buf, f=(k == 0):
                         e.matmul(c.ps[:, bank, :MV], whlF[:, buf, 0, k, :],
                                  mhi[:, k, :MV], start=f, stop=False))
            emits.append(lambda e, k=k, bank=bank, buf=buf:
                         e.matmul(c.ps[:, bank, :MV], whlF[:, buf, 0, k, :],
                                  mlo[:, k, :MV], start=False, stop=False))
        for k in range(KT):
            emits.append(lambda e, k=k, bank=bank, buf=buf, l=(k == KT - 1):
                         e.matmul(c.ps[:, bank, :MV], whlF[:, buf, 1, k, :],
                                  mhi[:, k, :MV], start=False, stop=l))
        return mm_group(c, emits, deps)

    prev_half = None
    c.wbuf_rr = 0
    for h in range(NHALF):
        hsl = slice(h * T_HALF, (h + 1) * T_HALF)
        xsrc = fm(c.din["xT"])[:, :, hsl]
        tx = dma_in(c, "dx2%d" % (h % 2), c.xh2[:], xsrc,
                    [prev_half, c.bar])
        mbc = c.mbc0[:, h, :]
        rstdn = c.rstdn0[:, h, :]
        xn_toks = build_xn(c, c.xh2, c.xh2, mbc, rstdn, [tx, c.bar], None)
        thi = tlo = None
        for k in range(KT):
            thi, tlo = xsplit_k(c, c.xnhi2[:, k, :], c.xnlo2[:, k, :],
                                c.xh2[:, k, :], [xn_toks[k]])
        txn = [thi, tlo]
        tz0 = gp(c, lambda e: e.memset(c.xh2[:], 0.0),
                 deps=[thi, tlo, prev_half, c.bar])
        tacc = {}
        acc_last = tz0

        seq = []
        for ex in range(E):
            for s in range(KT):
                seq.append(("g", ex, s))
                seq.append(("u", ex, s))
            for n in range(KT):
                seq.append(("d", ex, n))

        def wload3(i):
            kind, ex, s = seq[i]
            mat = {"g": "ffn_wg", "u": "ffn_wu", "d": "ffn_wd"}[kind]
            buf = i % 2
            war = c.wstage_gate.get(buf)
            tl = dma_in(c, "dw%s%d" % (kind, buf), wstF[:, buf],
                        c.din[mat][ex][:, s * 128:(s + 1) * 128]
                        .rearrange("(k p) cc -> p k cc", p=128),
                        [war] if war else [])
            wg = c.whl_gate.get(buf)
            th, tsp = wsplit(c, whlF[:, buf, 0], whlF[:, buf, 1],
                             wstF[:, buf], [tl] + ([wg] if wg else []))
            c.wstage_gate[buf] = tsp
            return buf, th, tsp

        pend = wload3(0)
        gk_state = None
        hb_toks = []
        for i, (kind, ex, s) in enumerate(seq):
            buf, th, tsp = pend
            if i + 1 < len(seq):
                pend = wload3(i + 1)
            if kind == "g" and s == 0:
                twt = mm_group(c, [
                    (lambda e, ex=ex, hsl=hsl: e.matmul(
                        c.ps[:, 7, :T_HALF],
                        c.oh16[0:8, ex * 128:(ex + 1) * 128],
                        c.wt8[:, hsl], start=True, stop=True))],
                    [c.tc["oh16"], c.bar, c.wtbc_war, c.b7_war])
                twtb = dve(c, lambda e: e.tensor_copy(
                    c.wtbc[:], c.ps[:, 7, :T_HALF]), deps=[twt])
                c.b7_war = twtb
            bank = c.bank_rr
            c.bank_rr = (c.bank_rr + 1) % 4
            bwar = c.bank_gate.get(bank)
            if kind == "g":
                tmm = mm3(c, bank, buf, c.xnhi2, c.xnlo2,
                          [th, tsp] + txn + ([bwar] if bwar else []))
                c.whl_gate[buf] = tmm
                tsi = act(c, lambda e, bank=bank: e.activation(
                    c.gstg[:], c.ps[:, bank, :MV], AF.Silu),
                    deps=[tmm] + ([c.gstg_war0] if c.gstg_war0 else []))
                c.bank_gate[bank] = tsi
                gk_state = tsi
            elif kind == "u":
                tmm = mm3(c, bank, buf, c.xnhi2, c.xnlo2,
                          [th, tsp] + txn + ([bwar] if bwar else []))
                c.whl_gate[buf] = tmm
                th1 = dve(c, lambda e, bank=bank: e.tensor_tensor(
                    c.gstg[:], c.gstg[:], c.ps[:, bank, :MV], OP.mult),
                    deps=[gk_state, tmm])
                th2 = dve(c, lambda e: e.tensor_tensor(
                    c.gstg[:], c.gstg[:], c.wtbc[:], OP.mult),
                    deps=[th1, twtb])
                c.bank_gate[bank] = th1
                hwar = c.hbuf_war.get(s)
                tbh = act(c, lambda e, s=s: e.activation(
                    c.hbhi[:, s, :], c.gstg[:], AF.Copy),
                    deps=[th2] + ([hwar] if hwar else []))
                tbl = dve(c, lambda e, s=s: e.tensor_tensor(
                    c.hblo[:, s, :], c.gstg[:], c.hbhi[:, s, :],
                    OP.subtract), deps=[tbh])
                c.gstg_war0 = tbl
                hb_toks = [tbh, tbl]
            else:
                tmm = mm3(c, bank, buf, c.hbhi, c.hblo,
                          [th, tsp] + hb_toks + ([bwar] if bwar else []))
                c.whl_gate[buf] = tmm
                tad = dve(c, lambda e, s=s, bank=bank: e.tensor_tensor(
                    c.xh2[:, s, :], c.xh2[:, s, :], c.ps[:, bank, :MV],
                    OP.add), deps=[tmm, tacc.get(s, tz0), acc_last])
                tacc[s] = tad
                acc_last = tad
                c.bank_gate[bank] = tad
                if s == KT - 1:
                    c.hbuf_war = {k2: acc_last for k2 in range(KT)}
                    c.wtbc_war = acc_last
        x1_toks = []
        for s in range(KT):
            sp2 = s % 2
            tsa = dma_in(c, "dsat%d" % sp2, c.sath[:, sp2, :],
                         fm(c.ds["sattn_scr"])[:, s, hsl],
                         [c.bar, c.sath_war.get(sp2)])
            txr = dma_in(c, "dxr%d" % sp2, c.xstr[:, sp2, :],
                         fm(c.din["xT"])[:, s, hsl],
                         [c.xstr_war.get(sp2)])
            t1 = act(c, lambda e, s=s: e.activation(
                c.xh2[:, s, :], c.xh2[:, s, :], AF.Silu),
                deps=[acc_last])
            t2 = dve(c, lambda e, s=s, sp2=sp2: e.tensor_tensor(
                c.xh2[:, s, :], c.xh2[:, s, :], c.sath[:, sp2, :],
                OP.mult), deps=[t1, tsa])
            c.sath_war[sp2] = t2
            t3 = dve(c, lambda e, s=s, sp2=sp2: e.tensor_tensor(
                c.xh2[:, s, :], c.xh2[:, s, :], c.xstr[:, sp2, :],
                OP.add), deps=[t2, txr])
            c.xstr_war[sp2] = t3
            x1_toks.append(t3)
        tw1 = dma_out(c, "wx1", fm(c.ds["x1_scr"])[:, :, hsl], c.xh2[:],
                      x1_toks)
        tok_x1w.append(tw1)
        st_toks = plan_stats(c, c.xh2, c.sqx, c.mbc1[:, h, :],
                             c.rstdn1[:, h, :], c.vsc, x1_toks[-1],
                             [getattr(c, "st1_war", None)])
        c.st1_war = st_toks[-1]
        prev_half = barrier(c)
    c.tok_x1w = tok_x1w


def plan_l1attn(c):
    sb = c.sbuf
    x1b = sb("x1b", (128, KT, T_OWN))
    wst = sb("w_stage3", (128, 2, KT, 128))
    pstg = sb("pstg", (128, 3, T_OWN))
    zt = sb("zt1", (128, T_OWN))
    scn = sb("scn", (128, 2, T_OWN))
    cc3stage = sb("cc3stage", (128, KT, 2))
    gath3 = sb("gath3", (128, GRP, KT, 2))
    carry = sb("carry", (128, 3, KT))
    bdec = sb("bdec", (128, KT))
    c.gstg1 = sb("gstg1", (128, 2, 512))
    NMC = max(1, T_OWN // 512)
    MVC = min(512, T_OWN)

    tz = gp(c, lambda e: e.memset(zt[:], 0.0), deps=[c.bar])
    tx1 = dma_in(c, "dx1", x1b[:], fm(c.ds["x1_scr"]), [c.bar])
    txn1 = None
    for hc in range(NHALF):
        hsl0 = slice(hc * T_HALF, (hc + 1) * T_HALF)
        for k in range(KT):
            t1_ = dve(c, lambda e, k=k, hsl0=hsl0, hc=hc:
                      e.scalar_tensor_tensor(
                          x1b[:, k, hsl0], c.mbc1[:, hc, :], 1.0,
                          x1b[:, k, hsl0], OP.mult, OP.subtract),
                      deps=[tx1, c.bar])
            txn1 = dve(c, lambda e, k=k, hsl0=hsl0, hc=hc: e.tensor_tensor(
                x1b[:, k, hsl0], x1b[:, k, hsl0], c.rstdn1[:, hc, :],
                OP.mult), deps=[t1_])
    tbd = dve(c, lambda e: e.tensor_tensor(
        bdec[:], c.tensors["l1bd"][:], c.tensors["l1db"][:], OP.add),
        deps=[c.tc["l1_b_dec"], c.tc["l1_dec_bias"], c.bar])

    c.wstage_gate = {0: c.bar, 1: c.bar}
    c.bank_gate = dict(c.bank_gate2)
    c.bank_rr = 0
    pstg_war = [c.bar, c.bar, c.bar]
    scn_war = [c.bar, c.bar]
    cc3_toks = []
    wdec_toks = {}
    wd1_toks = {}
    wgate_toks = {}

    def proj(mat, s, buf, out_slot, bias_col, af, extra_dep):
        war = c.wstage_gate.get(buf)
        tl = dma_in(c, "dl1%d" % buf, wst[:, buf],
                    c.din[mat][:, s * 128:(s + 1) * 128]
                    .rearrange("(k p) cc -> p k cc", p=128),
                    [war] + ([extra_dep] if extra_dep else []))
        touts = []
        for mc in range(NMC):
            msl = slice(mc * 512, mc * 512 + MVC)
            bk = c.bank_rr
            c.bank_rr = (c.bank_rr + 1) % 4
            bw = c.bank_gate.get(bk)
            tmm = mm_group(c, [
                (lambda e, k=k, buf=buf, bk=bk, msl=msl, f=(k == 0),
                 l=(k == KT - 1):
                 e.matmul(c.ps[:, bk, :MVC], wst[:, buf, k, :],
                          x1b[:, k, msl], start=f, stop=l))
                for k in range(KT)],
                [tl, txn1] + ([bw] if bw else []))
            out_ap = pstg[:, out_slot, msl]
            ta = act(c, lambda e, out_ap=out_ap, bias_col=bias_col, af=af,
                     bk=bk: e.activation(out_ap, c.ps[:, bk, :MVC], af,
                                         bias=bias_col),
                     deps=[tmm, tbd] + c.tok_consts_l1 +
                     ([pstg_war[out_slot]] if pstg_war[out_slot] else []))
            c.bank_gate[bk] = ta
            touts.append(ta)
        c.wstage_gate[buf] = touts[-1]
        return touts

    c.tok_consts_l1 = [c.tc["l1_b_in"], c.tc["l1_b_gate"], c.tc["l1_b_out"]]
    for s in range(KT):
        t_in = proj("l1_w_in", s, s % 2, 0, c.tensors["l1bi"][:, s:s + 1],
                    AF.Silu, None)
        t_dec = proj("l1_w_dec", s, (s + 1) % 2, 1, bdec[:, s:s + 1],
                     AF.Sigmoid, None)
        td1 = dve(c, lambda e: e.tensor_tensor(
            pstg[:, 2, :], pstg[:, 1, :], pstg[:, 0, :], OP.mult),
            deps=t_in + t_dec + ([pstg_war[2]] if pstg_war[2] else []))
        td1 = dve(c, lambda e: e.tensor_tensor(
            pstg[:, 2, :], pstg[:, 0, :], pstg[:, 2, :], OP.subtract),
            deps=[td1])
        ts1 = dve(c, lambda e: e.tensor_tensor_scan(
            scn[:, 0, :], pstg[:, 1, :], pstg[:, 2, :], 0.0, OP.mult,
            OP.add), deps=[td1] + ([scn_war[0]] if scn_war[0] else []))
        ts2 = dve(c, lambda e: e.tensor_tensor_scan(
            scn[:, 1, :], pstg[:, 1, :], zt[:], 1.0, OP.mult, OP.add),
            deps=[td1, tz] + ([scn_war[1]] if scn_war[1] else []))
        te1 = dve(c, lambda e, s=s: e.tensor_copy(
            cc3stage[:, s, 0:1], scn[:, 1, T_OWN - 1:T_OWN]), deps=[ts2])
        te2 = dve(c, lambda e, s=s: e.tensor_copy(
            cc3stage[:, s, 1:2], scn[:, 0, T_OWN - 1:T_OWN]), deps=[ts1])
        scn_war = [te2, te1]
        cc3_toks += [te1, te2]
        twd = dma_out(c, "wdec", fm(c.ds["dec_scr"])[:, s, :],
                      pstg[:, 1, :], t_dec + [td1, ts1, ts2])
        twd1 = dma_out(c, "wdd1", fm(c.ds["dd1_scr"])[:, s, :],
                       pstg[:, 2, :], [td1, ts1])
        wdec_toks[s] = twd
        wd1_toks[s] = twd1
        pstg_war[1] = twd
        pstg_war[2] = twd1
        pstg_war[0] = td1
        t_g = proj("l1_w_gate", s, s % 2, 0, c.tensors["l1bg"][:, s:s + 1],
                   AF.Sigmoid, t_in[-1])
        c.x1b_free = t_g[-1]
        twg = dma_out(c, "wgt", fm(c.ds["gate_scr"])[:, s, :],
                      pstg[:, 0, :], t_g)
        wgate_toks[s] = twg
        pstg_war[0] = twg
    tw3 = dma_out(c, "wcc3",
                  c.ds["cc3_in"].rearrange("(s p) two -> p s two", p=128),
                  cc3stage[:], cc3_toks)
    tcc = ccop(c, lambda e: e.collective_compute(
        "AllGather", OP.bypass, replica_groups=RG,
        ins=[c.ds["cc3_in"]], outs=[c.ds["cc3_out"]]), deps=[tw3])
    tg3 = []
    for g in range(GRP):
        tg3.append(dma_in(
            c, "dcc3", gath3[:, g],
            c.ds["cc3_out"][g].rearrange("(s p) two -> p s two", p=128),
            [tcc]))
    tc1 = dve(c, lambda e: e.tensor_copy(carry[:, 0, :], gath3[:, 0, :, 1]),
              deps=tg3)
    tprev = tc1
    for rr in (2, 3):
        t_a = dve(c, lambda e, rr=rr: e.tensor_tensor(
            carry[:, rr - 1, :], gath3[:, rr - 1, :, 0],
            carry[:, rr - 2, :], OP.mult), deps=[tprev] + tg3)
        tprev = dve(c, lambda e, rr=rr: e.tensor_tensor(
            carry[:, rr - 1, :], carry[:, rr - 1, :],
            gath3[:, rr - 1, :, 1], OP.add), deps=[t_a])
    tsel = dve(c, lambda e: e.tensor_scalar(
        carry[:, 0, :], carry[:, 0, :], c.sel[:, 1:2], None, OP.mult),
        deps=[tprev, c.tc["sel"]])
    for rr in (2, 3):
        tsel = dve(c, lambda e, rr=rr: e.scalar_tensor_tensor(
            carry[:, 0, :], carry[:, rr - 1, :], c.sel[:, rr:rr + 1],
            carry[:, 0, :], OP.mult, OP.add), deps=[tsel])
    y_toks = []
    for s in range(KT):
        sl2 = s % 2
        tdd = dma_in(c, "drd%d" % sl2, pstg[:, 1, :],
                     fm(c.ds["dec_scr"])[:, s, :],
                     [wdec_toks[s], tsel] +
                     ([pstg_war[1]] if pstg_war[1] else []))
        tdd1 = dma_in(c, "drd1%d" % sl2, pstg[:, 2, :],
                      fm(c.ds["dd1_scr"])[:, s, :],
                      [wd1_toks[s]] +
                      ([pstg_war[2]] if pstg_war[2] else []))
        ts = dve(c, lambda e, s=s: e.tensor_tensor_scan(
            scn[:, 0, :], pstg[:, 1, :], pstg[:, 2, :],
            carry[:, 0, s:s + 1], OP.mult, OP.add),
            deps=[tdd, tdd1, tsel] + ([scn_war[0]] if scn_war[0] else []))
        pstg_war[1] = ts
        pstg_war[2] = ts
        tgt = dma_in(c, "drg%d" % sl2, pstg[:, 0, :],
                     fm(c.ds["gate_scr"])[:, s, :],
                     [wgate_toks[s]] +
                     ([pstg_war[0]] if pstg_war[0] else []))
        ty = dve(c, lambda e, s=s: e.tensor_tensor(
            x1b[:, s, :], pstg[:, 0, :], scn[:, 0, :], OP.mult),
            deps=[tgt, ts, c.x1b_free])
        pstg_war[0] = ty
        scn_war[0] = ty
        y_toks.append(ty)
    stg_war = [c.bar, c.bar]
    for s in range(KT):
        buf = s % 2
        war = c.wstage_gate.get(buf)
        tl = dma_in(c, "dl1o%d" % buf, wst[:, buf],
                    c.din["l1_w_out"][:, s * 128:(s + 1) * 128]
                    .rearrange("(k p) cc -> p k cc", p=128),
                    [war] + y_toks)
        for mc in range(NMC):
            msl = slice(mc * 512, mc * 512 + MVC)
            bk = c.bank_rr
            c.bank_rr = (c.bank_rr + 1) % 4
            bw = c.bank_gate.get(bk)
            tmm = mm_group(c, [
                (lambda e, k=k, buf=buf, bk=bk, msl=msl, f=(k == 0),
                 l=(k == KT - 1):
                 e.matmul(c.ps[:, bk, :MVC], wst[:, buf, k, :],
                          x1b[:, k, msl], start=f, stop=l))
                for k in range(KT)],
                [tl] + y_toks + ([bw] if bw else []))
            sg = (s * NMC + mc) % 2
            tev = dve(c, lambda e, bk=bk, sg=sg, s=s: e.tensor_scalar(
                pstg[:, sg, :MVC] if False else c.gstg1[:, sg, :MVC],
                c.ps[:, bk, :MVC], c.tensors["l1bo"][:, s:s + 1], None,
                OP.add),
                deps=[tmm, c.tc["l1_b_out"]] +
                ([stg_war[sg]] if stg_war[sg] else []))
            c.bank_gate[bk] = tev
            tw = dma_out(c, "ws1%d" % sg,
                         fm(c.ds["s1_scr"])[:, s, msl],
                         c.gstg1[:, sg, :MVC], [tev])
            stg_war[sg] = tw
        c.wstage_gate[buf] = tw


def plan_bitnet(c):
    import contextlib
    sb = c.sbuf
    wst = sb("w_stage4", (128, 2, KT, 128))
    wqs = sb("wqs", (128, 2, KT, 128), BF16)
    amaxb = sb("amaxb", (128, T_OWN))
    scaleb = sb("scaleb", (128, T_OWN))
    deqb = sb("deqb", (128, T_OWN))
    cols4 = sb("cols4", (128, 16))
    colacc = sb("colacc", (128, 3))
    mrow = sb("mrow", (1, 128))
    NMC = max(1, T_OWN // 512)
    MVC = min(512, T_OWN)
    NQ = max(1, T_OWN // 128)
    QV = min(128, T_OWN)

    tzc = gp(c, lambda e: e.memset(colacc[:], 0.0), deps=[c.bar])
    stat_done = {}
    c.wstage_gate = {}
    tacc = tzc
    for mi, (mat, nk, nstr) in enumerate(
            (("bn_wgT", KT, FI // 128), ("bn_wuT", KT, FI // 128),
             ("bn_wdT", KT, (FI // 2048) * KT))):
        for s in range(nstr):
            buf = s % 2
            war = c.wstage_gate.get(buf)
            if mat == "bn_wdT":
                n_col, kg = divmod(s, FI // 2048)
                srcv = c.din[mat][kg * 2048:(kg + 1) * 2048,
                                  n_col * 128:(n_col + 1) * 128]
            else:
                srcv = c.din[mat][:, s * 128:(s + 1) * 128]
            tl = dma_in(c, "dbs%d" % buf, wst[:, buf],
                        srcv.rearrange("(k p) cc -> p k cc", p=128),
                        [war, c.bar])
            tr = dve(c, lambda e, buf=buf, mi=mi: e.tensor_reduce(
                cols4[:, 13:14], wst[:, buf].rearrange("p a b -> p (a b)"),
                AX.X, OP.add, apply_absolute_value=True),
                deps=[tl, tacc])
            tacc = dve(c, lambda e, mi=mi: e.tensor_tensor(
                colacc[:, mi:mi + 1], colacc[:, mi:mi + 1],
                cols4[:, 13:14], OP.add), deps=[tr])
            c.wstage_gate[buf] = tr
        stat_done[mi] = tacc
    nel = float(FI * D)
    tstat = []
    for mi in range(3):
        ap, tmm, b = mmp(c, [
            (lambda e, ap, mi=mi: e.matmul(
                ap, c.ones[:], colacc[:, mi:mi + 1], start=True,
                stop=True))], [stat_done[mi], c.tc["ones"]], width=1)
        t1 = dve(c, lambda e, ap=ap, mi=mi: e.tensor_scalar(
            cols4[:, mi * 4:mi * 4 + 1], ap, 1.0 / nel, 1e-5,
            OP.mult, OP.max), deps=[tmm])
        c.bank_gate2[b] = t1
        t2 = dve(c, lambda e, mi=mi: e.tensor_scalar(
            cols4[:, mi * 4 + 1:mi * 4 + 2], cols4[:, mi * 4:mi * 4 + 1],
            0.5, None, OP.mult), deps=[t1])
        t2b = dve(c, lambda e, mi=mi: e.tensor_scalar(
            cols4[:, mi * 4 + 2:mi * 4 + 3], cols4[:, mi * 4:mi * 4 + 1],
            -0.5, None, OP.mult), deps=[t1])
        tstat.append((t1, t2, t2b))

    def tern(dst_ap, src_ap, mi, deps):
        t_a = dve(c, lambda e, src_ap=src_ap, mi=mi: e.tensor_scalar(
            ta_s[:], src_ap, cols4[:, mi * 4 + 2:mi * 4 + 3], None,
            OP.is_lt), deps=deps + [tstat[mi][2], c.tern_war])
        t_b = dve(c, lambda e, dst_ap=dst_ap, src_ap=src_ap, mi=mi:
                  e.scalar_tensor_tensor(
                      dst_ap, src_ap, cols4[:, mi * 4 + 1:mi * 4 + 2],
                      ta_s[:], OP.is_gt, OP.subtract),
                  deps=[t_a, tstat[mi][1]])
        c.tern_war = t_b
        return t_b
    c.tern_war = None

    scopeA = contextlib.ExitStack()
    old_es = c.phase_es
    c.phase_es = scopeA
    xqb = sb("xqb", (128, KT, T_OWN), BF16)
    c.phase_es = old_es
    with contextlib.ExitStack() as sub:
        c.phase_es = sub
        x1nb = sb("x1nb", (128, KT, T_OWN))
        c.phase_es = old_es
        tx1 = dma_in(c, "dx1b", x1nb[:], fm(c.ds["x1_scr"]), [c.bar])
        xn_toks = []
        for s in range(KT):
            tks = []
            for hc in range(NHALF):
                hsl = slice(hc * T_HALF, (hc + 1) * T_HALF)
                t1 = dve(c, lambda e, s=s, hc=hc, hsl=hsl:
                         e.scalar_tensor_tensor(
                             x1nb[:, s, hsl], c.mbc1[:, hc, :], 1.0,
                             x1nb[:, s, hsl], OP.mult, OP.subtract),
                         deps=[tx1, c.bar])
                t2 = dve(c, lambda e, s=s, hc=hc, hsl=hsl: e.tensor_tensor(
                    x1nb[:, s, hsl], x1nb[:, s, hsl], c.rstdn1[:, hc, :],
                    OP.mult), deps=[t1])
                tks.append(t2)
            xn_toks.append(tks[-1])
        amax_toks = []
        for tc2 in range(T_OWN // 128):
            tmx = None
            for k in range(KT):
                ap, tt, b = trp(c, x1nb[:, k, tc2 * 128:(tc2 + 1) * 128],
                                c.eye[:], [xn_toks[k]])
                trd = dve(c, lambda e, ap=ap: e.tensor_reduce(
                    cols4[:, 13:14], ap, AX.X, OP.max,
                    apply_absolute_value=True),
                    deps=[tt] + ([tmx] if tmx else []) +
                    ([c.col11_war] if getattr(c, "col11_war", None)
                     else []))
                c.col11_war = trd
                c.bank_gate2[b] = trd
                if k == 0:
                    tmx = dve(c, lambda e: e.tensor_copy(
                        cols4[:, 12:13], cols4[:, 13:14]), deps=[trd])
                else:
                    tmx = dve(c, lambda e: e.tensor_tensor(
                        cols4[:, 12:13], cols4[:, 12:13], cols4[:, 13:14],
                        OP.max), deps=[trd])
            ap2, tt2, b2 = trp(c, cols4[:, 12:13], c.eye[:], [tmx],
                               width=128, rows=1)
            tcp = dve(c, lambda e, ap2=ap2: e.tensor_copy(
                mrow[:1, :128], ap2[:1, :128]), deps=[tt2])
            c.bank_gate2[b2] = tcp
            ap3, tt3, b3 = mmp(c, [
                (lambda e, ap: e.matmul(ap, c.ones[:1, :], mrow[:1, :128],
                                        start=True, stop=True))],
                [tcp, c.tc["ones"]])
            tev = dve(c, lambda e, ap3=ap3, tc2=tc2: e.tensor_copy(
                amaxb[:, tc2 * 128:(tc2 + 1) * 128], ap3), deps=[tt3])
            c.bank_gate2[b3] = tev
            amax_toks.append(tev)
        t1 = dve(c, lambda e: e.tensor_single_scalar(
            amaxb[:], amaxb[:], 1e-5, OP.max), deps=amax_toks)
        t2 = dve(c, lambda e: e.reciprocal(scaleb[:], amaxb[:]), deps=[t1])
        t3 = dve(c, lambda e: e.tensor_scalar_mul(scaleb[:], scaleb[:],
                                                  128.0), deps=[t2])
        t4 = dve(c, lambda e: e.tensor_scalar_mul(deqb[:], amaxb[:],
                                                  1.0 / 128.0), deps=[t1])
        c.tok_deqb = t4
        xq_toks = []
        for s in range(KT):
            tq = dve(c, lambda e, s=s: e.tensor_tensor(
                x1nb[:, s, :], x1nb[:, s, :], scaleb[:], OP.mult),
                deps=[t3, xn_toks[s]])
            tq = act(c, lambda e, s=s: e.activation(
                x1nb[:, s, :], x1nb[:, s, :], AF.Copy, bias=MAGIC),
                deps=[tq])
            tq = act(c, lambda e, s=s: e.activation(
                x1nb[:, s, :], x1nb[:, s, :], AF.Copy, bias=-MAGIC),
                deps=[tq])
            tq = dve(c, lambda e, s=s: e.tensor_single_scalar(
                x1nb[:, s, :], x1nb[:, s, :], 127.0, OP.min), deps=[tq])
            tq = dve(c, lambda e, s=s: e.tensor_single_scalar(
                x1nb[:, s, :], x1nb[:, s, :], -128.0, OP.max), deps=[tq])
            tq = dve(c, lambda e, s=s: e.tensor_copy(
                xqb[:, s, :], x1nb[:, s, :]), deps=[tq])
            xq_toks.append(tq)
        c.xq_done = barrier(c)
    c.bar_q = barrier(c)
    c.phase_es = scopeA
    istg = sb("istg", (128, 3, T_OWN))
    ta_s = sb("ta_s", (128, KT * 128))
    c.phase_es = old_es

    c.wstage_gate = {}
    istg_war = [None, None, None]
    for n in range(FI // 128):
        buf = n % 2
        war = c.wstage_gate.get(buf)
        tlg = dma_in(c, "dbg%d" % buf, wst[:, buf],
                     c.din["bn_wgT"][:, n * 128:(n + 1) * 128]
                     .rearrange("(k p) cc -> p k cc", p=128),
                     [war, c.bar_q])
        ttg = tern(wqs[:, buf].rearrange("p a b -> p (a b)"),
                   wst[:, buf].rearrange("p a b -> p (a b)"), 0, [tlg])
        g_toks = []
        for mc in range(NMC):
            msl = slice(mc * 512, mc * 512 + MVC)
            ap, tmm, b = mmp(c, [
                (lambda e, ap, k=k, buf=buf, msl=msl, f=(k == 0),
                 l=(k == KT - 1):
                 e.matmul(ap[:, :MVC], wqs[:, buf, k, :], xqb[:, k, msl],
                          start=f, stop=l))
                for k in range(KT)], [ttg, c.xq_done], width=512)
            tde = dve(c, lambda e, ap=ap, msl=msl: e.scalar_tensor_tensor(
                istg[:, 0, msl], ap[:, :MVC], cols4[:, 0:1], deqb[:, msl],
                OP.mult, OP.mult),
                deps=[tmm, tstat[0][0], c.tok_deqb] +
                ([istg_war[0]] if istg_war[0] else []))
            c.bank_gate2[b] = tde
            g_toks.append(tde)
        tlu = dma_in(c, "dbu%d" % buf, wst[:, buf],
                     c.din["bn_wuT"][:, n * 128:(n + 1) * 128]
                     .rearrange("(k p) cc -> p k cc", p=128), [ttg])
        ttu = tern(wqs[:, buf].rearrange("p a b -> p (a b)"),
                   wst[:, buf].rearrange("p a b -> p (a b)"), 1,
                   [tlu] + g_toks)
        u_toks = []
        for mc in range(NMC):
            msl = slice(mc * 512, mc * 512 + MVC)
            ap, tmm, b = mmp(c, [
                (lambda e, ap, k=k, buf=buf, msl=msl, f=(k == 0),
                 l=(k == KT - 1):
                 e.matmul(ap[:, :MVC], wqs[:, buf, k, :], xqb[:, k, msl],
                          start=f, stop=l))
                for k in range(KT)], [ttu], width=512)
            tdu = dve(c, lambda e, ap=ap, msl=msl: e.scalar_tensor_tensor(
                istg[:, 1, msl], ap[:, :MVC], cols4[:, 4:5], deqb[:, msl],
                OP.mult, OP.mult),
                deps=[tmm, tstat[1][0], c.tok_deqb] +
                ([istg_war[1]] if istg_war[1] else []))
            c.bank_gate2[b] = tdu
            u_toks.append(tdu)
        tsi = act(c, lambda e: e.activation(
            istg[:, 0, :], istg[:, 0, :], AF.Silu), deps=g_toks)
        tmul = dve(c, lambda e: e.tensor_tensor(
            istg[:, 2, :], istg[:, 0, :], istg[:, 1, :], OP.mult),
            deps=[tsi] + u_toks + ([istg_war[2]] if istg_war[2] else []))
        twi = dma_out(c, "wint%d" % buf, fm(c.ds["int_scr"])[:, n, :],
                      istg[:, 2, :], [tmul])
        istg_war[0] = tmul
        istg_war[1] = tmul
        istg_war[2] = twi
        c.wstage_gate[buf] = ttu
        c.int_w = getattr(c, "int_w", {})
        c.int_w[n] = twi

    wd_toks = {}
    for s in range((FI // 2048) * KT):
        n_col, kg = divmod(s, FI // 2048)
        buf = s % 2
        war = c.wstage_gate.get(buf)
        tl = dma_in(c, "dbd%d" % buf, wst[:, buf],
                    c.din["bn_wdT"][kg * 2048:(kg + 1) * 2048,
                                    n_col * 128:(n_col + 1) * 128]
                    .rearrange("(k p) cc -> p k cc", p=128),
                    [war, c.bar_q])
        tt = tern(wqs[:, buf].rearrange("p a b -> p (a b)"),
                  wst[:, buf].rearrange("p a b -> p (a b)"), 2, [tl])
        tw = dma_out(c, "wwdq%d" % buf,
                     c.ds["wdq_scr"][kg * 2048:(kg + 1) * 2048,
                                     n_col * 128:(n_col + 1) * 128]
                     .rearrange("(k p) cc -> p k cc", p=128),
                     wqs[:, buf], [tt])
        wd_toks[(n_col, kg)] = tw
        c.wstage_gate[buf] = tw
    c.bar_w = barrier(c)
    scopeA.close()

    c.had = c.tensors["had128"]
    intq = sb("intq", (128, FIT, QV))
    iqb = sb("iqb", (128, FIT, QV), BF16)
    bout = sb("bout", (128, KT, QV))
    amax2 = sb("amax2", (128, QV))
    scale2 = sb("scale2", (128, QV))
    deq2 = sb("deq2", (128, QV))
    ostg = sb("ostg", (128, KT, QV))
    s1q = sb("s1q", (128, KT, QV))
    FW_C = 1.0 / math.sqrt(float(FI))
    q_done = None
    out_toks = []
    for qt in range(NQ):
        qsl = slice(qt * QV, (qt + 1) * QV)
        tiq = dma_in(c, "dintq", intq[:, :, :QV],
                     fm(c.ds["int_scr"])[:, :, qsl],
                     [c.bar_w, q_done] + [c.int_w[n]
                                          for n in range(FI // 128)])
        slab_toks = []
        for nsl in range(FIT):
            ap, tmm, b = mmp(c, [
                (lambda e, ap, nsl=nsl: e.matmul(
                    ap[:, :QV], c.had[:], intq[:, nsl, :QV],
                    start=True, stop=True))],
                [tiq, c.tc["had128"]] + slab_toks[-1:], width=256)
            tev = dve(c, lambda e, ap=ap, nsl=nsl: e.tensor_copy(
                intq[:, nsl, :QV], ap[:, :QV]), deps=[tmm])
            c.bank_gate2[b] = tev
            slab_toks.append(tev)
        tlv = slab_toks[-1:]
        for lv in range(6):
            st = 2 ** lv
            va = intq.rearrange("p (a b cc) t -> p a b cc t", b=2, cc=st)
            t_u = dve(c, lambda e, va=va: e.tensor_tensor(
                va[:, :, 1, :, :QV], va[:, :, 0, :, :QV],
                va[:, :, 1, :, :QV], OP.subtract), deps=tlv + slab_toks)
            t_v = dve(c, lambda e, va=va: e.scalar_tensor_tensor(
                va[:, :, 0, :, :QV], va[:, :, 0, :, :QV], 2.0,
                va[:, :, 1, :, :QV], OP.mult, OP.subtract), deps=[t_u])
            tlv = [t_v]
        amax_toks = []
        for tc2 in range(QV // 128):
            tmx = None
            for k in range(FIT):
                ap, tt, b = trp(c, intq[:, k, tc2 * 128:(tc2 + 1) * 128],
                                c.eye[:], tlv)
                trd = dve(c, lambda e, ap=ap: e.tensor_reduce(
                    cols4[:, 13:14], ap, AX.X, OP.max,
                    apply_absolute_value=True),
                    deps=[tt] + ([tmx] if tmx else []) + [c.col11_war])
                c.col11_war = trd
                c.bank_gate2[b] = trd
                if k == 0:
                    tmx = dve(c, lambda e: e.tensor_copy(
                        cols4[:, 12:13], cols4[:, 13:14]), deps=[trd])
                else:
                    tmx = dve(c, lambda e: e.tensor_tensor(
                        cols4[:, 12:13], cols4[:, 12:13], cols4[:, 13:14],
                        OP.max), deps=[trd])
            ap2, tt2, b2 = trp(c, cols4[:, 12:13], c.eye[:], [tmx],
                               width=128, rows=1)
            tcp = dve(c, lambda e, ap2=ap2: e.tensor_copy(
                mrow[:1, :128], ap2[:1, :128]), deps=[tt2])
            c.bank_gate2[b2] = tcp
            ap3, tt3, b3 = mmp(c, [
                (lambda e, ap: e.matmul(ap, c.ones[:1, :], mrow[:1, :128],
                                        start=True, stop=True))],
                [tcp, c.tc["ones"]])
            tev = dve(c, lambda e, ap3=ap3, tc2=tc2: e.tensor_copy(
                amax2[:, tc2 * 128:(tc2 + 1) * 128], ap3), deps=[tt3])
            c.bank_gate2[b3] = tev
            amax_toks.append(tev)
        t1 = dve(c, lambda e: e.tensor_scalar(
            amax2[:], amax2[:], c.kc(6), None, OP.max),
            deps=amax_toks + [c.tc["kcol"]])
        t2 = dve(c, lambda e: e.reciprocal(scale2[:], amax2[:]), deps=[t1])
        t3 = dve(c, lambda e: e.tensor_scalar_mul(scale2[:], scale2[:],
                                                  128.0), deps=[t2])
        t4 = dve(c, lambda e: e.tensor_scalar_mul(
            deq2[:], amax2[:], FW_C / 128.0), deps=[t1])
        iq_toks = []
        for k in range(FIT):
            tq = dve(c, lambda e, k=k: e.tensor_tensor(
                intq[:, k, :QV], intq[:, k, :QV], scale2[:, :QV], OP.mult),
                deps=[t3] + tlv)
            tq = act(c, lambda e, k=k: e.activation(
                intq[:, k, :QV], intq[:, k, :QV], AF.Copy, bias=MAGIC),
                deps=[tq])
            tq = act(c, lambda e, k=k: e.activation(
                intq[:, k, :QV], intq[:, k, :QV], AF.Copy, bias=-MAGIC),
                deps=[tq])
            tq = dve(c, lambda e, k=k: e.tensor_single_scalar(
                intq[:, k, :QV], intq[:, k, :QV], 127.0, OP.min), deps=[tq])
            tq = dve(c, lambda e, k=k: e.tensor_single_scalar(
                intq[:, k, :QV], intq[:, k, :QV], -128.0, OP.max),
                deps=[tq])
            tq = dve(c, lambda e, k=k: e.tensor_copy(
                iqb[:, k, :QV], intq[:, k, :QV]), deps=[tq])
            iq_toks.append(tq)
        dn_gate = {}
        for n in range(KT):
            bank = n % 8
            bwar = dn_gate.get(bank, c.bank_gate2.get(bank))
            mm_last = None
            for kg in range(FI // 2048):
                buf = (n * (FI // 2048) + kg) % 2
                war = c.wstage_gate.get(buf)
                tl = dma_in(c, "dbq%d" % buf, wqs[:, buf],
                            c.ds["wdq_scr"][kg * 2048:(kg + 1) * 2048,
                                            n * 128:(n + 1) * 128]
                            .rearrange("(k p) cc -> p k cc", p=128),
                            [war, wd_toks[(n, kg)]])
                mm_last = mm_group(c, [
                    (lambda e, k=k, buf=buf, bank=bank, kg=kg,
                     f=(kg == 0 and k == 0),
                     l=(kg == FI // 2048 - 1 and k == KT - 1):
                     e.matmul(c.ps[:, bank, :QV], wqs[:, buf, k, :],
                              iqb[:, kg * KT + k, :QV], start=f, stop=l))
                    for k in range(KT)],
                    [tl] + iq_toks + ([bwar] if bwar else []))
                c.wstage_gate[buf] = mm_last
            tde = dve(c, lambda e, bank=bank, n=n: e.scalar_tensor_tensor(
                bout[:, n, :QV], c.ps[:, bank, :QV], cols4[:, 8:9],
                deq2[:, :QV], OP.mult, OP.mult),
                deps=[mm_last, tstat[2][0], t4] + (
                    [c.bout_war.get(n)] if getattr(c, "bout_war", None)
                    and c.bout_war.get(n) else []))
            dn_gate[bank] = tde
            c.bank_gate2[bank] = tde
            c.bout_war = getattr(c, "bout_war", {})
            c.bout_war[n] = tde
        tx1q = dma_in(c, "dx1q", ostg[:], fm(c.ds["x1_scr"])[:, :, qsl],
                      [c.bar_w, q_done])
        for s in range(KT):
            tb = act(c, lambda e, s=s: e.activation(
                bout[:, s, :QV], bout[:, s, :QV], AF.Silu),
                deps=[c.bout_war[s]])
            c.bout_war[s] = tb
        ts1 = dma_in(c, "ds1q", s1q[:],
                     fm(c.ds["s1_scr"])[:, :, qsl], [c.bar, q_done])
        fin_toks = []
        for s in range(KT):
            t5 = dve(c, lambda e, s=s: e.tensor_tensor(
                bout[:, s, :QV], bout[:, s, :QV], s1q[:, s, :QV],
                OP.mult), deps=[c.bout_war[s], ts1])
            t6 = dve(c, lambda e, s=s: e.tensor_tensor(
                ostg[:, s, :QV], ostg[:, s, :QV], bout[:, s, :QV],
                OP.add), deps=[t5, tx1q])
            c.bout_war[s] = t6
            tw = dma_out(c, "wout%d" % (s % 4), fm(c.dout)[:, s, qsl],
                         ostg[:, s, :QV], [t6])
            fin_toks.append(tw)
        out_toks += fin_toks
        q_done = barrier(c)
    c.tok_out = out_toks


def plan_final(c):
    gp(c, lambda e: e.engine_nop(), deps=c.tok_out)


def _constants():
    i = np.arange(128)
    return dict(
        eye=np.eye(128, dtype=np.float32),
        ones=np.ones((128, 128), np.float32),
        oh16=np.kron(np.eye(16, dtype=np.float32), np.ones((1, 128), np.float32)
                     ).reshape(16, 16 * 128) if False else
        np.repeat(np.eye(16, dtype=np.float32), 128, axis=1),
        mask_ge=np.where(i[None, :] >= i[:, None], 1e30, 0.0).astype(np.float32),
        maskT_incl=np.where(i[:, None] <= i[None, :], 1e30, -1e30).astype(np.float32),
        maskT_strict=np.where(i[:, None] < i[None, :], 1e30, -1e30).astype(np.float32),
        had128=(1.0 - 2.0 * (np.array([[bin(i & j).count("1") & 1
                                        for j in range(128)]
                                       for i in range(128)])
                             )).astype(np.float32),
        kcol=np.broadcast_to(np.array(
            [LN_EPS, 1.0, 1e-6, -GOLDEN_CENTER, MAGIC, -MAGIC,
             1e-5 * math.sqrt(float(FI)), 0.0],
            np.float32), (128, 8)).copy(),
    )


def stage_inputs(inputs):
    f32 = np.float32
    co = np.asarray(inputs["L0_attn_conv_w"], f32).reshape(NCT, 128, CK)
    shared = {
        "w_qkv": np.ascontiguousarray(np.asarray(inputs["L0_attn_W_qkv"], f32)),
        "conv_w": np.ascontiguousarray(
            co.transpose(1, 0, 2).reshape(128, NCT * CK)),
        "w_z": np.ascontiguousarray(np.asarray(inputs["L0_attn_W_z"], f32)),
        "w_b": np.ascontiguousarray(np.asarray(inputs["L0_attn_W_b"], f32)),
        "w_a": np.ascontiguousarray(np.asarray(inputs["L0_attn_W_a"], f32)),
        "dt_bias": np.ascontiguousarray(
            np.asarray(inputs["L0_attn_dt_bias"], f32).reshape(H, 1)),
        "a_log": np.ascontiguousarray(
            np.asarray(inputs["L0_attn_A_log"], f32).reshape(H, 1)),
        "norm_w": np.ascontiguousarray(
            np.asarray(inputs["L0_attn_norm_weight"], f32).reshape(128, 1)),
        "w_out0": np.ascontiguousarray(np.asarray(inputs["L0_attn_W_out"], f32)),
        "ffn_wg": np.ascontiguousarray(np.asarray(inputs["L0_ffn_Wg"], f32)),
        "ffn_wu": np.ascontiguousarray(np.asarray(inputs["L0_ffn_Wu"], f32)),
        "ffn_wd": np.ascontiguousarray(np.asarray(inputs["L0_ffn_Wd"], f32)),
        "ffn_wr": np.ascontiguousarray(np.asarray(inputs["L0_ffn_Wr"], f32)),
        "temp": np.ascontiguousarray(
            np.asarray(inputs["L0_ffn_temperature"], f32).reshape(1, 1)),
        "l1_w_in": np.ascontiguousarray(np.asarray(inputs["L1_attn_W_in"], f32)),
        "l1_w_dec": np.ascontiguousarray(np.asarray(inputs["L1_attn_W_decay"], f32)),
        "l1_w_gate": np.ascontiguousarray(np.asarray(inputs["L1_attn_W_gate"], f32)),
        "l1_w_out": np.ascontiguousarray(np.asarray(inputs["L1_attn_W_out"], f32)),
        "bn_wgT": np.ascontiguousarray(np.asarray(inputs["L1_ffn_w_gate"], f32).T),
        "bn_wuT": np.ascontiguousarray(np.asarray(inputs["L1_ffn_w_up"], f32).T),
        "bn_wdT": np.ascontiguousarray(np.asarray(inputs["L1_ffn_w_down"], f32).T),
    }
    for nm, key in (("l1_b_in", "L1_attn_b_in"), ("l1_b_dec", "L1_attn_b_decay"),
                    ("l1_b_gate", "L1_attn_b_gate"), ("l1_b_out", "L1_attn_b_out"),
                    ("l1_dec_bias", "L1_attn_decay_bias")):
        shared[nm] = np.ascontiguousarray(
            np.asarray(inputs[key], f32).reshape(KT, 128).T)
    shared.update(_constants())
    return shared


def kernel(**inputs):
    x = np.asarray(inputs["L0_attn_hidden_states"], np.float32)
    B, T, _ = x.shape
    shared = stage_inputs(inputs)
    in_maps = []
    for core in range(8):
        b, r = core // GRP, core % GRP
        sel = np.zeros((128, GRP), np.float32)
        sel[:, r] = 1.0
        m = dict(shared)
        m["xT"] = np.ascontiguousarray(x[b, r * T_OWN:(r + 1) * T_OWN, :].T)
        m["sel"] = sel
        in_maps.append(m)

    nc = build_program()
    res = run_bass_kernel_spmd(nc, in_maps, core_ids=list(range(8)))
    out = np.zeros((B, T, D), np.float32)
    for core in range(8):
        b, r = core // GRP, core % GRP
        out[b, r * T_OWN:(r + 1) * T_OWN, :] = \
            np.asarray(res.results[core]["outT"]).T
    return out


# version 05 (final)
# speedup vs baseline: 1.0999x; 1.0999x over previous
import math
import os

import numpy as np

import concourse.bass as bass
import concourse.mybir as mybir
from concourse.bass_utils import run_bass_kernel_spmd

F32 = mybir.dt.float32
F32R = mybir.dt.float32r
BF16 = mybir.dt.bfloat16
OP = mybir.AluOpType
AF = mybir.ActivationFunctionType
AX = mybir.AxisListType

D = 2048
KT = D // 128
H = 16
CONV_DIM = 3 * D
NCT = CONV_DIM // 128
CK = 4
E = 8
FI = 8192
FIT = FI // 128
GRP = 4

T_OWN = int(os.environ.get("MK_DEV_T") or 1024)
CH = 128
NCH = T_OWN // CH
T_HALF = min(512, T_OWN)
NHALF = T_OWN // T_HALF
NM = max(1, T_HALF // 512)
MV = min(512, T_HALF)
DEV_DUMP = bool(int(os.environ.get("MK_DEV_DUMP", "0")))
DEV_RAWO = bool(int(os.environ.get("MK_DEV_RAWO", "0")))

LN_EPS = 1e-5
GOLDEN_CENTER = 1.0 / math.e
GOLDEN_LOWER = 0.5 - math.log(4.0 / 3.0)
GOLDEN_UPPER = 0.5
MAGIC = float(np.float32(1.5 * 2.0 ** 23))
RG = [[0, 1, 2, 3], [4, 5, 6, 7]]


class Tok:
    __slots__ = ("sem", "used", "val", "inc")

    def __init__(self, sem, inc):
        self.sem = sem
        self.used = False
        self.val = None
        self.inc = inc


class Plan:
    ENGINES = ("sp", "gp", "act", "dve", "pe")

    def __init__(self):
        self.ops = {e: [] for e in self.ENGINES}
        self.producers = {}

    def op(self, eng, emit, deps=(), sem=None, inc=1, force=False):
        tok = None
        if sem is not None:
            tok = Tok(sem, inc)
            tok.used = force
            self.producers.setdefault(sem, []).append(tok)
        deps = tuple(d for d in deps if d is not None)
        for d in deps:
            d.used = True
        self.ops[eng].append((emit, deps, tok))
        return tok

    def finalize(self):
        for sem, lst in self.producers.items():
            v = 0
            for t in lst:
                if t.used:
                    v += t.inc
                    t.val = v

    def replay(self, eng, e, sems):
        waited = {}
        for emit, deps, tok in self.ops[eng]:
            needed = []
            for d in deps:
                if waited.get(d.sem, -1) < d.val:
                    needed.append((d.sem, d.val))
                    waited[d.sem] = d.val
            for s, v in needed[:-1]:
                e.wait_ge(sems[s], v)
            ins = emit(e)
            if needed:
                s, v = needed[-1]
                try:
                    ins._wait_ge(sems[s], v)
                except Exception:
                    e_ = None
                    raise
            if tok is not None and tok.used:
                ins.then_inc(sems[tok.sem], tok.inc)


class Ctx:
    pass


def pe(c, emit, deps=()):
    return c.plan.op("pe", emit, deps, sem="pe")


def dve(c, emit, deps=()):
    return c.plan.op("dve", emit, deps, sem="dve")


def act(c, emit, deps=()):
    return c.plan.op("act", emit, deps, sem="act")


def gp(c, emit, deps=()):
    return c.plan.op("gp", emit, deps, sem="gp")


def ccop(c, emit, deps=()):
    return c.plan.op("gp", emit, deps, sem="cc" if c.sem("cc") else "cc",
                     force=True)


def _dma(c, eng, stream, out_ap, in_ap, deps):
    if not hasattr(c, "dma_prev"):
        c.dma_prev = {}
    deps = list(deps) + [c.dma_prev.get(stream)]
    tok = c.plan.op(eng, lambda e, o=out_ap, i=in_ap: e.dma_start(out=o, in_=i),
                    deps, sem=c.sem(stream), inc=16, force=True)
    c.dma_prev[stream] = tok
    return tok


def dma_in(c, stream, out_ap, in_ap, deps=()):
    return _dma(c, "sp", stream, out_ap, in_ap, deps)


def dma_out(c, stream, out_ap, in_ap, deps=()):
    return _dma(c, "gp", stream, out_ap, in_ap, deps)


def mm_group(c, emits, deps=()):
    tok = None
    for i, em in enumerate(emits):
        d = deps if i == 0 else ()
        if i == len(emits) - 1:
            tok = c.plan.op("pe", em, d, sem="pe")
        else:
            c.plan.op("pe", em, d)
    return tok


def fm(dram_ap, p=128):
    return dram_ap.rearrange("(n p) t -> p n t", p=p)


def build_program():
    nc = bass.Bass("TRN2", target_bir_lowering=False, debug=False)
    c = Ctx()
    c.nc = nc
    c.plan = Plan()
    c.tensors = {}
    scr_kind = "ExternalOutput" if DEV_DUMP else "Internal"

    def dram(name, shape, dtype=F32, kind="Internal"):
        return nc.dram_tensor(name, list(shape), dtype, kind=kind).ap()

    din = {}
    for name, shape in [
        ("xT", (D, T_OWN)), ("w_qkv", (D, CONV_DIM)),
        ("conv_w", (128, NCT * CK)), ("w_z", (D, D)), ("w_b", (D, H)),
        ("w_a", (D, H)), ("dt_bias", (H, 1)), ("a_log", (H, 1)),
        ("norm_w", (128, 1)), ("w_out0", (D, D)), ("ffn_wg", (E, D, D)),
        ("ffn_wu", (E, D, D)), ("ffn_wd", (E, D, D)), ("ffn_wr", (D, E)),
        ("temp", (1, 1)), ("l1_w_in", (D, D)), ("l1_b_in", (128, KT)),
        ("l1_w_dec", (D, D)), ("l1_b_dec", (128, KT)),
        ("l1_w_gate", (D, D)), ("l1_b_gate", (128, KT)),
        ("l1_w_out", (D, D)), ("l1_b_out", (128, KT)),
        ("l1_dec_bias", (128, KT)), ("bn_wgT", (D, FI)), ("bn_wuT", (D, FI)),
        ("bn_wdT", (FI, D)), ("eye", (128, 128)), ("ones", (128, 128)),
        ("oh16", (16, 16 * 128)), ("mask_ge", (128, 128)),
        ("maskT_incl", (128, 128)), ("maskT_strict", (128, 128)),
        ("sel", (128, GRP)), ("kcol", (128, 8)), ("had128", (128, 128)),
    ]:
        din[name] = dram(name, shape, kind="ExternalInput")
    c.din = din

    ds = {}
    for name, shape in [
        ("mixed_scr", (CONV_DIM, T_OWN)), ("conv_scr", (CONV_DIM, T_OWN)),
        ("z_scr", (D, T_OWN)), ("ogn_scr", (D, T_OWN)),
        ("sattn_scr", (D, T_OWN)), ("x1_scr", (D, T_OWN)),
        ("s1_scr", (D, T_OWN)), ("int_scr", (FI, T_OWN)),
        ("dec_scr", (D, T_OWN)), ("dd1_scr", (D, T_OWN)),
        ("gate_scr", (D, T_OWN)),
    ]:
        ds[name] = dram(name, shape, kind=scr_kind)
    for name, shape in [
        ("cc1_in", (CONV_DIM, 3)), ("cc1_out", (GRP, CONV_DIM, 3)),
        ("cc2_in", (H * 2 * 128, 128)), ("cc2_out", (GRP, H * 2 * 128, 128)),
        ("cc3_in", (D, 2)), ("cc3_out", (GRP, D, 2)),
    ]:
        ds[name] = dram(name, shape)
    ds["wdq_scr"] = dram("wdq_scr", (FI, D), BF16)
    c.ds = ds
    c.dout = dram("outT", (D, T_OWN), kind="ExternalOutput")

    import contextlib
    es = contextlib.ExitStack()
    c.phase_es = None

    def sbuf_p(name, shape, dtype=F32):
        t = es.enter_context(nc.sbuf_tensor("sb_" + name, list(shape), dtype,
                                            side="left"))
        c.tensors[name] = t
        return t

    def sbuf(name, shape, dtype=F32):
        t = (c.phase_es or es).enter_context(
            nc.sbuf_tensor("sb_" + name, list(shape), dtype, side="right"))
        c.tensors[name] = t
        return t

    c.sbuf_p = sbuf_p
    c.sbuf = sbuf
    c.sems = {}

    def sem(name):
        if name not in c.sems:
            c.sems[name] = es.enter_context(nc.semaphore(name))
        return name

    c.sem = sem
    for e in ("pe", "dve", "act", "gp"):
        sem(e)

    with es:
        c.ps = es.enter_context(nc.psum_tensor("ps", [128, 8, 512], F32))
        plan_all(c)
        c.plan.finalize()
        with nc.Block() as block:
            @block.sync
            def _(e):
                c.plan.replay("sp", e, c.sems)

            @block.gpsimd
            def _(e):
                c.plan.replay("gp", e, c.sems)

            @block.scalar
            def _(e):
                c.plan.replay("act", e, c.sems)

            @block.vector
            def _(e):
                c.plan.replay("dve", e, c.sems)

            @block.tensor
            def _(e):
                c.plan.replay("pe", e, c.sems)
    return nc


def barrier(c):
    deps = [lst[-1] for lst in c.plan.producers.values() if lst]
    return gp(c, lambda e: e.engine_nop(), deps=deps)


def plan_all(c):
    import contextlib
    plan_P0(c)
    for fn in (plan_PA, plan_delta, plan_wout, plan_ffn, plan_l1attn,
               plan_bitnet, plan_final):
        with contextlib.ExitStack() as pes:
            c.phase_es = pes
            c.bar = barrier(c)
            c.bank_gate2 = {i: c.bar for i in range(8)}
            c.drr = 0
            fn(c)
    c.phase_es = None


def plan_P0(c):
    sb = c.sbuf
    c.eye = sb("eye", (128, 128))
    c.ones = sb("ones", (128, 128))
    c.oh16 = sb("oh16", (16, 16 * 128))
    c.mask_ge = sb("mask_ge", (128, 128))
    c.maskT_incl = sb("maskT_incl", (128, 128))
    c.maskT_strict = sb("maskT_strict", (128, 128))
    c.sel = sb("sel", (128, GRP))
    c.convw = sb("convw", (128, NCT * CK))
    c.norm_w = sb("norm_w", (128, 1))
    c.dtb = sb("dtb", (H, 1))
    c.alog = sb("alog", (H, 1))
    c.negexpA = sb("negexpA", (H, 1))
    c.rtau = sb("rtau", (128, 1))

    tc = {}
    for name, dst in (("eye", c.eye), ("ones", c.ones), ("oh16", c.oh16),
                      ("mask_ge", c.mask_ge), ("maskT_incl", c.maskT_incl),
                      ("maskT_strict", c.maskT_strict), ("sel", c.sel),
                      ("conv_w", c.convw), ("norm_w", c.norm_w),
                      ("dt_bias", c.dtb), ("a_log", c.alog)):
        tc[name] = dma_in(c, "dconst", dst[:], c.din[name])
    tc["tau"] = dma_in(c, "dconst", c.rtau[:],
                       c.din["temp"].to_broadcast((128, 1)))
    c.kcol = sb("kcol", (128, 8))
    tc["kcol"] = dma_in(c, "dconst", c.kcol[:], c.din["kcol"])
    c.kc = lambda i: c.kcol[:, i:i + 1]
    t_ = sb("had128", (128, 128))
    tc["had128"] = dma_in(c, "dconst", t_[:], c.din["had128"])
    for nm, key in (("l1bi", "l1_b_in"), ("l1bd", "l1_b_dec"),
                    ("l1bg", "l1_b_gate"), ("l1bo", "l1_b_out"),
                    ("l1db", "l1_dec_bias")):
        t_ = sb(nm, (128, KT))
        tc[key] = dma_in(c, "dconst", t_[:], c.din[key])
    c.tc = tc
    t1 = act(c, lambda e: e.activation(c.negexpA[:], c.alog[:], AF.Exp),
             deps=[tc["a_log"]])
    c.tok_negexpA = dve(c, lambda e: e.tensor_scalar_mul(
        c.negexpA[:], c.negexpA[:], -1.0), deps=[t1])
    c.tok_rtau = dve(c, lambda e: e.reciprocal(c.rtau[:], c.rtau[:]),
                     deps=[tc["tau"]])


def plan_stats(c, xh, sqx, mbc, rstdn, vsc, tx, war_deps):
    toks = []
    for m in range(NM):
        msl = slice(m * 512, m * 512 + MV)
        t1 = mm_group(c, [
            (lambda e, k=k, msl=msl, f=(k == 0), l=(k == KT - 1):
             e.matmul(c.ps[:, 4, :MV], c.ones[:], xh[:, k, msl],
                      start=f, stop=l)) for k in range(KT)],
            deps=[tx, c.tc["ones"]] + war_deps)
        sq_toks = []
        for k in range(KT):
            war2 = (sq_toks[-2:-1] if len(sq_toks) >= 2 else
                    ([c.sqx_war] if getattr(c, "sqx_war", None) else []))
            tsq = act(c, lambda e, k=k, msl=msl: e.activation(
                sqx[:, k % 2, :], xh[:, k, msl], AF.Square),
                deps=[tx] + war_deps + war2)
            tmm = c.plan.op(
                "pe",
                (lambda e, k=k, f=(k == 0), l=(k == KT - 1):
                 e.matmul(c.ps[:, 5, :MV], c.ones[:], sqx[:, k % 2, :],
                          start=f, stop=l)),
                [tsq] + ([t1] if k == 0 else []), sem="pe")
            sq_toks.append(tmm)
        t2 = sq_toks[-1]
        c.sqx_war = t2
        mb, rs, vs = mbc[:, msl], rstdn[:, msl], vsc[:, msl]
        t3 = dve(c, lambda e, mb=mb: e.tensor_scalar_mul(
            mb, c.ps[:, 4, :MV], 1.0 / D), deps=[t1])
        t4 = dve(c, lambda e, rs=rs: e.tensor_scalar_mul(
            rs, c.ps[:, 5, :MV], 1.0 / D), deps=[t2])
        t5 = dve(c, lambda e, mb=mb, vs=vs: e.tensor_tensor(
            vs, mb, mb, OP.mult), deps=[t3])
        t6 = dve(c, lambda e, rs=rs, vs=vs: e.tensor_tensor(
            rs, rs, vs, OP.subtract), deps=[t4, t5])
        t7 = act(c, lambda e, rs=rs: e.activation(rs, rs, AF.Sqrt,
                                                  bias=c.kc(0)), deps=[t6, c.tc["kcol"]])
        t8 = dve(c, lambda e, rs=rs: e.reciprocal(rs, rs), deps=[t7])
        t9 = dve(c, lambda e, rs=rs: e.tensor_scalar_mul(rs, rs, -1.0),
                 deps=[t8])
        toks += [t3, t9]
    return toks


def weight_stream(c, name, nstrips, wsrc_fn, xh, tx, h, post_fn,
                  cw=128, rows=128):
    for s in range(nstrips):
        buf = s % 2
        war = c.wstage_gate.get(buf)
        tl = dma_in(c, "dw%d" % buf, c.w_stage[:, buf, :, :cw], wsrc_fn(s),
                    [war] if war else [])
        last_mm = None
        for m in range(NM):
            msl = slice(m * 512, m * 512 + MV)
            bank = c.bank_rr
            c.bank_rr = (c.bank_rr + 1) % 4
            bwar = c.bank_gate.get(bank)
            tmm = mm_group(c, [
                (lambda e, k=k, msl=msl, bank=bank, buf=buf, cw=cw,
                 f=(k == 0), l=(k == KT - 1):
                 e.matmul(c.ps[:cw, bank, :MV], c.w_stage[:, buf, k, :cw],
                          xh[:, k, msl], start=f, stop=l))
                for k in range(KT)],
                deps=[tl, tx] + ([bwar] if bwar else []))
            tpost = post_fn(s, m, c.ps[:cw, bank, :MV], tmm)
            c.bank_gate[bank] = tpost
            last_mm = tpost
        c.wstage_gate[buf] = last_mm


def wsplit(c, whi, wlo, wf, deps):
    t1 = act(c, lambda e: e.activation(whi, wf, AF.Copy), deps=deps)
    t2 = dve(c, lambda e: e.tensor_tensor(wlo, wf, whi, OP.subtract),
             deps=[t1] + list(deps))
    return t1, t2


def weight_stream3(c, name, nstrips, wsrc_fn, xhi, xlo, tx, post_fn, cw=128):
    def load(s):
        buf = s % 2
        war = c.wstage_gate.get(buf)
        tl = dma_in(c, "dw%d" % buf, c.w_stage[:, buf, :, :cw], wsrc_fn(s),
                    [war] if war else [])
        wg = c.whl_gate.get(buf)
        th, tsp = wsplit(c, c.whl[:, buf, 0, :, :cw],
                         c.whl[:, buf, 1, :, :cw],
                         c.w_stage[:, buf, :, :cw],
                         [tl] + ([wg] if wg else []))
        c.wstage_gate[buf] = tsp
        return buf, th, tsp

    pend = load(0)
    for s in range(nstrips):
        buf, th, tsp = pend
        if s + 1 < nstrips:
            pend = load(s + 1)
        last_mm = None
        for m in range(NM):
            msl = slice(m * 512, m * 512 + MV)
            bank = c.bank_rr
            c.bank_rr = (c.bank_rr + 1) % 4
            bwar = c.bank_gate.get(bank)
            emits = []
            for k in range(KT):
                emits.append(lambda e, k=k, msl=msl, bank=bank, buf=buf,
                             cw=cw, f=(k == 0):
                             e.matmul(c.ps[:cw, bank, :MV],
                                      c.whl[:, buf, 0, k, :cw],
                                      xhi[:, k, msl], start=f, stop=False))
                emits.append(lambda e, k=k, msl=msl, bank=bank, buf=buf,
                             cw=cw:
                             e.matmul(c.ps[:cw, bank, :MV],
                                      c.whl[:, buf, 0, k, :cw],
                                      xlo[:, k, msl], start=False,
                                      stop=False))
            for k in range(KT):
                emits.append(lambda e, k=k, msl=msl, bank=bank, buf=buf,
                             cw=cw, l=(k == KT - 1):
                             e.matmul(c.ps[:cw, bank, :MV],
                                      c.whl[:, buf, 1, k, :cw],
                                      xhi[:, k, msl], start=False, stop=l))
            tmm = mm_group(c, emits,
                           deps=[th, tsp] + list(tx) +
                           ([bwar] if bwar else []))
            tpost = post_fn(s, m, c.ps[:cw, bank, :MV], tmm)
            c.bank_gate[bank] = tpost
            last_mm = tpost
        c.whl_gate[buf] = last_mm


def xsplit_k(c, xhi_k, xlo_k, src_k, deps):
    t1 = act(c, lambda e: e.activation(xhi_k, src_k, AF.Copy), deps=deps)
    t2 = dve(c, lambda e: e.tensor_tensor(xlo_k, src_k, xhi_k, OP.subtract),
             deps=[t1] + list(deps))
    return t1, t2


def build_xn(c, dst, src, mbc, rstdn, deps, war):
    toks = []
    for k in range(KT):
        t1 = dve(c, lambda e, k=k: e.scalar_tensor_tensor(
            dst[:, k, :], mbc, 1.0, src[:, k, :], OP.mult, OP.subtract),
            deps=deps + ([war[k]] if war and war[k] else []))
        t2 = dve(c, lambda e, k=k: e.tensor_tensor(
            dst[:, k, :], dst[:, k, :], rstdn, OP.mult), deps=[t1])
        toks.append(t2)
    return toks


def plan_PA(c):
    sb = c.sbuf
    sbp = c.sbuf_p
    c.xh = sb("xh", (128, KT, T_HALF))
    c.xnscr = sb("xnscr", (128, T_HALF))
    c.xnhi = sb("xnhi", (128, KT, T_HALF), BF16)
    c.xnlo = sb("xnlo", (128, KT, T_HALF), BF16)
    c.sqx = sb("sqx", (128, 2, MV))
    c.vsc = sb("vsc", (128, T_HALF))
    c.mbc0 = sbp("mbc0", (128, NHALF, T_HALF))
    c.rstdn0 = sbp("rstdn0", (128, NHALF, T_HALF))
    c.w_stage = sb("w_stage", (128, 2, KT, 128))
    c.whl = sb("whl", (128, 2, 2, KT, 128), BF16)
    c.mx_stage = sb("mx_stage", (128, 4, MV))
    c.beta16 = sbp("beta16", (H, T_OWN))
    c.bneg16 = sbp("bneg16", (H, T_OWN))
    c.g16 = sbp("g16", (H, T_OWN))
    c.B16 = sbp("B16", (H, T_OWN))
    c.expB16 = sbp("expB16", (H, T_OWN))
    c.wt8 = sbp("wt8", (8, T_OWN))
    c.halo = sbp("halo", (128, NCT, 3))
    c.cc1stage = sb("cc1stage", (128, NCT, 3))
    c.rt = sb("rt", (128, 10, 512))

    c.wstage_gate = {}
    c.whl_gate = {}
    c.bank_gate = dict(c.bank_gate2)
    c.bank_rr = 0
    c.mx_gate = {}
    c.rt_pswar = None
    c.ba_gate = None
    tok_cc1 = []
    c.tok_mixed_w = []
    c.tok_z_w = []
    prev_half_done = None
    stats_war = []
    xn_war = None

    for h in range(NHALF):
        hsl = slice(h * T_HALF, (h + 1) * T_HALF)
        xsrc = fm(c.din["xT"])[:, :, hsl]
        tx = dma_in(c, "dx%d" % (h % 2), c.xh[:], xsrc,
                    [prev_half_done] if prev_half_done else [])
        mbc = c.mbc0[:, h, :]
        rstdn = c.rstdn0[:, h, :]
        tok_stats = plan_stats(c, c.xh, c.sqx, mbc, rstdn, c.vsc, tx,
                               stats_war)
        stats_war = list(tok_stats)
        scr_war = None
        thi = tlo = None
        for k in range(KT):
            d0 = [tx] + tok_stats + ([xn_war] if xn_war else []) + \
                ([scr_war] if scr_war else [])
            t1 = dve(c, lambda e, k=k, mbc=mbc: e.scalar_tensor_tensor(
                c.xnscr[:], mbc, 1.0, c.xh[:, k, :], OP.mult, OP.subtract),
                deps=d0)
            t2 = dve(c, lambda e, rstdn=rstdn: e.tensor_tensor(
                c.xnscr[:], c.xnscr[:], rstdn, OP.mult), deps=[t1])
            thi, tlo = xsplit_k(c, c.xnhi[:, k, :], c.xnlo[:, k, :],
                                c.xnscr[:], [t2])
            scr_war = tlo
        txn = [thi, tlo]

        def mk_post(dst_scr, cc_stage):
            def post(s, m, psum_ap, tmm):
                buf2 = c.mx_rr
                c.mx_rr = (c.mx_rr + 1) % 4
                out_ap = c.mx_stage[:, buf2, :MV]
                war = c.mx_gate.get(buf2)
                te = dve(c, lambda e, out_ap=out_ap, psum_ap=psum_ap:
                         e.tensor_copy(out_ap, psum_ap),
                         deps=[tmm] + ([war] if war else []))
                toks = [te]
                if cc_stage is not None and h == NHALF - 1 and m == NM - 1:
                    tok_cc1.append(dve(c, lambda e, s=s, out_ap=out_ap:
                                       e.tensor_copy(c.cc1stage[:, s, :],
                                                     out_ap[:, MV - 3:MV]),
                                       deps=[te]))
                    toks.append(tok_cc1[-1])
                dst = fm(c.ds[dst_scr])[
                    :, s, slice(h * T_HALF + m * 512,
                                h * T_HALF + m * 512 + MV)]
                tw = dma_out(c, "ws%d" % buf2, dst, out_ap, toks)
                c.mx_gate[buf2] = tw
                if dst_scr == "mixed_scr":
                    c.tok_mixed_w.append(tw)
                else:
                    c.tok_z_w.append(tw)
                return tw
            return post

        c.mx_rr = 0

        def qkv_src(s):
            return c.din["w_qkv"][:, s * 128:(s + 1) * 128].rearrange(
                "(k p) c -> p k c", p=128)

        weight_stream3(c, "qkv", NCT, qkv_src, c.xnhi, c.xnlo, txn,
                       mk_post("mixed_scr", True))

        def z_src(s):
            return c.din["w_z"][:, s * 128:(s + 1) * 128].rearrange(
                "(k p) c -> p k c", p=128)

        weight_stream3(c, "z", KT, z_src, c.xnhi, c.xnlo, txn,
                       mk_post("z_scr", None))

        def fin_b(s, m, psum_ap, tmm):
            osl = slice(h * T_HALF + m * 512, h * T_HALF + m * 512 + MV)
            t1 = act(c, lambda e, psum_ap=psum_ap, osl=osl: e.activation(
                c.beta16[:, osl], psum_ap, AF.Sigmoid), deps=[tmm])
            t2 = dve(c, lambda e, osl=osl: e.tensor_scalar_mul(
                c.bneg16[:, osl], c.beta16[:, osl], -1.0), deps=[t1])
            return t2

        def fin_a(s, m, psum_ap, tmm):
            osl = slice(h * T_HALF + m * 512, h * T_HALF + m * 512 + MV)
            t1 = act(c, lambda e, psum_ap=psum_ap, osl=osl: e.activation(
                c.g16[:, osl], psum_ap, AF.Exp, bias=c.dtb[:]),
                deps=[tmm, c.tc["dt_bias"]])
            t2 = act(c, lambda e, osl=osl: e.activation(
                c.g16[:, osl], c.g16[:, osl], AF.Ln, bias=c.kc(1)[:16]),
                deps=[t1, c.tc["kcol"]])
            t3 = dve(c, lambda e, osl=osl: e.tensor_scalar(
                c.g16[:, osl], c.g16[:, osl], c.negexpA[:], None, OP.mult),
                deps=[t2, c.tok_negexpA])
            tlast = t3
            base = h * T_HALF + m * 512
            for ci in range(MV // CH):
                csl = slice(base + ci * CH, base + (ci + 1) * CH)
                ts = dve(c, lambda e, csl=csl: e.tensor_tensor_scan(
                    c.B16[:, csl], c.ones[:16, :CH], c.g16[:, csl], 0.0,
                    OP.mult, OP.add), deps=[t3, c.tc["ones"]])
                te2 = act(c, lambda e, csl=csl: e.activation(
                    c.expB16[:, csl], c.B16[:, csl], AF.Exp), deps=[ts])
                tlast = te2
            return tlast

        def wb_src(s):
            return c.din["w_b"].rearrange("(k p) c -> p k c", p=128)

        def wa_src(s):
            return c.din["w_a"].rearrange("(k p) c -> p k c", p=128)

        weight_stream3(c, "w_b", 1, wb_src, c.xnhi, c.xnlo, txn, fin_b,
                       cw=16)
        weight_stream3(c, "w_a", 1, wa_src, c.xnhi, c.xnlo, txn, fin_a,
                       cw=16)

        def wr_src(s):
            return c.din["ffn_wr"].rearrange("(k p) c -> p k c", p=128)

        rt_toks = []

        def fin_r(s, m, psum_ap, tmm):
            t1 = act(c, lambda e, psum_ap=psum_ap: e.activation(
                c.rt[:8, 0, :MV], psum_ap, AF.Sigmoid, scale=c.rtau[:8]),
                deps=[tmm, c.tok_rtau] +
                ([c.rt_pswar] if c.rt_pswar else []))
            tprev = t1
            for ci in range(MV // 128):
                tprev = plan_router_chunk(
                    c, h * T_HALF + m * 512 + ci * 128,
                    c.rt[:8, 0, ci * 128:(ci + 1) * 128], tprev)
                rt_toks.append(tprev)
            return tprev

        weight_stream3(c, "wr", 1, wr_src, c.xnhi, c.xnlo, txn, fin_r,
                       cw=8)

        prev_half_done = rt_toks[-1]
        xn_war = rt_toks[-1]

    c.gath1 = sb("gath1", (128, GRP, NCT, 3))
    t1 = dma_out(c, "wcc1", fm(c.ds["cc1_in"]), c.cc1stage[:], tok_cc1)
    tcc = ccop(c, lambda e: e.collective_compute(
        "AllGather", OP.bypass, replica_groups=RG,
        ins=[c.ds["cc1_in"]], outs=[c.ds["cc1_out"]]), deps=[t1])
    t2 = dma_in(c, "dcc1", c.gath1[:],
                c.ds["cc1_out"].rearrange("g (n p) t -> p g n t", p=128),
                [tcc])
    th = dve(c, lambda e: e.tensor_scalar(
        c.halo[:], c.gath1[:, 0], c.sel[:, 1:2], None, OP.mult),
        deps=[t2, c.tc["sel"]])
    for j in (2, 3):
        th = dve(c, lambda e, j=j: e.scalar_tensor_tensor(
            c.halo[:], c.gath1[:, j - 1], c.sel[:, j:j + 1], c.halo[:],
            OP.mult, OP.add), deps=[th])
    c.tok_halo = th

    if DEV_DUMP:
        dbg = c.nc.dram_tensor("dbg_pa", [64, T_OWN], F32,
                               kind="ExternalOutput").ap()
        deps = [c.ba_gate, c.rt_pswar, th]
        dma_out(c, "wdbg", dbg[0:16, :], c.beta16[:], deps)
        dma_out(c, "wdbg", dbg[16:32, :], c.g16[:], deps)
        dma_out(c, "wdbg", dbg[32:48, :], c.B16[:], deps)
        dma_out(c, "wdbg", dbg[48:56, :], c.wt8[:], deps)


def plan_router_chunk(c, tok0, inh_ap, dep):
    ps7 = c.ps[:, 7, :]
    tt = pe(c, lambda e: e.transpose(ps7[:, :8], inh_ap, c.eye[:8, :8]),
            deps=[dep, c.tc["eye"], c.rt_pswar] if c.rt_pswar
            else [dep, c.tc["eye"]])
    A = c.rt[:, 1, :8]
    Dst = c.rt[:, 2, :8]
    Z1 = c.rt[:, 3, :8]
    W = c.rt[:, 4, :8]
    FB = c.rt[:, 5, :8]
    M1 = c.rt[:, 6, :8]
    M2 = c.rt[:, 7, :8]
    cols = c.rt[:, 8, :8]
    tA = dve(c, lambda e: e.tensor_copy(A, ps7[:128, :8]), deps=[tt])
    td = act(c, lambda e: e.activation(Dst, A, AF.Abs, bias=c.kc(3)),
             deps=[tA, c.tc["kcol"]])
    tz1 = dve(c, lambda e: e.tensor_single_scalar(Z1, A, GOLDEN_LOWER,
                                                  OP.is_ge), deps=[tA])
    tw = act(c, lambda e: e.activation(W, Dst, AF.Exp, scale=-10.0),
             deps=[td])
    tw = dve(c, lambda e: e.tensor_tensor(W, W, Z1, OP.mult),
             deps=[tw, tz1])
    tz2 = dve(c, lambda e: e.tensor_single_scalar(Z1, A, GOLDEN_UPPER,
                                                  OP.is_le), deps=[tA, tw])
    tw = dve(c, lambda e: e.tensor_tensor(W, W, Z1, OP.mult),
             deps=[tw, tz2])
    ws = cols[:, 0:1]
    no = cols[:, 1:2]
    tws = dve(c, lambda e: e.tensor_reduce(ws, W, AX.X, OP.add), deps=[tw])
    tno = dve(c, lambda e: e.tensor_single_scalar(no, ws, 1e-8, OP.is_lt),
              deps=[tws])
    tfb = act(c, lambda e: e.activation(FB, Dst, AF.Exp, scale=-10.0 / 3.0),
              deps=[td])
    mx = cols[:, 2:3]
    tm1 = dve(c, lambda e: e.tensor_reduce(mx, FB, AX.X, OP.max), deps=[tfb])
    tM1 = dve(c, lambda e: e.tensor_scalar(M1, FB, mx, None, OP.is_ge),
              deps=[tm1, tfb])
    tfb2 = dve(c, lambda e: e.scalar_tensor_tensor(
        M2, M1, -1e30, FB, OP.mult, OP.add), deps=[tM1, tfb])
    tm2 = dve(c, lambda e: e.tensor_reduce(mx, M2, AX.X, OP.max),
              deps=[tfb2, tM1])
    tM2 = dve(c, lambda e: e.tensor_scalar(M2, M2, mx, None, OP.is_ge),
              deps=[tm2])
    tmk = dve(c, lambda e: e.tensor_tensor(M1, M1, M2, OP.add),
              deps=[tM1, tM2])
    tfw = dve(c, lambda e: e.tensor_tensor(FB, FB, M1, OP.mult),
              deps=[tmk, tfb])
    fs = cols[:, 3:4]
    tfs = dve(c, lambda e: e.tensor_reduce(fs, FB, AX.X, OP.add), deps=[tfw])
    tfs = dve(c, lambda e: e.tensor_single_scalar(fs, fs, 1e-8, OP.max),
              deps=[tfs])
    tfs = dve(c, lambda e: e.reciprocal(fs, fs), deps=[tfs])
    tfw = dve(c, lambda e: e.tensor_scalar(FB, FB, fs, None, OP.mult),
              deps=[tfs, tfw])
    tdel = dve(c, lambda e: e.tensor_tensor(FB, FB, W, OP.subtract),
               deps=[tfw, tw])
    tsel = dve(c, lambda e: e.scalar_tensor_tensor(
        W, FB, no, W, OP.mult, OP.add), deps=[tdel, tno, tw])
    tws2 = dve(c, lambda e: e.tensor_reduce(ws, W, AX.X, OP.add),
               deps=[tsel])
    tws2 = dve(c, lambda e: e.tensor_single_scalar(ws, ws, 1e-8, OP.max),
               deps=[tws2])
    tws2 = dve(c, lambda e: e.reciprocal(ws, ws), deps=[tws2])
    tfin = dve(c, lambda e: e.tensor_scalar(W, W, ws, None, OP.mult),
               deps=[tws2, tsel])
    ttb = pe(c, lambda e: e.transpose(ps7[:8, :128], W, c.eye[:]),
             deps=[tfin, tA])
    tout = dve(c, lambda e, tok0=tok0: e.tensor_copy(
        c.wt8[:, tok0:tok0 + 128], ps7[:8, :128]), deps=[ttb])
    c.rt_pswar = tout
    return tout


def mmp(c, emit_fns, deps, width=128):
    b = c.drr
    c.drr = (c.drr + 1) % 8
    ap = c.ps[:, b, :width]
    war = c.bank_gate2.get(b)
    tok = mm_group(c, [(lambda e, f=f, ap=ap: f(e, ap)) for f in emit_fns],
                   list(deps) + [war])
    return ap, tok, b


def trp(c, in_ap, ident, deps, width=128, rows=128):
    b = c.drr
    c.drr = (c.drr + 1) % 8
    ap = c.ps[:rows, b, :width]
    war = c.bank_gate2.get(b)
    tok = pe(c, lambda e, ap=ap, i=in_ap, idn=ident: e.transpose(ap, i, idn),
             list(deps) + [war, c.tc["eye"]])
    return ap, tok, b


def delta_ingest(c, i, pass2):
    par = i % 2
    gate = c.chunk_gate[par]
    chsl = slice(i * CH, (i + 1) * CH)
    if pass2:
        tin = dma_in(c, "dc2%d" % par, c.cvo[:, 0],
                     fm(c.ds["conv_scr"])[:, :, chsl],
                     c.chunk_gate[0] + c.chunk_gate[1] + [c.cvw_tok[i]])
        toks_cv = [tin]
    else:
        tmain = dma_in(c, "dch", c.mxch[:, :, 3:],
                       fm(c.ds["mixed_scr"])[:, :, chsl],
                       c.chunk_gate[0] + c.chunk_gate[1] + c.mx_war)
        if i == 0:
            tpre = dve(c, lambda e: e.tensor_copy(
                c.mxch[:, :, 0:3], c.halo[:]),
                deps=[c.tok_halo, tmain] + gate)
        else:
            tpre = dma_in(c, "dch", c.mxch[:, :, 0:3],
                          fm(c.ds["mixed_scr"])[:, :, i * CH - 3:i * CH], gate)
        conv_toks = []
        for ct in range(NCT):
            t0 = dve(c, lambda e, ct=ct: e.tensor_scalar(
                c.cvo[:, 0, ct, :], c.mxch[:, ct, 0:CH],
                c.convw[:, ct * CK:ct * CK + 1], None, OP.mult),
                deps=[tmain, tpre, c.tc["conv_w"]] +
                c.chunk_gate[0] + c.chunk_gate[1])
            for j in (1, 2, 3):
                t0 = dve(c, lambda e, par=par, ct=ct, j=j:
                         e.scalar_tensor_tensor(
                             c.cvo[:, 0, ct, :], c.mxch[:, ct, j:j + CH],
                             c.convw[:, ct * CK + j:ct * CK + j + 1],
                             c.cvo[:, 0, ct, :], OP.mult, OP.add),
                         deps=[t0])
            conv_toks.append(t0)
        c.mx_war = conv_toks
        tsilu = act(c, lambda e, par=par: e.activation(
            c.cvo[:, 0], c.cvo[:, 0], AF.Silu), deps=conv_toks)
        toks_cv = []
        for ct2 in range(32):
            p2 = ct2 % 2
            tsq = act(c, lambda e, par=par, ct2=ct2, p2=p2: e.activation(
                c.sqk[:, p2, :], c.cvo[:, 0, ct2, :], AF.Square),
                deps=[tsilu] + c.sq_war.get(p2, []))
            ap, tmm, b = mmp(c, [
                (lambda e, ap, p2=p2: e.matmul(
                    ap, c.ones[:], c.sqk[:, p2, :], start=True, stop=True))],
                [tsq, c.tc["ones"]])
            c.sq_war[p2] = [tmm]
            tsr = act(c, lambda e, p2=p2, ap=ap: e.activation(
                c.rnq[:, p2, :], ap, AF.Sqrt, bias=c.kc(2)),
                deps=[tmm, c.tc["kcol"]] + c.rn_war.get(p2, []))
            c.bank_gate2[b] = tsr
            trc = dve(c, lambda e, p2=p2: e.reciprocal(
                c.rnq[:, p2, :], c.rnq[:, p2, :]), deps=[tsr])
            if ct2 < 16:
                tsc = dve(c, lambda e, par=par, ct2=ct2, p2=p2:
                          e.scalar_tensor_tensor(
                              c.cvo[:, 0, ct2, :], c.cvo[:, 0, ct2, :],
                              128.0 ** -0.5, c.rnq[:, p2, :],
                              OP.mult, OP.mult), deps=[trc, tsilu])
            else:
                tsc = dve(c, lambda e, par=par, ct2=ct2, p2=p2: e.tensor_tensor(
                    c.cvo[:, 0, ct2, :], c.cvo[:, 0, ct2, :],
                    c.rnq[:, p2, :], OP.mult), deps=[trc, tsilu])
            c.rn_war[p2] = [tsc]
            toks_cv.append(tsc)
        tw = dma_out(c, "wcv%d" % par, fm(c.ds["conv_scr"])[:, :, chsl],
                     c.cvo[:, 0], toks_cv + [tsilu])
        c.cvw_tok[i] = tw
        toks_cv.append(tw)
    col_toks = []
    for src, dst in ((c.B16, c.Btr), (c.beta16, c.btr),
                     (c.bneg16, c.bntr), (c.expB16, c.eBtr)):
        ap, tt, b = trp(c, src[:, chsl], c.eye[:16, :16],
                        list(gate), width=16)
        tev = dve(c, lambda e, dst=dst, ap=ap, par=par: e.tensor_copy(
            dst[:, par], ap), deps=[tt])
        c.bank_gate2[b] = tev
        col_toks.append(tev)
    return toks_cv, col_toks


def delta_head_common(c, i, h, par, ph, dep_cv, dep_cols):
    r = {}
    gate = c.head_gate[ph]
    chsl = slice(i * CH, (i + 1) * CH)
    kn = c.cvo[:, 0, 16 + h, :]
    bap, tb, bb = mmp(c, [
        (lambda e, ap, h=h, chsl=chsl: e.matmul(
            ap, c.oh16[:, h * 128:(h + 1) * 128], c.B16[:, chsl],
            start=True, stop=True))], dep_cols + gate + [c.tc["oh16"]])
    tmp = c.tmp[:, ph]
    t_tmp = dve(c, lambda e, tmp=tmp, bap=bap, h=h, par=par: e.tensor_scalar(
        tmp, bap, c.Btr[:, par, h:h + 1], None, OP.subtract),
        deps=[tb] + dep_cols)
    t_gbc = act(c, lambda e, bap=bap, ph=ph: e.activation(
        c.gbc[:, ph], bap, AF.Exp), deps=[tb, t_tmp])
    c.bank_gate2[bb] = t_gbc
    nap, tn, bn = mmp(c, [
        (lambda e, ap, h=h, chsl=chsl: e.matmul(
            ap, c.oh16[:, h * 128:(h + 1) * 128], c.bneg16[:, chsl],
            start=True, stop=True))], dep_cols + gate + [c.tc["oh16"]])
    kap, tkk, bk = mmp(c, [
        (lambda e, ap, kn=kn: e.matmul(ap, kn, kn, start=True, stop=True))],
        dep_cv + gate)
    t1 = c.t1[:, ph]
    ta = dve(c, lambda e, t1=t1, tmp=tmp: e.tensor_tensor(
        t1, tmp, c.mask_ge[:], OP.max), deps=[t_tmp, c.tc["mask_ge"]])
    tb2 = act(c, lambda e, t1=t1: e.activation(t1, t1, AF.Exp, scale=-1.0),
              deps=[ta])
    N = c.Nsb[:, ph]
    tc1 = dve(c, lambda e, N=N, t1=t1, kap=kap: e.tensor_tensor(
        N, t1, kap, OP.mult), deps=[tb2, tkk])
    tc2 = dve(c, lambda e, N=N, h=h, par=par: e.tensor_scalar(
        N, N, c.bntr[:, par, h:h + 1], None, OP.mult), deps=[tc1] + dep_cols)
    t1b = c.t1b[:, ph]
    td = dve(c, lambda e, t1b=t1b, tmp=tmp: e.tensor_tensor(
        t1b, tmp, c.maskT_strict[:], OP.min),
        deps=[t_tmp, c.tc["maskT_strict"]])
    te = act(c, lambda e, t1b=t1b: e.activation(t1b, t1b, AF.Exp), deps=[td])
    Z = c.Zsb[:, ph]
    tf = dve(c, lambda e, Z=Z, t1b=t1b, kap=kap: e.tensor_tensor(
        Z, t1b, kap, OP.mult), deps=[te, tkk, tc1])
    tg = dve(c, lambda e, Z=Z, nap=nap: e.tensor_tensor(
        Z, Z, nap, OP.mult), deps=[tf, tn])
    c.bank_gate2[bk] = tf
    c.bank_gate2[bn] = tg
    PZ = c.PZ[:, ph]
    tpz = dve(c, lambda e, PZ=PZ, Z=Z: e.tensor_tensor(
        PZ, Z, c.eye[:], OP.add), deps=[tg, c.tc["eye"]])
    Ms = [c.Ma[:, ph], c.Mb[:, ph]]
    Zs = [c.Za[:, ph], c.Zb[:, ph]]
    tM, tZ = tc2, tg
    Mcur, Zcur = N, Z
    for j in range(1, 7):
        pj = j % 2
        map_, tmj, bm = mmp(c, [
            (lambda e, ap, Z=Zcur, M=Mcur: e.matmul(ap, Z, M,
                                                    start=True, stop=True))],
            [tM, tZ])
        tMn = dve(c, lambda e, d=Ms[pj], ap=map_: e.tensor_copy(d, ap),
                  deps=[tmj])
        c.bank_gate2[bm] = tMn
        if j < 6:
            zap, tzj, bz = mmp(c, [
                (lambda e, ap, Z=Zcur, M=Mcur: e.matmul(
                    ap, M, Z, start=True, stop=True))], [tM, tZ])
            tZn = dve(c, lambda e, d=Zs[pj], ap=zap: e.tensor_copy(d, ap),
                      deps=[tzj])
            c.bank_gate2[bz] = tZn
        else:
            tZn = tZ
        pap, tpj, bp = mmp(c, [
            (lambda e, ap, M=Ms[pj], PZ=PZ: e.matmul(
                ap, M, PZ, start=True, stop=True))], [tMn, tpz])
        tpz = dve(c, lambda e, PZ=PZ, ap=pap: e.tensor_tensor(
            PZ, PZ, ap, OP.add), deps=[tpj, tpz])
        c.bank_gate2[bp] = tpz
        Mcur, Zcur = Ms[pj], Zs[pj]
        tM, tZ = tMn, tZn
    r["PZ"], r["tPZ"] = PZ, tpz
    ghat = c.cols[:, ph, 0:1]
    gC = c.cols[:, ph, 1:2]
    bg = c.cols[:, ph, 2:3]
    tgh = act(c, lambda e, ghat=ghat, tmp=tmp: e.activation(
        ghat, tmp[:, 127:128], AF.Exp), deps=[t_tmp])
    tgc = dve(c, lambda e, gC=gC, ghat=ghat, h=h, par=par: e.tensor_scalar(
        gC, ghat, c.eBtr[:, par, h:h + 1], None, OP.mult),
        deps=[tgh] + dep_cols)
    tbg = dve(c, lambda e, bg=bg, h=h, par=par: e.tensor_tensor(
        bg, c.btr[:, par, h:h + 1], c.eBtr[:, par, h:h + 1], OP.mult),
        deps=dep_cols + gate)
    r.update(ghat=ghat, gC=gC, bg=bg, tgh=tgh, tgc=tgc, tbg=tbg)
    kap2, tkt, bkt = trp(c, kn, c.eye[:], dep_cv + gate)
    kT = c.kT[:, ph]
    tktev = dve(c, lambda e, kT=kT, ap=kap2: e.tensor_copy(kT, ap),
                deps=[tkt])
    c.bank_gate2[bkt] = tktev
    r.update(kT=kT, tkT=tktev, kn=kn, tmp=tmp, t_tmp=t_tmp,
             gbc=c.gbc[:, ph], t_gbc=t_gbc)
    Khat = c.Khat[:, ph]
    tkh = dve(c, lambda e, Khat=Khat, kT=kT, ghat=ghat: e.tensor_scalar(
        Khat, kT, ghat, None, OP.mult), deps=[tktev, tgh])
    r.update(Khat=Khat, tKhat=tkh)
    return r


def delta_state_update(c, i, h, par, ph, r, S_in_ap, S_in_tok, S_out_ap):
    kn = r["kn"]
    v = c.cvo[:, 0, 32 + h, :]
    kb = c.kb[:, ph]
    tkb = dve(c, lambda e, kb=kb, kn=kn: e.tensor_tensor(
        kb, kn, r["gbc"], OP.mult), deps=[r["t_gbc"], c.head_gate[ph][0]])
    ksap, tks, bks = mmp(c, [
        (lambda e, ap, kb=kb, S=S_in_ap: e.matmul(ap, kb, S,
                                                  start=True, stop=True))],
        [tkb, S_in_tok])
    vap, tvt, bvt = trp(c, v, c.eye[:], [c.head_gate[ph][0]])
    vT = c.vT[:, ph]
    tvte = dve(c, lambda e, vT=vT, ap=vap: e.tensor_copy(vT, ap), deps=[tvt])
    c.bank_gate2[bvt] = tvte
    RHS = c.RHS[:, ph]
    t1 = dve(c, lambda e, RHS=RHS, vT=vT, ap=ksap: e.tensor_tensor(
        RHS, vT, ap, OP.subtract), deps=[tvte, tks])
    c.bank_gate2[bks] = t1
    t2 = dve(c, lambda e, RHS=RHS, h=h, par=par: e.tensor_scalar(
        RHS, RHS, c.btr[:, par, h:h + 1], None, OP.mult), deps=[t1])
    dap, tdl, bdl = mmp(c, [
        (lambda e, ap, PZ=r["PZ"], RHS=RHS: e.matmul(
            ap, PZ, RHS, start=True, stop=True))], [r["tPZ"], t2])
    dlt = c.dlt[:, ph]
    tdle = dve(c, lambda e, dlt=dlt, ap=dap: e.tensor_copy(dlt, ap),
               deps=[tdl])
    c.bank_gate2[bdl] = tdle
    kdap, tkd, bkd = mmp(c, [
        (lambda e, ap, Khat=r["Khat"], dlt=dlt: e.matmul(
            ap, Khat, dlt, start=True, stop=True))], [r["tKhat"], tdle])
    tso = dve(c, lambda e, So=S_out_ap, Si=S_in_ap, gC=r["gC"], ap=kdap:
              e.scalar_tensor_tensor(So, Si, gC, ap, OP.mult, OP.add),
              deps=[tkd, r["tgc"], S_in_tok])
    c.bank_gate2[bkd] = tso
    return tso, dlt, tdle


def plan_delta(c):
    sb = c.sbuf
    c.mxch = sb("mxch", (128, NCT, CH + 3))
    c.cvo = sb("cvo", (128, 1, NCT, CH))
    c.sqk = sb("sqk", (128, 2, CH))
    c.rnq = sb("rnq", (128, 2, CH))
    c.S_a = sb("S_a", (128, H, 128))
    c.S_b = sb("S_b", (128, H, 128))
    c.Xt = sb("Xt", (128, H, 128))
    c.XTt = sb("XTt", (128, H, 128))
    c.Btr = sb("Btr", (128, 2, 16))
    c.btr = sb("btr", (128, 2, 16))
    c.bntr = sb("bntr", (128, 2, 16))
    c.eBtr = sb("eBtr", (128, 2, 16))
    for nm in ("tmp", "t1", "t1b", "Nsb", "Zsb", "PZ", "Ma", "Mb", "Za",
               "Zb", "kb", "kT", "vT", "Khat", "RHS", "dlt", "Ysb", "MT"):
        setattr(c, nm, sb(nm, (128, 2, 128)))
    c.cols = sb("dcols", (128, 2, 8))
    c.gbc = sb("gbc", (128, 2, 128))

    c.chunk_gate = {0: [c.bar], 1: [c.bar]}
    c.head_gate = {0: [c.bar], 1: [c.bar]}
    c.cvw_tok = {}
    c.mx_war = []
    c.sq_war = {}
    c.rn_war = {}

    tz = [gp(c, lambda e: e.memset(c.S_a[:], 0.0), deps=[c.bar])]
    tS = {h: tz[0] for h in range(H)}
    S_cur, S_nxt = c.S_a, c.S_b

    tX = {}
    for i in range(NCH):
        par = i % 2
        dep_cv, dep_cols = delta_ingest(c, i, pass2=False)
        chunk_term = []
        for h in range(H):
            ph = h % 2
            r = delta_head_common(c, i, h, par, ph, dep_cv, dep_cols)
            tso, dlt, tdle = delta_state_update(
                c, i, h, par, ph, r, S_cur[:, h, :], tS[h], S_nxt[:, h, :])
            tS[h] = tso
            kbb = c.kb[:, ph]
            tkbb = dve(c, lambda e, kbb=kbb, kT=r["kT"], bg=r["bg"]:
                       e.tensor_scalar(kbb, kT, bg, None, OP.mult),
                       deps=[r["tkT"], r["tbg"], tso])
            yap, tY, by = mmp(c, [
                (lambda e, ap, PZ=r["PZ"], kbb=kbb: e.matmul(
                    ap, PZ, kbb, start=True, stop=True))],
                [r["tPZ"], tkbb])
            Y = c.Ysb[:, ph]
            tYe = dve(c, lambda e, Y=Y, ap=yap: e.tensor_copy(Y, ap),
                      deps=[tY])
            c.bank_gate2[by] = tYe
            ykap, tYK, byk = mmp(c, [
                (lambda e, ap, Y=Y, Khat=r["Khat"]: e.matmul(
                    ap, Y, Khat, start=True, stop=True))],
                [tYe, r["tKhat"]])
            MT = c.MT[:, ph]
            tMT = dve(c, lambda e, MT=MT, gC=r["gC"], ap=ykap:
                      e.scalar_tensor_tensor(MT, c.eye[:], gC, ap,
                                             OP.mult, OP.subtract),
                      deps=[tYK, r["tgc"], c.tc["eye"]])
            c.bank_gate2[byk] = tMT
            if i == 0:
                tx1 = dve(c, lambda e, h=h, MT=MT: e.tensor_copy(
                    c.Xt[:, h, :], MT), deps=[tMT])
                tx2 = dve(c, lambda e, h=h, MT=MT: e.tensor_copy(
                    c.XTt[:, h, :], MT), deps=[tMT])
                tX[h] = [tx1, tx2]
            else:
                xap, txa, bx = mmp(c, [
                    (lambda e, ap, h=h, MT=MT: e.matmul(
                        ap, c.XTt[:, h, :], MT, start=True, stop=True))],
                    tX[h] + [tMT])
                x2ap, txb, bx2 = mmp(c, [
                    (lambda e, ap, h=h, MT=MT: e.matmul(
                        ap, MT, c.XTt[:, h, :], start=True, stop=True))],
                    tX[h] + [tMT])
                tx1 = dve(c, lambda e, h=h, ap=xap: e.tensor_copy(
                    c.Xt[:, h, :], ap), deps=[txa, txb])
                tx2 = dve(c, lambda e, h=h, ap=x2ap: e.tensor_copy(
                    c.XTt[:, h, :], ap), deps=[txb, tx1])
                c.bank_gate2[bx] = tx1
                c.bank_gate2[bx2] = tx2
                tX[h] = [tx1, tx2]
            c.head_gate[ph] = [tMT, tx2, tso]
            chunk_term += [tso, tx2]
        c.chunk_gate[par] = chunk_term
        S_cur, S_nxt = S_nxt, S_cur
    cc2v = c.ds["cc2_in"].rearrange("(h s p) t -> p h s t", p=128, s=2)
    tw1 = dma_out(c, "wcc2", cc2v[:, :, 0, :], c.Xt[:],
                  [t for h in range(H) for t in tX[h]])
    tw2 = dma_out(c, "wcc2", cc2v[:, :, 1, :], S_cur[:],
                  [tS[h] for h in range(H)] + [tw1])
    tcc = ccop(c, lambda e: e.collective_compute(
        "AllGather", OP.bypass, replica_groups=RG,
        ins=[c.ds["cc2_in"]], outs=[c.ds["cc2_out"]]), deps=[tw2])
    gout = c.ds["cc2_out"].rearrange("g (h s p) t -> p g h s t", p=128, s=2)
    Sin = S_nxt
    gbuf = c.gx = sb("gx", (128, GRP, 2, 128))
    pre = c.pre = sb("pre", (128, 2, 128))
    tSin = {}
    for h in range(H):
        war_g = [tSin[h - 1]] if h > 0 else []
        tga = dma_in(c, "dcc2", gbuf[:, :, 0, :],
                     gout[:, :, h, 0, :], [tcc] + war_g)
        tg = dma_in(c, "dcc2", gbuf[:, :, 1, :],
                    gout[:, :, h, 1, :], [tga])
        tp = dve(c, lambda e: e.tensor_copy(pre[:, 0, :], gbuf[:, 0, 1, :]),
                 deps=[tg])
        tsel = dve(c, lambda e, h=h: e.tensor_scalar(
            Sin[:, h, :], pre[:, 0, :], c.sel[:, 1:2], None, OP.mult),
            deps=[tp, c.tc["sel"], tS[h]])
        for rr in (2, 3):
            pap, tpp, bpp = mmp(c, [
                (lambda e, ap, rr=rr: e.matmul(
                    ap, gbuf[:, rr - 1, 0, :], pre[:, (rr - 2) % 2, :],
                    start=True, stop=True))], [tp, tg])
            tp = dve(c, lambda e, rr=rr, ap=pap: e.tensor_tensor(
                pre[:, (rr - 1) % 2, :], ap, gbuf[:, rr - 1, 1, :], OP.add),
                deps=[tpp, tg])
            c.bank_gate2[bpp] = tp
            tsel = dve(c, lambda e, h=h, rr=rr: e.scalar_tensor_tensor(
                Sin[:, h, :], pre[:, (rr - 1) % 2, :], c.sel[:, rr:rr + 1],
                Sin[:, h, :], OP.mult, OP.add), deps=[tp, tsel])
        tSin[h] = tsel
    c.zch = sb("zch", (128, 1, KT, CH))
    c.ogn = sb("ogn", (128, 1, KT, CH))
    c.oT = sb("oT", (128, 2, 128))
    c.rms = sb("rms", (128, 2, 128))
    S_cur, S_nxt = Sin, S_cur
    tS = dict(tSin)
    for i in range(NCH):
        par = i % 2
        chsl = slice(i * CH, (i + 1) * CH)
        dep_cv, dep_cols = delta_ingest(c, i, pass2=True)
        tzc = dma_in(c, "dz2", c.zch[:, 0],
                     fm(c.ds["z_scr"])[:, :, chsl],
                     c.chunk_gate[0] + c.chunk_gate[1])
        tzs = act(c, lambda e: e.activation(
            c.zch[:, 0], c.zch[:, 0], AF.Silu), deps=[tzc])
        chunk_term = []
        ogn_toks = []
        for h in range(H):
            ph = h % 2
            r = delta_head_common(c, i, h, par, ph, dep_cv, dep_cols)
            tso, dlt, tdle = delta_state_update(
                c, i, h, par, ph, r, S_cur[:, h, :], tS[h], S_nxt[:, h, :])
            tS[h] = tso
            qn = c.cvo[:, 0, h, :]
            kqap, tkq, bkq = mmp(c, [
                (lambda e, ap, kn=r["kn"], qn=qn: e.matmul(
                    ap, kn, qn, start=True, stop=True))],
                dep_cv + [c.head_gate[ph][0]])
            t1b = c.t1b[:, ph]
            tgt = dve(c, lambda e, t1b=t1b, tmp=r["tmp"]: e.tensor_tensor(
                t1b, tmp, c.maskT_incl[:], OP.min),
                deps=[r["t_tmp"], c.tc["maskT_incl"], tso])
            tge = act(c, lambda e, t1b=t1b: e.activation(t1b, t1b, AF.Exp),
                      deps=[tgt])
            PT = c.MT[:, ph]
            tpt = dve(c, lambda e, PT=PT, t1b=t1b, ap=kqap: e.tensor_tensor(
                PT, t1b, ap, OP.mult), deps=[tge, tkq])
            c.bank_gate2[bkq] = tpt
            qb = c.kb[:, ph]
            tqb = dve(c, lambda e, qb=qb, qn=qn, gb=r["gbc"]: e.tensor_tensor(
                qb, qn, gb, OP.mult), deps=[r["t_gbc"], tpt])
            oap, to1, bo = mmp(c, [
                (lambda e, ap, S=S_cur[:, h, :], qb=qb: e.matmul(
                    ap, S, qb, start=True, stop=False)),
                (lambda e, ap, dlt=dlt, PT=PT: e.matmul(
                    ap, dlt, PT, start=False, stop=True))],
                [tso, tqb, tdle, tpt])
            if DEV_DUMP and i == 1 and h in (0, 2):
                if not hasattr(c, "dbg_war"):
                    c.dbg_war = []
                if not hasattr(c, "dbg_d"):
                    c.dbg_d = c.nc.dram_tensor(
                        "dbg_d", [2, 4, 128, 128], F32,
                        kind="ExternalOutput").ap()
                    c.dbg_d2 = c.nc.dram_tensor(
                        "dbg_d2", [2, 128, 128], F32,
                        kind="ExternalOutput").ap()
                hh = 0 if h == 0 else 1
                d1 = dma_out(c, "wdbgd", c.dbg_d[hh, 0], dlt, [tdle])
                d2 = dma_out(c, "wdbgd", c.dbg_d[hh, 1], PT, [tpt, d1])
                d3 = dma_out(c, "wdbgd", c.dbg_d[hh, 2], qb, [tqb, d2])
                d3b = dma_out(c, "wdbgd", c.dbg_d2[hh], r["gbc"],
                              [r["t_gbc"], d3, tqb])
                t1ap, tt1, bt1 = mmp(c, [
                    (lambda e, ap, S=S_cur[:, h, :], qb=qb: e.matmul(
                        ap, S, qb, start=True, stop=True))],
                    [tso, tqb])
                if not hasattr(c, "dbgoT"):
                    c.dbgoT = c.sbuf("dbgoT", (128, 128))
                tt1e = dve(c, lambda e, ap=t1ap: e.tensor_copy(
                    c.dbgoT[:], ap), deps=[tt1] + c.dbg_war)
                c.bank_gate2[bt1] = tt1e
                d4 = dma_out(c, "wdbgd", c.dbg_d[hh, 3], c.dbgoT[:],
                             [tt1e, d3])
                c.dbg_war = [d4]
                c.dbg_extra = [d3, d3b, d4]
            oT = c.oT[:, ph]
            if DEV_RAWO:
                tg1 = dve(c, lambda e, oT=oT, ap=oap: e.tensor_copy(oT, ap),
                          deps=[to1, tzs])
            else:
                tg1 = dve(c, lambda e, oT=oT, ap=oap, h=h:
                          e.tensor_tensor(oT, ap, c.zch[:, 0, h, :], OP.mult),
                          deps=[to1, tzs])
            c.bank_gate2[bo] = tg1
            tg2 = act(c, lambda e, oT=oT, ph=ph: e.activation(
                c.rms[:, ph], oT, AF.Square), deps=[tg1])
            rmap, tr1, br = mmp(c, [
                (lambda e, ap, ph=ph: e.matmul(
                    ap, c.ones[:], c.rms[:, ph], start=True, stop=True))],
                [tg2, c.tc["ones"]])
            trs = act(c, lambda e, ph=ph, ap=rmap: e.activation(
                c.rms[:, ph], ap, AF.Sqrt, bias=c.kc(2), scale=1.0 / 128.0),
                deps=[tr1])
            c.bank_gate2[br] = trs
            trr = dve(c, lambda e, ph=ph: e.reciprocal(
                c.rms[:, ph], c.rms[:, ph]), deps=[trs])
            if DEV_RAWO:
                tg3 = dve(c, lambda e, oT=oT, ph=ph: e.tensor_scalar_mul(
                    c.rms[:, ph], c.rms[:, ph], 1.0), deps=[trr, tg1])
                tg3 = tg1
            else:
                tg3 = dve(c, lambda e, oT=oT, ph=ph: e.tensor_tensor(
                    oT, oT, c.rms[:, ph], OP.mult), deps=[trr, tg1])
            if DEV_RAWO:
                tg4 = tg3
            else:
                tg4 = dve(c, lambda e, oT=oT: e.tensor_scalar(
                    oT, oT, c.norm_w[:], None, OP.mult),
                    deps=[tg3, c.tc["norm_w"]])
            tg5 = dve(c, lambda e, h=h, oT=oT: e.tensor_copy(
                c.ogn[:, 0, h, :], oT),
                deps=[tg4] + c.chunk_gate[0] + c.chunk_gate[1])
            ogn_toks.append(tg5)
            c.head_gate[ph] = [tso, tg5] + getattr(c, "dbg_extra", [])
            c.dbg_extra = []
            chunk_term += [tso, tg5]
        tw = dma_out(c, "wog", fm(c.ds["ogn_scr"])[:, :, chsl],
                     c.ogn[:, 0], ogn_toks)
        if DEV_DUMP:
            if not hasattr(c, "dbg_s"):
                c.dbg_s = c.nc.dram_tensor("dbg_s", [NCH, H * 128, 128], F32,
                                           kind="ExternalOutput").ap()
            dma_out(c, "wdbgs", c.dbg_s[i].rearrange("(h p) t -> p h t", p=128),
                    S_nxt[:], [tS[h] for h in range(H)])
        c.chunk_gate[par] = chunk_term + [tw]
        S_cur, S_nxt = S_nxt, S_cur
    c.tok_ogn_done = barrier(c)


def plan_wout(c):
    sb = c.sbuf
    ognb = sb("ognb", (128, 2, T_OWN))
    wslab = sb("wslab", (128, 2, 512))
    stg = sb("wo_stg", (128, 2, 512))
    NMC = T_OWN // 512 if T_OWN >= 512 else 1
    MVC = min(512, T_OWN)
    NG = 4
    ob_gate = {0: c.bar, 1: c.bar}
    ws_gate = {0: c.bar, 1: c.bar}
    stg_gate = {0: c.bar, 1: c.bar}
    for ng in range(NG):
        mm_toks = {}
        for k in range(KT):
            kb = k % 2
            tog = dma_in(c, "dogn%d" % kb, ognb[:, kb, :],
                         fm(c.ds["ogn_scr"])[:, k, :],
                         [ob_gate[kb], c.bar])
            tws = dma_in(c, "dwo%d" % kb, wslab[:, kb, :],
                         c.din["w_out0"][k * 128:(k + 1) * 128,
                                         ng * 512:(ng + 1) * 512],
                         [ws_gate[kb]])
            for j in range(4):
                for mc in range(NMC):
                    bank = j * NMC + mc
                    msl = slice(mc * 512, mc * 512 + MVC)
                    war = c.bank_gate2.get(bank) if k == 0 else None
                    tmm = mm_group(c, [
                        (lambda e, k=k, kb=kb, j=j, msl=msl, bank=bank:
                         e.matmul(c.ps[:, bank, :MVC],
                                  wslab[:, kb, j * 128:(j + 1) * 128],
                                  ognb[:, kb, msl],
                                  start=(k == 0), stop=(k == KT - 1)))],
                        [tog, tws] + ([war] if war else []))
                    mm_toks[(j, mc)] = tmm
            ob_gate[kb] = mm_toks[(3, NMC - 1)]
            ws_gate[kb] = mm_toks[(3, NMC - 1)]
        for j in range(4):
            for mc in range(NMC):
                bank = j * NMC + mc
                sg = (j * NMC + mc) % 2
                tev = dve(c, lambda e, bank=bank, sg=sg: e.tensor_copy(
                    stg[:, sg, :MVC], c.ps[:, bank, :MVC]),
                    deps=[mm_toks[(j, mc)]] +
                    ([stg_gate[sg]] if stg_gate[sg] else []))
                c.bank_gate2[bank] = tev
                n = ng * 4 + j
                tw = dma_out(c, "wsat%d" % sg,
                             fm(c.ds["sattn_scr"])[:, n,
                                                   mc * 512:mc * 512 + MVC],
                             stg[:, sg, :MVC], [tev])
                stg_gate[sg] = tw


def plan_ffn(c):
    sb = c.sbuf
    c.xh2 = sb("xh2", (128, KT, T_HALF))
    c.xnhi2 = sb("xnhi2", (128, KT, T_HALF), BF16)
    c.xnlo2 = sb("xnlo2", (128, KT, T_HALF), BF16)
    c.hbhi = sb("hbhi", (128, KT, T_HALF), BF16)
    c.hblo = sb("hblo", (128, KT, T_HALF), BF16)
    wstF = sb("w_stage2", (128, 2, KT, 128))
    whlF = sb("whl2", (128, 2, 2, KT, 128), BF16)
    c.wtbc = sb("wtbc", (128, T_HALF))
    c.gstg = sb("gstg", (128, MV))
    c.sath = sb("sath", (128, 2, T_HALF))
    c.xstr = sb("xstr", (128, 2, T_HALF))
    c.mbc1 = c.sbuf_p("mbc1", (128, NHALF, T_HALF))
    c.rstdn1 = c.sbuf_p("rstdn1", (128, NHALF, T_HALF))
    c.vsc = sb("vsc2", (128, T_HALF))
    c.sqx = sb("sqx2", (128, 2, MV))

    c.wstage_gate = {}
    c.whl_gate = {}
    c.bank_gate = dict(c.bank_gate2)
    c.bank_rr = 0
    c.gstg_war0 = None
    c.hbuf_war = {}
    c.wtbc_war = None
    c.b7_war = c.bar
    c.sath_war = {}
    c.xstr_war = {}
    tok_x1w = []

    def mm3(c, bank, buf, mhi, mlo, deps):
        emits = []
        for k in range(KT):
            emits.append(lambda e, k=k, bank=bank, buf=buf, f=(k == 0):
                         e.matmul(c.ps[:, bank, :MV], whlF[:, buf, 0, k, :],
                                  mhi[:, k, :MV], start=f, stop=False))
            emits.append(lambda e, k=k, bank=bank, buf=buf:
                         e.matmul(c.ps[:, bank, :MV], whlF[:, buf, 0, k, :],
                                  mlo[:, k, :MV], start=False, stop=False))
        for k in range(KT):
            emits.append(lambda e, k=k, bank=bank, buf=buf, l=(k == KT - 1):
                         e.matmul(c.ps[:, bank, :MV], whlF[:, buf, 1, k, :],
                                  mhi[:, k, :MV], start=False, stop=l))
        return mm_group(c, emits, deps)

    prev_half = None
    c.wbuf_rr = 0
    for h in range(NHALF):
        hsl = slice(h * T_HALF, (h + 1) * T_HALF)
        xsrc = fm(c.din["xT"])[:, :, hsl]
        tx = dma_in(c, "dx2%d" % (h % 2), c.xh2[:], xsrc,
                    [prev_half, c.bar])
        mbc = c.mbc0[:, h, :]
        rstdn = c.rstdn0[:, h, :]
        xn_toks = build_xn(c, c.xh2, c.xh2, mbc, rstdn, [tx, c.bar], None)
        thi = tlo = None
        for k in range(KT):
            thi, tlo = xsplit_k(c, c.xnhi2[:, k, :], c.xnlo2[:, k, :],
                                c.xh2[:, k, :], [xn_toks[k]])
        txn = [thi, tlo]
        tz0 = gp(c, lambda e: e.memset(c.xh2[:], 0.0),
                 deps=[thi, tlo, prev_half, c.bar])
        tacc = {}
        acc_last = tz0

        seq = []
        for ex in range(E):
            for s in range(KT):
                seq.append(("g", ex, s))
                seq.append(("u", ex, s))
            for n in range(KT):
                seq.append(("d", ex, n))

        def wload3(i):
            kind, ex, s = seq[i]
            mat = {"g": "ffn_wg", "u": "ffn_wu", "d": "ffn_wd"}[kind]
            buf = i % 2
            war = c.wstage_gate.get(buf)
            tl = dma_in(c, "dw%s%d" % (kind, buf), wstF[:, buf],
                        c.din[mat][ex][:, s * 128:(s + 1) * 128]
                        .rearrange("(k p) cc -> p k cc", p=128),
                        [war] if war else [])
            wg = c.whl_gate.get(buf)
            th, tsp = wsplit(c, whlF[:, buf, 0], whlF[:, buf, 1],
                             wstF[:, buf], [tl] + ([wg] if wg else []))
            c.wstage_gate[buf] = tsp
            return buf, th, tsp

        pend = wload3(0)
        gk_state = None
        hb_toks = []
        for i, (kind, ex, s) in enumerate(seq):
            buf, th, tsp = pend
            if i + 1 < len(seq):
                pend = wload3(i + 1)
            if kind == "g" and s == 0:
                twt = mm_group(c, [
                    (lambda e, ex=ex, hsl=hsl: e.matmul(
                        c.ps[:, 7, :T_HALF],
                        c.oh16[0:8, ex * 128:(ex + 1) * 128],
                        c.wt8[:, hsl], start=True, stop=True))],
                    [c.tc["oh16"], c.bar, c.wtbc_war, c.b7_war])
                twtb = dve(c, lambda e: e.tensor_copy(
                    c.wtbc[:], c.ps[:, 7, :T_HALF]), deps=[twt])
                c.b7_war = twtb
            bank = c.bank_rr
            c.bank_rr = (c.bank_rr + 1) % 4
            bwar = c.bank_gate.get(bank)
            if kind == "g":
                tmm = mm3(c, bank, buf, c.xnhi2, c.xnlo2,
                          [th, tsp] + txn + ([bwar] if bwar else []))
                c.whl_gate[buf] = tmm
                tsi = act(c, lambda e, bank=bank: e.activation(
                    c.gstg[:], c.ps[:, bank, :MV], AF.Silu),
                    deps=[tmm] + ([c.gstg_war0] if c.gstg_war0 else []))
                c.bank_gate[bank] = tsi
                gk_state = tsi
            elif kind == "u":
                tmm = mm3(c, bank, buf, c.xnhi2, c.xnlo2,
                          [th, tsp] + txn + ([bwar] if bwar else []))
                c.whl_gate[buf] = tmm
                th1 = dve(c, lambda e, bank=bank: e.tensor_tensor(
                    c.gstg[:], c.gstg[:], c.ps[:, bank, :MV], OP.mult),
                    deps=[gk_state, tmm])
                th2 = dve(c, lambda e: e.tensor_tensor(
                    c.gstg[:], c.gstg[:], c.wtbc[:], OP.mult),
                    deps=[th1, twtb])
                c.bank_gate[bank] = th1
                hwar = c.hbuf_war.get(s)
                tbh = act(c, lambda e, s=s: e.activation(
                    c.hbhi[:, s, :], c.gstg[:], AF.Copy),
                    deps=[th2] + ([hwar] if hwar else []))
                tbl = dve(c, lambda e, s=s: e.tensor_tensor(
                    c.hblo[:, s, :], c.gstg[:], c.hbhi[:, s, :],
                    OP.subtract), deps=[tbh])
                c.gstg_war0 = tbl
                hb_toks = [tbh, tbl]
            else:
                tmm = mm3(c, bank, buf, c.hbhi, c.hblo,
                          [th, tsp] + hb_toks + ([bwar] if bwar else []))
                c.whl_gate[buf] = tmm
                tad = dve(c, lambda e, s=s, bank=bank: e.tensor_tensor(
                    c.xh2[:, s, :], c.xh2[:, s, :], c.ps[:, bank, :MV],
                    OP.add), deps=[tmm, tacc.get(s, tz0), acc_last])
                tacc[s] = tad
                acc_last = tad
                c.bank_gate[bank] = tad
                if s == KT - 1:
                    c.hbuf_war = {k2: acc_last for k2 in range(KT)}
                    c.wtbc_war = acc_last
        x1_toks = []
        for s in range(KT):
            sp2 = s % 2
            tsa = dma_in(c, "dsat%d" % sp2, c.sath[:, sp2, :],
                         fm(c.ds["sattn_scr"])[:, s, hsl],
                         [c.bar, c.sath_war.get(sp2)])
            txr = dma_in(c, "dxr%d" % sp2, c.xstr[:, sp2, :],
                         fm(c.din["xT"])[:, s, hsl],
                         [c.xstr_war.get(sp2)])
            t1 = act(c, lambda e, s=s: e.activation(
                c.xh2[:, s, :], c.xh2[:, s, :], AF.Silu),
                deps=[acc_last])
            t2 = dve(c, lambda e, s=s, sp2=sp2: e.tensor_tensor(
                c.xh2[:, s, :], c.xh2[:, s, :], c.sath[:, sp2, :],
                OP.mult), deps=[t1, tsa])
            c.sath_war[sp2] = t2
            t3 = dve(c, lambda e, s=s, sp2=sp2: e.tensor_tensor(
                c.xh2[:, s, :], c.xh2[:, s, :], c.xstr[:, sp2, :],
                OP.add), deps=[t2, txr])
            c.xstr_war[sp2] = t3
            x1_toks.append(t3)
        tw1 = dma_out(c, "wx1", fm(c.ds["x1_scr"])[:, :, hsl], c.xh2[:],
                      x1_toks)
        tok_x1w.append(tw1)
        st_toks = plan_stats(c, c.xh2, c.sqx, c.mbc1[:, h, :],
                             c.rstdn1[:, h, :], c.vsc, x1_toks[-1],
                             [getattr(c, "st1_war", None)])
        c.st1_war = st_toks[-1]
        prev_half = barrier(c)
    c.tok_x1w = tok_x1w


def plan_l1attn(c):
    sb = c.sbuf
    x1b = sb("x1b", (128, KT, T_OWN))
    wst = sb("w_stage3", (128, 2, KT, 128))
    pstg = sb("pstg", (128, 3, T_OWN))
    zt = sb("zt1", (128, T_OWN))
    scn = sb("scn", (128, 2, T_OWN))
    cc3stage = sb("cc3stage", (128, KT, 2))
    gath3 = sb("gath3", (128, GRP, KT, 2))
    carry = sb("carry", (128, 3, KT))
    bdec = sb("bdec", (128, KT))
    c.gstg1 = sb("gstg1", (128, 2, 512))
    NMC = max(1, T_OWN // 512)
    MVC = min(512, T_OWN)

    tz = gp(c, lambda e: e.memset(zt[:], 0.0), deps=[c.bar])
    tx1 = dma_in(c, "dx1", x1b[:], fm(c.ds["x1_scr"]), [c.bar])
    txn1 = None
    for hc in range(NHALF):
        hsl0 = slice(hc * T_HALF, (hc + 1) * T_HALF)
        for k in range(KT):
            t1_ = dve(c, lambda e, k=k, hsl0=hsl0, hc=hc:
                      e.scalar_tensor_tensor(
                          x1b[:, k, hsl0], c.mbc1[:, hc, :], 1.0,
                          x1b[:, k, hsl0], OP.mult, OP.subtract),
                      deps=[tx1, c.bar])
            txn1 = dve(c, lambda e, k=k, hsl0=hsl0, hc=hc: e.tensor_tensor(
                x1b[:, k, hsl0], x1b[:, k, hsl0], c.rstdn1[:, hc, :],
                OP.mult), deps=[t1_])
    tbd = dve(c, lambda e: e.tensor_tensor(
        bdec[:], c.tensors["l1bd"][:], c.tensors["l1db"][:], OP.add),
        deps=[c.tc["l1_b_dec"], c.tc["l1_dec_bias"], c.bar])

    c.wstage_gate = {0: c.bar, 1: c.bar}
    c.bank_gate = dict(c.bank_gate2)
    c.bank_rr = 0
    pstg_war = [c.bar, c.bar, c.bar]
    scn_war = [c.bar, c.bar]
    cc3_toks = []
    wdec_toks = {}
    wd1_toks = {}
    wgate_toks = {}

    def proj(mat, s, buf, out_slot, bias_col, af, extra_dep):
        war = c.wstage_gate.get(buf)
        tl = dma_in(c, "dl1%d" % buf, wst[:, buf],
                    c.din[mat][:, s * 128:(s + 1) * 128]
                    .rearrange("(k p) cc -> p k cc", p=128),
                    [war] + ([extra_dep] if extra_dep else []))
        touts = []
        for mc in range(NMC):
            msl = slice(mc * 512, mc * 512 + MVC)
            bk = c.bank_rr
            c.bank_rr = (c.bank_rr + 1) % 4
            bw = c.bank_gate.get(bk)
            tmm = mm_group(c, [
                (lambda e, k=k, buf=buf, bk=bk, msl=msl, f=(k == 0),
                 l=(k == KT - 1):
                 e.matmul(c.ps[:, bk, :MVC], wst[:, buf, k, :],
                          x1b[:, k, msl], start=f, stop=l))
                for k in range(KT)],
                [tl, txn1] + ([bw] if bw else []))
            out_ap = pstg[:, out_slot, msl]
            ta = act(c, lambda e, out_ap=out_ap, bias_col=bias_col, af=af,
                     bk=bk: e.activation(out_ap, c.ps[:, bk, :MVC], af,
                                         bias=bias_col),
                     deps=[tmm, tbd] + c.tok_consts_l1 +
                     ([pstg_war[out_slot]] if pstg_war[out_slot] else []))
            c.bank_gate[bk] = ta
            touts.append(ta)
        c.wstage_gate[buf] = touts[-1]
        return touts

    c.tok_consts_l1 = [c.tc["l1_b_in"], c.tc["l1_b_gate"], c.tc["l1_b_out"]]
    for s in range(KT):
        t_in = proj("l1_w_in", s, s % 2, 0, c.tensors["l1bi"][:, s:s + 1],
                    AF.Silu, None)
        t_dec = proj("l1_w_dec", s, (s + 1) % 2, 1, bdec[:, s:s + 1],
                     AF.Sigmoid, None)
        td1 = dve(c, lambda e: e.tensor_tensor(
            pstg[:, 2, :], pstg[:, 1, :], pstg[:, 0, :], OP.mult),
            deps=t_in + t_dec + ([pstg_war[2]] if pstg_war[2] else []))
        td1 = dve(c, lambda e: e.tensor_tensor(
            pstg[:, 2, :], pstg[:, 0, :], pstg[:, 2, :], OP.subtract),
            deps=[td1])
        ts1 = dve(c, lambda e: e.tensor_tensor_scan(
            scn[:, 0, :], pstg[:, 1, :], pstg[:, 2, :], 0.0, OP.mult,
            OP.add), deps=[td1] + ([scn_war[0]] if scn_war[0] else []))
        ts2 = dve(c, lambda e: e.tensor_tensor_scan(
            scn[:, 1, :], pstg[:, 1, :], zt[:], 1.0, OP.mult, OP.add),
            deps=[td1, tz] + ([scn_war[1]] if scn_war[1] else []))
        te1 = dve(c, lambda e, s=s: e.tensor_copy(
            cc3stage[:, s, 0:1], scn[:, 1, T_OWN - 1:T_OWN]), deps=[ts2])
        te2 = dve(c, lambda e, s=s: e.tensor_copy(
            cc3stage[:, s, 1:2], scn[:, 0, T_OWN - 1:T_OWN]), deps=[ts1])
        scn_war = [te2, te1]
        cc3_toks += [te1, te2]
        twd = dma_out(c, "wdec", fm(c.ds["dec_scr"])[:, s, :],
                      pstg[:, 1, :], t_dec + [td1, ts1, ts2])
        twd1 = dma_out(c, "wdd1", fm(c.ds["dd1_scr"])[:, s, :],
                       pstg[:, 2, :], [td1, ts1])
        wdec_toks[s] = twd
        wd1_toks[s] = twd1
        pstg_war[1] = twd
        pstg_war[2] = twd1
        pstg_war[0] = td1
        t_g = proj("l1_w_gate", s, s % 2, 0, c.tensors["l1bg"][:, s:s + 1],
                   AF.Sigmoid, t_in[-1])
        c.x1b_free = t_g[-1]
        twg = dma_out(c, "wgt", fm(c.ds["gate_scr"])[:, s, :],
                      pstg[:, 0, :], t_g)
        wgate_toks[s] = twg
        pstg_war[0] = twg
    tw3 = dma_out(c, "wcc3",
                  c.ds["cc3_in"].rearrange("(s p) two -> p s two", p=128),
                  cc3stage[:], cc3_toks)
    tcc = ccop(c, lambda e: e.collective_compute(
        "AllGather", OP.bypass, replica_groups=RG,
        ins=[c.ds["cc3_in"]], outs=[c.ds["cc3_out"]]), deps=[tw3])
    tg3 = []
    for g in range(GRP):
        tg3.append(dma_in(
            c, "dcc3", gath3[:, g],
            c.ds["cc3_out"][g].rearrange("(s p) two -> p s two", p=128),
            [tcc]))
    tc1 = dve(c, lambda e: e.tensor_copy(carry[:, 0, :], gath3[:, 0, :, 1]),
              deps=tg3)
    tprev = tc1
    for rr in (2, 3):
        t_a = dve(c, lambda e, rr=rr: e.tensor_tensor(
            carry[:, rr - 1, :], gath3[:, rr - 1, :, 0],
            carry[:, rr - 2, :], OP.mult), deps=[tprev] + tg3)
        tprev = dve(c, lambda e, rr=rr: e.tensor_tensor(
            carry[:, rr - 1, :], carry[:, rr - 1, :],
            gath3[:, rr - 1, :, 1], OP.add), deps=[t_a])
    tsel = dve(c, lambda e: e.tensor_scalar(
        carry[:, 0, :], carry[:, 0, :], c.sel[:, 1:2], None, OP.mult),
        deps=[tprev, c.tc["sel"]])
    for rr in (2, 3):
        tsel = dve(c, lambda e, rr=rr: e.scalar_tensor_tensor(
            carry[:, 0, :], carry[:, rr - 1, :], c.sel[:, rr:rr + 1],
            carry[:, 0, :], OP.mult, OP.add), deps=[tsel])
    y_toks = []
    for s in range(KT):
        sl2 = s % 2
        tdd = dma_in(c, "drd%d" % sl2, pstg[:, 1, :],
                     fm(c.ds["dec_scr"])[:, s, :],
                     [wdec_toks[s], tsel] +
                     ([pstg_war[1]] if pstg_war[1] else []))
        tdd1 = dma_in(c, "drd1%d" % sl2, pstg[:, 2, :],
                      fm(c.ds["dd1_scr"])[:, s, :],
                      [wd1_toks[s]] +
                      ([pstg_war[2]] if pstg_war[2] else []))
        ts = dve(c, lambda e, s=s: e.tensor_tensor_scan(
            scn[:, 0, :], pstg[:, 1, :], pstg[:, 2, :],
            carry[:, 0, s:s + 1], OP.mult, OP.add),
            deps=[tdd, tdd1, tsel] + ([scn_war[0]] if scn_war[0] else []))
        pstg_war[1] = ts
        pstg_war[2] = ts
        tgt = dma_in(c, "drg%d" % sl2, pstg[:, 0, :],
                     fm(c.ds["gate_scr"])[:, s, :],
                     [wgate_toks[s]] +
                     ([pstg_war[0]] if pstg_war[0] else []))
        ty = dve(c, lambda e, s=s: e.tensor_tensor(
            x1b[:, s, :], pstg[:, 0, :], scn[:, 0, :], OP.mult),
            deps=[tgt, ts, c.x1b_free])
        pstg_war[0] = ty
        scn_war[0] = ty
        y_toks.append(ty)
    stg_war = [c.bar, c.bar]
    for s in range(KT):
        buf = s % 2
        war = c.wstage_gate.get(buf)
        tl = dma_in(c, "dl1o%d" % buf, wst[:, buf],
                    c.din["l1_w_out"][:, s * 128:(s + 1) * 128]
                    .rearrange("(k p) cc -> p k cc", p=128),
                    [war] + y_toks)
        for mc in range(NMC):
            msl = slice(mc * 512, mc * 512 + MVC)
            bk = c.bank_rr
            c.bank_rr = (c.bank_rr + 1) % 4
            bw = c.bank_gate.get(bk)
            tmm = mm_group(c, [
                (lambda e, k=k, buf=buf, bk=bk, msl=msl, f=(k == 0),
                 l=(k == KT - 1):
                 e.matmul(c.ps[:, bk, :MVC], wst[:, buf, k, :],
                          x1b[:, k, msl], start=f, stop=l))
                for k in range(KT)],
                [tl] + y_toks + ([bw] if bw else []))
            sg = (s * NMC + mc) % 2
            tev = dve(c, lambda e, bk=bk, sg=sg, s=s: e.tensor_scalar(
                pstg[:, sg, :MVC] if False else c.gstg1[:, sg, :MVC],
                c.ps[:, bk, :MVC], c.tensors["l1bo"][:, s:s + 1], None,
                OP.add),
                deps=[tmm, c.tc["l1_b_out"]] +
                ([stg_war[sg]] if stg_war[sg] else []))
            c.bank_gate[bk] = tev
            tw = dma_out(c, "ws1%d" % sg,
                         fm(c.ds["s1_scr"])[:, s, msl],
                         c.gstg1[:, sg, :MVC], [tev])
            stg_war[sg] = tw
        c.wstage_gate[buf] = tw


def plan_bitnet(c):
    import contextlib
    sb = c.sbuf
    wst = sb("w_stage4", (128, 2, KT, 128))
    wqs = sb("wqs", (128, 4, KT, 128), BF16)
    amaxb = sb("amaxb", (128, T_OWN))
    scaleb = sb("scaleb", (128, T_OWN))
    deqb = sb("deqb", (128, T_OWN))
    cols4 = sb("cols4", (128, 16))
    colacc = sb("colacc", (128, 3))
    mrow = sb("mrow", (1, 128))
    NMC = max(1, T_OWN // 512)
    MVC = min(512, T_OWN)
    NQ = max(1, T_OWN // 128)
    QV = min(128, T_OWN)

    tzc = gp(c, lambda e: e.memset(colacc[:], 0.0), deps=[c.bar])
    stat_done = {}
    c.wstage_gate = {}
    tacc = tzc
    for mi, (mat, nk, nstr) in enumerate(
            (("bn_wgT", KT, FI // 128), ("bn_wuT", KT, FI // 128),
             ("bn_wdT", KT, (FI // 2048) * KT))):
        for s in range(nstr):
            buf = s % 2
            war = c.wstage_gate.get(buf)
            if mat == "bn_wdT":
                n_col, kg = divmod(s, FI // 2048)
                srcv = c.din[mat][kg * 2048:(kg + 1) * 2048,
                                  n_col * 128:(n_col + 1) * 128]
            else:
                srcv = c.din[mat][:, s * 128:(s + 1) * 128]
            tl = dma_in(c, "dbs%d" % buf, wst[:, buf],
                        srcv.rearrange("(k p) cc -> p k cc", p=128),
                        [war, c.bar])
            tr = dve(c, lambda e, buf=buf, mi=mi: e.tensor_reduce(
                cols4[:, 13:14], wst[:, buf].rearrange("p a b -> p (a b)"),
                AX.X, OP.add, apply_absolute_value=True),
                deps=[tl, tacc])
            tacc = dve(c, lambda e, mi=mi: e.tensor_tensor(
                colacc[:, mi:mi + 1], colacc[:, mi:mi + 1],
                cols4[:, 13:14], OP.add), deps=[tr])
            c.wstage_gate[buf] = tr
        stat_done[mi] = tacc
    nel = float(FI * D)
    tstat = []
    for mi in range(3):
        ap, tmm, b = mmp(c, [
            (lambda e, ap, mi=mi: e.matmul(
                ap, c.ones[:], colacc[:, mi:mi + 1], start=True,
                stop=True))], [stat_done[mi], c.tc["ones"]], width=1)
        t1 = dve(c, lambda e, ap=ap, mi=mi: e.tensor_scalar(
            cols4[:, mi * 4:mi * 4 + 1], ap, 1.0 / nel, 1e-5,
            OP.mult, OP.max), deps=[tmm])
        c.bank_gate2[b] = t1
        t2 = dve(c, lambda e, mi=mi: e.tensor_scalar(
            cols4[:, mi * 4 + 1:mi * 4 + 2], cols4[:, mi * 4:mi * 4 + 1],
            0.5, None, OP.mult), deps=[t1])
        t2b = dve(c, lambda e, mi=mi: e.tensor_scalar(
            cols4[:, mi * 4 + 2:mi * 4 + 3], cols4[:, mi * 4:mi * 4 + 1],
            -0.5, None, OP.mult), deps=[t1])
        tstat.append((t1, t2, t2b))

    def tern(dst_ap, src_ap, mi, deps):
        t_a = dve(c, lambda e, src_ap=src_ap, mi=mi: e.tensor_scalar(
            ta_s[:], src_ap, cols4[:, mi * 4 + 2:mi * 4 + 3], None,
            OP.is_lt), deps=deps + [tstat[mi][2], c.tern_war])
        t_b = dve(c, lambda e, dst_ap=dst_ap, src_ap=src_ap, mi=mi:
                  e.scalar_tensor_tensor(
                      dst_ap, src_ap, cols4[:, mi * 4 + 1:mi * 4 + 2],
                      ta_s[:], OP.is_gt, OP.subtract),
                  deps=[t_a, tstat[mi][1]])
        c.tern_war = t_b
        return t_b
    c.tern_war = None

    scopeA = contextlib.ExitStack()
    old_es = c.phase_es
    c.phase_es = scopeA
    xqb = sb("xqb", (128, KT, T_OWN), BF16)
    c.phase_es = old_es
    with contextlib.ExitStack() as sub:
        c.phase_es = sub
        x1nb = sb("x1nb", (128, KT, T_OWN))
        c.phase_es = old_es
        tx1 = dma_in(c, "dx1b", x1nb[:], fm(c.ds["x1_scr"]), [c.bar])
        xn_toks = []
        for s in range(KT):
            tks = []
            for hc in range(NHALF):
                hsl = slice(hc * T_HALF, (hc + 1) * T_HALF)
                t1 = dve(c, lambda e, s=s, hc=hc, hsl=hsl:
                         e.scalar_tensor_tensor(
                             x1nb[:, s, hsl], c.mbc1[:, hc, :], 1.0,
                             x1nb[:, s, hsl], OP.mult, OP.subtract),
                         deps=[tx1, c.bar])
                t2 = dve(c, lambda e, s=s, hc=hc, hsl=hsl: e.tensor_tensor(
                    x1nb[:, s, hsl], x1nb[:, s, hsl], c.rstdn1[:, hc, :],
                    OP.mult), deps=[t1])
                tks.append(t2)
            xn_toks.append(tks[-1])
        amax_toks = []
        for tc2 in range(T_OWN // 128):
            tmx = None
            for k in range(KT):
                ap, tt, b = trp(c, x1nb[:, k, tc2 * 128:(tc2 + 1) * 128],
                                c.eye[:], [xn_toks[k]])
                trd = dve(c, lambda e, ap=ap: e.tensor_reduce(
                    cols4[:, 13:14], ap, AX.X, OP.max,
                    apply_absolute_value=True),
                    deps=[tt] + ([tmx] if tmx else []) +
                    ([c.col11_war] if getattr(c, "col11_war", None)
                     else []))
                c.col11_war = trd
                c.bank_gate2[b] = trd
                if k == 0:
                    tmx = dve(c, lambda e: e.tensor_copy(
                        cols4[:, 12:13], cols4[:, 13:14]), deps=[trd])
                else:
                    tmx = dve(c, lambda e: e.tensor_tensor(
                        cols4[:, 12:13], cols4[:, 12:13], cols4[:, 13:14],
                        OP.max), deps=[trd])
            ap2, tt2, b2 = trp(c, cols4[:, 12:13], c.eye[:], [tmx],
                               width=128, rows=1)
            tcp = dve(c, lambda e, ap2=ap2: e.tensor_copy(
                mrow[:1, :128], ap2[:1, :128]), deps=[tt2])
            c.bank_gate2[b2] = tcp
            ap3, tt3, b3 = mmp(c, [
                (lambda e, ap: e.matmul(ap, c.ones[:1, :], mrow[:1, :128],
                                        start=True, stop=True))],
                [tcp, c.tc["ones"]])
            tev = dve(c, lambda e, ap3=ap3, tc2=tc2: e.tensor_copy(
                amaxb[:, tc2 * 128:(tc2 + 1) * 128], ap3), deps=[tt3])
            c.bank_gate2[b3] = tev
            amax_toks.append(tev)
        t1 = dve(c, lambda e: e.tensor_single_scalar(
            amaxb[:], amaxb[:], 1e-5, OP.max), deps=amax_toks)
        t2 = dve(c, lambda e: e.reciprocal(scaleb[:], amaxb[:]), deps=[t1])
        t3 = dve(c, lambda e: e.tensor_scalar_mul(scaleb[:], scaleb[:],
                                                  128.0), deps=[t2])
        t4 = dve(c, lambda e: e.tensor_scalar_mul(deqb[:], amaxb[:],
                                                  1.0 / 128.0), deps=[t1])
        c.tok_deqb = t4
        xq_toks = []
        for s in range(KT):
            tq = dve(c, lambda e, s=s: e.tensor_tensor(
                x1nb[:, s, :], x1nb[:, s, :], scaleb[:], OP.mult),
                deps=[t3, xn_toks[s]])
            tq = act(c, lambda e, s=s: e.activation(
                x1nb[:, s, :], x1nb[:, s, :], AF.Copy, bias=MAGIC),
                deps=[tq])
            tq = act(c, lambda e, s=s: e.activation(
                x1nb[:, s, :], x1nb[:, s, :], AF.Copy, bias=-MAGIC),
                deps=[tq])
            tq = dve(c, lambda e, s=s: e.tensor_single_scalar(
                x1nb[:, s, :], x1nb[:, s, :], 127.0, OP.min), deps=[tq])
            tq = dve(c, lambda e, s=s: e.tensor_single_scalar(
                x1nb[:, s, :], x1nb[:, s, :], -128.0, OP.max), deps=[tq])
            tq = dve(c, lambda e, s=s: e.tensor_copy(
                xqb[:, s, :], x1nb[:, s, :]), deps=[tq])
            xq_toks.append(tq)
        c.xq_done = barrier(c)
    c.bar_q = barrier(c)
    c.phase_es = scopeA
    istg = sb("istg", (128, 3, T_OWN))
    ta_s = sb("ta_s", (128, KT * 128))
    c.phase_es = old_es

    c.wstage_gate = {}
    c.wqs_gate = {}
    istg_war = [None, None, None]
    seq = []
    for n in range(FI // 128):
        seq.append(("g", n))
        seq.append(("u", n))

    def ldtern(i):
        mat, n = seq[i]
        buf = i % 2
        war = c.wstage_gate.get(buf)
        qwar = c.wqs_gate.get(buf)
        srcn = "bn_wgT" if mat == "g" else "bn_wuT"
        tl = dma_in(c, "db%s%d" % (mat, n % 2), wst[:, buf],
                    c.din[srcn][:, n * 128:(n + 1) * 128]
                    .rearrange("(k p) cc -> p k cc", p=128),
                    [war, c.bar_q])
        tt = tern(wqs[:, buf].rearrange("p a b -> p (a b)"),
                  wst[:, buf].rearrange("p a b -> p (a b)"),
                  0 if mat == "g" else 1,
                  [tl] + ([qwar] if qwar else []))
        c.wstage_gate[buf] = tt
        return buf, tt

    pend = ldtern(0)
    g_toks = []
    for i, (mat, n) in enumerate(seq):
        buf, tt = pend
        if i + 1 < len(seq):
            pend = ldtern(i + 1)
        mi = 0 if mat == "g" else 1
        slot = 0 if mat == "g" else 1
        mm_last = None
        de_toks = []
        for mc in range(NMC):
            msl = slice(mc * 512, mc * 512 + MVC)
            ap, tmm, b = mmp(c, [
                (lambda e, ap, k=k, buf=buf, msl=msl, f=(k == 0),
                 l=(k == KT - 1):
                 e.matmul(ap[:, :MVC], wqs[:, buf, k, :], xqb[:, k, msl],
                          start=f, stop=l))
                for k in range(KT)], [tt, c.xq_done], width=512)
            mm_last = tmm
            tde = dve(c, lambda e, ap=ap, msl=msl, mi=mi, slot=slot:
                      e.scalar_tensor_tensor(
                          istg[:, slot, msl], ap[:, :MVC],
                          cols4[:, mi * 4:mi * 4 + 1], deqb[:, msl],
                          OP.mult, OP.mult),
                      deps=[tmm, tstat[mi][0], c.tok_deqb] +
                      ([istg_war[slot]] if istg_war[slot] else []))
            c.bank_gate2[b] = tde
            de_toks.append(tde)
        c.wqs_gate[buf] = mm_last
        if mat == "g":
            g_toks = de_toks
            istg_war[0] = de_toks[-1]
        else:
            istg_war[1] = de_toks[-1]
            tsi = act(c, lambda e: e.activation(
                istg[:, 0, :], istg[:, 0, :], AF.Silu), deps=g_toks)
            tmul = dve(c, lambda e: e.tensor_tensor(
                istg[:, 2, :], istg[:, 0, :], istg[:, 1, :], OP.mult),
                deps=[tsi] + de_toks + ([istg_war[2]] if istg_war[2]
                                        else []))
            twi = dma_out(c, "wint%d" % (n % 2),
                          fm(c.ds["int_scr"])[:, n, :],
                          istg[:, 2, :], [tmul])
            istg_war[0] = tmul
            istg_war[1] = tmul
            istg_war[2] = twi
            c.int_w = getattr(c, "int_w", {})
            c.int_w[n] = twi

    wd_toks = {}
    for s in range((FI // 2048) * KT):
        n_col, kg = divmod(s, FI // 2048)
        buf = s % 2
        war = c.wstage_gate.get(buf)
        tl = dma_in(c, "dbd%d" % buf, wst[:, buf],
                    c.din["bn_wdT"][kg * 2048:(kg + 1) * 2048,
                                    n_col * 128:(n_col + 1) * 128]
                    .rearrange("(k p) cc -> p k cc", p=128),
                    [war, c.bar_q])
        tt = tern(wqs[:, buf].rearrange("p a b -> p (a b)"),
                  wst[:, buf].rearrange("p a b -> p (a b)"), 2,
                  [tl] + ([c.wqs_gate.get(buf)] if c.wqs_gate.get(buf)
                          else []))
        tw = dma_out(c, "wwdq%d" % buf,
                     c.ds["wdq_scr"][kg * 2048:(kg + 1) * 2048,
                                     n_col * 128:(n_col + 1) * 128]
                     .rearrange("(k p) cc -> p k cc", p=128),
                     wqs[:, buf], [tt])
        wd_toks[(n_col, kg)] = tw
        c.wstage_gate[buf] = tw
    c.bar_w = barrier(c)
    scopeA.close()

    c.had = c.tensors["had128"]
    QV2 = min(2 * QV, T_OWN)
    intq = sb("intq", (128, FIT, QV))
    iqb = sb("iqb", (128, FIT, QV2), BF16)
    bout = sb("bout", (128, KT, QV2))
    amax2 = sb("amax2", (128, QV2))
    scale2 = sb("scale2", (128, QV2))
    deq2 = sb("deq2", (128, QV2))
    ostg = sb("ostg", (128, 2, QV2))
    s1q = sb("s1q", (128, 2, QV2))
    FW_C = 1.0 / math.sqrt(float(FI))
    tmem = None
    for buf_t in (intq, iqb, bout, amax2, scale2, deq2):
        tmem = gp(c, lambda e, t=buf_t: e.memset(t[:], 0.0),
                  deps=[c.bar_w] + ([tmem] if tmem else []))
    q_done = None
    out_toks = []
    pair_iq = []
    pair_t4 = []
    for qt in range(NQ):
        qsl = slice(qt * QV, (qt + 1) * QV)
        off = (qt % (QV2 // QV)) * QV
        osl = slice(off, off + QV)
        tiq = dma_in(c, "dintq", intq[:, :, :QV],
                     fm(c.ds["int_scr"])[:, :, qsl],
                     [c.bar_w, q_done, tmem] + [c.int_w[n]
                                                for n in range(FI // 128)])
        slab_toks = []
        for nsl in range(FIT):
            ap, tmm, b = mmp(c, [
                (lambda e, ap, nsl=nsl: e.matmul(
                    ap[:, :QV], c.had[:], intq[:, nsl, :QV],
                    start=True, stop=True))],
                [tiq, c.tc["had128"]], width=256)
            tev = dve(c, lambda e, ap=ap, nsl=nsl: e.tensor_copy(
                intq[:, nsl, :QV], ap[:, :QV]), deps=[tmm])
            c.bank_gate2[b] = tev
            slab_toks.append(tev)
        tlv = slab_toks[-1:]
        for lv in range(6):
            st = 2 ** lv
            va = intq.rearrange("p (a b cc) t -> p a b cc t", b=2, cc=st)
            t_u = dve(c, lambda e, va=va: e.tensor_tensor(
                va[:, :, 1, :, :QV], va[:, :, 0, :, :QV],
                va[:, :, 1, :, :QV], OP.subtract), deps=tlv + slab_toks)
            t_v = dve(c, lambda e, va=va: e.scalar_tensor_tensor(
                va[:, :, 0, :, :QV], va[:, :, 0, :, :QV], 2.0,
                va[:, :, 1, :, :QV], OP.mult, OP.subtract), deps=[t_u])
            tlv = [t_v]
        amax_toks = []
        for tc2 in range(QV // 128):
            tmx = None
            for k in range(FIT):
                ap, tt, b = trp(c, intq[:, k, tc2 * 128:(tc2 + 1) * 128],
                                c.eye[:], tlv)
                trd = dve(c, lambda e, ap=ap: e.tensor_reduce(
                    cols4[:, 13:14], ap, AX.X, OP.max,
                    apply_absolute_value=True),
                    deps=[tt] + ([tmx] if tmx else []) + [c.col11_war])
                c.col11_war = trd
                c.bank_gate2[b] = trd
                if k == 0:
                    tmx = dve(c, lambda e: e.tensor_copy(
                        cols4[:, 12:13], cols4[:, 13:14]), deps=[trd])
                else:
                    tmx = dve(c, lambda e: e.tensor_tensor(
                        cols4[:, 12:13], cols4[:, 12:13], cols4[:, 13:14],
                        OP.max), deps=[trd])
            ap2, tt2, b2 = trp(c, cols4[:, 12:13], c.eye[:], [tmx],
                               width=128, rows=1)
            tcp = dve(c, lambda e, ap2=ap2: e.tensor_copy(
                mrow[:1, :128], ap2[:1, :128]), deps=[tt2])
            c.bank_gate2[b2] = tcp
            ap3, tt3, b3 = mmp(c, [
                (lambda e, ap: e.matmul(ap, c.ones[:1, :], mrow[:1, :128],
                                        start=True, stop=True))],
                [tcp, c.tc["ones"]])
            tev = dve(c, lambda e, ap3=ap3, tc2=tc2, off=off: e.tensor_copy(
                amax2[:, off + tc2 * 128:off + (tc2 + 1) * 128], ap3),
                deps=[tt3])
            c.bank_gate2[b3] = tev
            amax_toks.append(tev)
        t1 = dve(c, lambda e, osl=osl: e.tensor_scalar(
            amax2[:, osl], amax2[:, osl], c.kc(6), None, OP.max),
            deps=amax_toks + [c.tc["kcol"]])
        t2 = dve(c, lambda e, osl=osl: e.reciprocal(scale2[:, osl],
                                                    amax2[:, osl]),
                 deps=[t1])
        t3 = dve(c, lambda e, osl=osl: e.tensor_scalar_mul(scale2[:, osl],
                                                           scale2[:, osl],
                                                           128.0), deps=[t2])
        t4 = dve(c, lambda e, osl=osl: e.tensor_scalar_mul(
            deq2[:, osl], amax2[:, osl], FW_C / 128.0), deps=[t1])
        iq_toks = []
        for k in range(FIT):
            tq = dve(c, lambda e, k=k, osl=osl: e.tensor_tensor(
                intq[:, k, :QV], intq[:, k, :QV], scale2[:, osl], OP.mult),
                deps=[t3] + tlv)
            tq = act(c, lambda e, k=k: e.activation(
                intq[:, k, :QV], intq[:, k, :QV], AF.Copy, bias=MAGIC),
                deps=[tq])
            tq = act(c, lambda e, k=k: e.activation(
                intq[:, k, :QV], intq[:, k, :QV], AF.Copy, bias=-MAGIC),
                deps=[tq])
            tq = dve(c, lambda e, k=k: e.tensor_single_scalar(
                intq[:, k, :QV], intq[:, k, :QV], 127.0, OP.min), deps=[tq])
            tq = dve(c, lambda e, k=k: e.tensor_single_scalar(
                intq[:, k, :QV], intq[:, k, :QV], -128.0, OP.max),
                deps=[tq])
            tq = dve(c, lambda e, k=k, osl=osl: e.tensor_copy(
                iqb[:, k, osl], intq[:, k, :QV]), deps=[tq])
            iq_toks.append(tq)
        pair_iq += iq_toks
        pair_t4.append(t4)
        if (qt + 1) % (QV2 // QV) != 0:
            q_done = barrier(c)
            continue
        q2sl = slice((qt + 1) * QV - QV2, (qt + 1) * QV)
        dn_gate = {}
        for n in range(KT):
            bank = n % 8
            bwar = dn_gate.get(bank, c.bank_gate2.get(bank))
            mm_last = None
            for kg in range(FI // 2048):
                buf = (n * (FI // 2048) + kg) % 4
                war = c.wstage_gate.get(buf)
                tl = dma_in(c, "dbq%d" % buf, wqs[:, buf],
                            c.ds["wdq_scr"][kg * 2048:(kg + 1) * 2048,
                                            n * 128:(n + 1) * 128]
                            .rearrange("(k p) cc -> p k cc", p=128),
                            [war, wd_toks[(n, kg)]])
                mm_last = mm_group(c, [
                    (lambda e, k=k, buf=buf, bank=bank, kg=kg,
                     f=(kg == 0 and k == 0),
                     l=(kg == FI // 2048 - 1 and k == KT - 1):
                     e.matmul(c.ps[:, bank, :QV2], wqs[:, buf, k, :],
                              iqb[:, kg * KT + k, :QV2], start=f, stop=l))
                    for k in range(KT)],
                    [tl] + pair_iq + ([bwar] if bwar else []))
                c.wstage_gate[buf] = mm_last
            tde = dve(c, lambda e, bank=bank, n=n: e.scalar_tensor_tensor(
                bout[:, n, :QV2], c.ps[:, bank, :QV2], cols4[:, 8:9],
                deq2[:, :QV2], OP.mult, OP.mult),
                deps=[mm_last, tstat[2][0]] + pair_t4 + (
                    [c.bout_war.get(n)] if getattr(c, "bout_war", None)
                    and c.bout_war.get(n) else []))
            dn_gate[bank] = tde
            c.bank_gate2[bank] = tde
            c.bout_war = getattr(c, "bout_war", {})
            c.bout_war[n] = tde
        pair_iq = []
        pair_t4 = []
        fin_toks = []
        ow = getattr(c, "ostg_war", {})
        for s in range(KT):
            sp = s % 2
            tx1q = dma_in(c, "dx1q%d" % sp, ostg[:, sp, :],
                          fm(c.ds["x1_scr"])[:, s, q2sl],
                          [c.bar_w, q_done] +
                          ([ow.get(sp)] if ow.get(sp) else []))
            ts1 = dma_in(c, "ds1q%d" % sp, s1q[:, sp, :],
                         fm(c.ds["s1_scr"])[:, s, q2sl],
                         [c.bar, q_done] +
                         ([ow.get(sp)] if ow.get(sp) else []))
            tb = act(c, lambda e, s=s: e.activation(
                bout[:, s, :QV2], bout[:, s, :QV2], AF.Silu),
                deps=[c.bout_war[s]])
            t5 = dve(c, lambda e, s=s, sp=sp: e.tensor_tensor(
                bout[:, s, :QV2], bout[:, s, :QV2], s1q[:, sp, :QV2],
                OP.mult), deps=[tb, ts1])
            t6 = dve(c, lambda e, s=s, sp=sp: e.tensor_tensor(
                ostg[:, sp, :QV2], ostg[:, sp, :QV2], bout[:, s, :QV2],
                OP.add), deps=[t5, tx1q])
            c.bout_war[s] = t6
            tw = dma_out(c, "wout%d" % (s % 4), fm(c.dout)[:, s, q2sl],
                         ostg[:, sp, :QV2], [t6])
            ow[sp] = tw
            fin_toks.append(tw)
        c.ostg_war = ow
        out_toks += fin_toks
        q_done = barrier(c)
    c.tok_out = out_toks


def plan_final(c):
    gp(c, lambda e: e.engine_nop(), deps=c.tok_out)


def _constants():
    i = np.arange(128)
    return dict(
        eye=np.eye(128, dtype=np.float32),
        ones=np.ones((128, 128), np.float32),
        oh16=np.kron(np.eye(16, dtype=np.float32), np.ones((1, 128), np.float32)
                     ).reshape(16, 16 * 128) if False else
        np.repeat(np.eye(16, dtype=np.float32), 128, axis=1),
        mask_ge=np.where(i[None, :] >= i[:, None], 1e30, 0.0).astype(np.float32),
        maskT_incl=np.where(i[:, None] <= i[None, :], 1e30, -1e30).astype(np.float32),
        maskT_strict=np.where(i[:, None] < i[None, :], 1e30, -1e30).astype(np.float32),
        had128=(1.0 - 2.0 * (np.array([[bin(i & j).count("1") & 1
                                        for j in range(128)]
                                       for i in range(128)])
                             )).astype(np.float32),
        kcol=np.broadcast_to(np.array(
            [LN_EPS, 1.0, 1e-6, -GOLDEN_CENTER, MAGIC, -MAGIC,
             1e-5 * math.sqrt(float(FI)), 0.0],
            np.float32), (128, 8)).copy(),
    )


def stage_inputs(inputs):
    f32 = np.float32
    co = np.asarray(inputs["L0_attn_conv_w"], f32).reshape(NCT, 128, CK)
    shared = {
        "w_qkv": np.ascontiguousarray(np.asarray(inputs["L0_attn_W_qkv"], f32)),
        "conv_w": np.ascontiguousarray(
            co.transpose(1, 0, 2).reshape(128, NCT * CK)),
        "w_z": np.ascontiguousarray(np.asarray(inputs["L0_attn_W_z"], f32)),
        "w_b": np.ascontiguousarray(np.asarray(inputs["L0_attn_W_b"], f32)),
        "w_a": np.ascontiguousarray(np.asarray(inputs["L0_attn_W_a"], f32)),
        "dt_bias": np.ascontiguousarray(
            np.asarray(inputs["L0_attn_dt_bias"], f32).reshape(H, 1)),
        "a_log": np.ascontiguousarray(
            np.asarray(inputs["L0_attn_A_log"], f32).reshape(H, 1)),
        "norm_w": np.ascontiguousarray(
            np.asarray(inputs["L0_attn_norm_weight"], f32).reshape(128, 1)),
        "w_out0": np.ascontiguousarray(np.asarray(inputs["L0_attn_W_out"], f32)),
        "ffn_wg": np.ascontiguousarray(np.asarray(inputs["L0_ffn_Wg"], f32)),
        "ffn_wu": np.ascontiguousarray(np.asarray(inputs["L0_ffn_Wu"], f32)),
        "ffn_wd": np.ascontiguousarray(np.asarray(inputs["L0_ffn_Wd"], f32)),
        "ffn_wr": np.ascontiguousarray(np.asarray(inputs["L0_ffn_Wr"], f32)),
        "temp": np.ascontiguousarray(
            np.asarray(inputs["L0_ffn_temperature"], f32).reshape(1, 1)),
        "l1_w_in": np.ascontiguousarray(np.asarray(inputs["L1_attn_W_in"], f32)),
        "l1_w_dec": np.ascontiguousarray(np.asarray(inputs["L1_attn_W_decay"], f32)),
        "l1_w_gate": np.ascontiguousarray(np.asarray(inputs["L1_attn_W_gate"], f32)),
        "l1_w_out": np.ascontiguousarray(np.asarray(inputs["L1_attn_W_out"], f32)),
        "bn_wgT": np.ascontiguousarray(np.asarray(inputs["L1_ffn_w_gate"], f32).T),
        "bn_wuT": np.ascontiguousarray(np.asarray(inputs["L1_ffn_w_up"], f32).T),
        "bn_wdT": np.ascontiguousarray(np.asarray(inputs["L1_ffn_w_down"], f32).T),
    }
    for nm, key in (("l1_b_in", "L1_attn_b_in"), ("l1_b_dec", "L1_attn_b_decay"),
                    ("l1_b_gate", "L1_attn_b_gate"), ("l1_b_out", "L1_attn_b_out"),
                    ("l1_dec_bias", "L1_attn_decay_bias")):
        shared[nm] = np.ascontiguousarray(
            np.asarray(inputs[key], f32).reshape(KT, 128).T)
    shared.update(_constants())
    return shared


def kernel(**inputs):
    x = np.asarray(inputs["L0_attn_hidden_states"], np.float32)
    B, T, _ = x.shape
    shared = stage_inputs(inputs)
    in_maps = []
    for core in range(8):
        b, r = core // GRP, core % GRP
        sel = np.zeros((128, GRP), np.float32)
        sel[:, r] = 1.0
        m = dict(shared)
        m["xT"] = np.ascontiguousarray(x[b, r * T_OWN:(r + 1) * T_OWN, :].T)
        m["sel"] = sel
        in_maps.append(m)

    nc = build_program()
    res = run_bass_kernel_spmd(nc, in_maps, core_ids=list(range(8)))
    out = np.zeros((B, T, D), np.float32)
    for core in range(8):
        b, r = core // GRP, core % GRP
        out[b, r * T_OWN:(r + 1) * T_OWN, :] = \
            np.asarray(res.results[core]["outT"]).T
    return out
